# version 01
# baseline (speedup 1.0000x reference)
import math
from contextlib import ExitStack

import numpy as np

import concourse.bass as bass
import concourse.mybir as mybir
from concourse.bass_core_rust import IndirectOffsetOnAxis
from concourse.bass_utils import run_bass_kernel_spmd

F32 = mybir.dt.float32
BF16 = mybir.dt.bfloat16
I32 = mybir.dt.int32
AF = mybir.ActivationFunctionType
OP = mybir.AluOpType
AX = mybir.AxisListType

NCORE = 8
B, S, D = 2, 8192, 1024
TPC = 2048
NG = TPC // 128
DC = D // 128
H, HD = 16, 64
CSHIFT = 16.0
EPS_LN = 1e-5
E_EXP, TOPK, CAP, FEXP = 64, 8, 512, 768
EPPC = E_EXP // NCORE
BIG = 65536.0

ENGINES = ("sync", "gpsimd", "tensor", "scalar", "vector")


class Prog:

    def __init__(self, name):
        self.nc = bass.Bass("TRN2", target_bir_lowering=False)
        self.es = ExitStack()
        self.streams = {e: [] for e in ENGINES}
        self.sem = {}
        self.cnt = {}
        for n in ("ld", "lw", "mm", "act", "vec", "st", "gp"):
            self.sem[n] = self.es.enter_context(self.nc.semaphore(n))
            self.cnt[n] = 0
        self.ap = {}

    def din(self, name, shape, dt=F32):
        self.ap[name] = self.nc.dram_tensor(name, shape, dt, kind="ExternalInput").ap()
        return self.ap[name]

    def dout(self, name, shape, dt=F32):
        self.ap[name] = self.nc.dram_tensor(name, shape, dt, kind="ExternalOutput").ap()
        return self.ap[name]

    def dtmp(self, name, shape, dt=F32):
        self.ap[name] = self.nc.dram_tensor(name, shape, dt).ap()
        return self.ap[name]

    def sb(self, name, shape, dt=F32, es=None):
        return (es or self.es).enter_context(self.nc.sbuf_tensor(name, shape, dt))

    def ps(self, name, shape, dt=F32, es=None):
        return (es or self.es).enter_context(self.nc.psum_tensor(name, shape, dt))

    def dov(self, emit, wait=None):
        w = [("vec", self.cnt["vec"])]
        if wait is not None:
            w += wait if isinstance(wait, list) else [wait]
        self.do("vector", emit, wait=w, inc="vec")

    def do(self, eng, emit, wait=None, inc=None, incn=1):
        w = []
        if wait is not None:
            for it in (wait if isinstance(wait, list) else [wait]):
                if isinstance(it, tuple):
                    w.append(it)
                else:
                    w.append((it, self.cnt[it]))
        self.streams[eng].append((w, emit, inc, incn))
        if inc is not None:
            self.cnt[inc] += incn

    def finalize(self):
        with self.es, self.nc.Block() as blk:
            for engname in ENGINES:
                items = self.streams[engname]
                if not items:
                    continue
                deco = getattr(blk, engname)

                def make(items):
                    def body(eng):
                        for w, emit, inc, incn in items:
                            for s, v in w:
                                if v > 0:
                                    eng.wait_ge(self.sem[s], v)
                            instr = emit(eng)
                            if inc is not None:
                                instr.then_inc(self.sem[inc], incn)
                    return body

                deco(make(items))
        return self.nc


def emit_ln(p, src, out_bf, w0=None, out_f32=None):
    p.do("scalar", lambda s, a=src: s.activation(
        out=p.scr[:], in_=a, func=AF.Square, accum_out=p.sq[:]),
        wait=(list(w0) if w0 else []) + [("vec", p.cnt["vec"])], inc="act")
    p.do("vector", lambda v, a=src: v.reduce_sum(p.musum[:], a, axis=AX.X),
         wait=(list(w0) if w0 else None), inc="vec")
    p.do("vector", lambda v: v.tensor_scalar(p.mu[:], p.musum[:], 1.0 / 1024.0, None, OP.mult),
         wait="vec", inc="vec")
    p.do("vector", lambda v: v.tensor_mul(p.musq[:], p.mu[:], p.mu[:]),
         wait=[("vec", p.cnt["vec"]), ("act", p.cnt["act"])], inc="vec")
    p.do("vector", lambda v: v.scalar_tensor_tensor(
        p.var[:], p.sq[:], 1.0 / 1024.0, p.musq[:], OP.mult, OP.subtract),
        wait=["vec", ("act", p.cnt["act"])], inc="vec")
    p.do("scalar", lambda s: s.activation(out=p.sd[:], in_=p.var[:], func=AF.Sqrt,
                                          bias=p.epsb[:]), wait="vec", inc="act")
    p.do("vector", lambda v: v.reciprocal(p.rstd[:], p.sd[:]), wait="act", inc="vec")
    p.do("vector", lambda v, a=src, o=out_bf: v.tensor_scalar(
        o, a, p.mu[:, 0:1], p.rstd[:, 0:1], OP.subtract, OP.mult), wait="vec", inc="vec")
    if out_f32 is not None:
        p.do("vector", lambda v, a=src, o=out_f32: v.tensor_scalar(
            o, a, p.mu[:, 0:1], p.rstd[:, 0:1], OP.subtract, OP.mult), inc="vec")


def emit_transpose(p, psT, pairs):
    for i, (srcs, dsts) in enumerate(pairs):
        bank = i % 2
        p.do("tensor", lambda t, s=srcs, b=bank: t.transpose(psT[b][:], s, p.it128[:]),
             wait=[("vec", p.cnt["vec"])], inc="mm")
        p.do("vector", lambda v, d=dsts, b=bank: v.tensor_copy(out=d, in_=psT[b][:]),
             wait="mm", inc="vec")


def emit_ln_tile(p, psT, srcs, xfm, gbase=0, w0s=None, out_f32_slices=None):
    for g in range(4):
        src = srcs[g]
        w0 = (w0s[g] if w0s else None)
        ww = list(w0) if w0 else []
        ww.append(("mm", p.cnt["mm"]))
        emit_ln(p, src, p.xlg[:], w0=ww,
                out_f32=(out_f32_slices[g] if out_f32_slices else None))
        emit_transpose(p, psT, [
            (p.xlg[:, dc * 128:(dc + 1) * 128],
             xfm[:, dc * 512 + g * 128: dc * 512 + g * 128 + 128]) for dc in range(DC)])


def build_launch_A(debug=False, stages=4):
    p = Prog("A")
    xb = p.din("xb", [S, D])
    xmy = p.din("xmy", [TPC, D])
    wq0 = p.din("wq0", [D, D]); wk0 = p.din("wk0", [D, D]); wv0 = p.din("wv0", [D, D])
    wout = p.din("wout", [D, D])
    w1t = p.din("w1t", [D, 4096]); w2t = p.din("w2t", [D, 4096])
    h1t = p.din("h1t", [4096, 4096]); h2t = p.din("h2t", [4096, 4096])
    w3t = p.din("w3t", [4096, D])
    s1 = p.din("s1", [128, 32]); s2 = p.din("s2", [128, 32])
    sh1 = p.din("sh1", [128, 32]); sh2 = p.din("sh2", [128, 32])
    s3 = p.din("s3", [128, 8])
    wq1 = p.din("wq1", [D, D]); wk1 = p.din("wk1", [D, D]); wv1 = p.din("wv1", [D, D])
    bq1 = p.din("bq1", [1, D]); bk1 = p.din("bk1", [1, D]); bv1 = p.din("bv1", [1, D])
    ident = p.din("ident", [128, 128])

    mk = p.dout if debug else p.dtmp
    kfull = mk("kfull", [D, S], BF16)
    vfull = mk("vfull", [S, D], BF16)

    x2o = p.dout("x2o", [TPC, D])
    x1o = p.dout("x1o", [TPC, D]) if debug else None
    kchk = p.dout("kchk", [128, 1024], BF16) if debug else None
    attnpre = p.dout("attnpre", [D, TPC], BF16) if debug else None
    qbo = p.dout("qbo", [D, TPC], BF16) if debug else None
    xfmo = p.dout("xfmo", [128, DC * 512], BF16) if debug else None
    wqso = p.dout("wqso", [128, DC * 1024], BF16) if debug else None
    wqso2 = p.dout("wqso2", [128, DC * 1024], BF16) if debug else None
    wkso = p.dout("wkso", [128, DC * 1024], BF16) if debug else None
    xfmo2 = p.dout("xfmo2", [128, DC * 512], BF16) if debug else None
    psoo = p.dout("psoo", [128, 512]) if debug else None
    qbo2 = p.dout("qbo2", [D, TPC], BF16) if debug else None
    pswo = p.dout("pswo", [NG, 128, 512]) if debug else None
    attno = p.dout("attno", [D, TPC], BF16) if debug else None
    qfo = p.dout("qfo", [D, TPC], BF16)
    kvo = p.dout("kvo", [H, HD, 65])

    resid = p.sb("resid", [128, NG * 1024], F32)
    p.it128 = p.sb("it128", [128, 128], BF16)
    onesbf = p.sb("onesbf", [128, 512], BF16)
    onesf32 = p.sb("onesf32", [128, 64], F32)
    p.scr = p.sb("scr", [128, 1024], F32)
    p.mu = p.sb("mu", [128, 1]); p.sq = p.sb("sq", [128, 1])
    p.var = p.sb("var", [128, 1]); p.rstd = p.sb("rstd", [128, 1])
    p.musum = p.sb("musum", [128, 1]); p.musq = p.sb("musq", [128, 1])
    p.sd = p.sb("sd", [128, 1])
    p.xlg = p.sb("xlg", [128, 1024], BF16)
    p.epsb = p.sb("epsb", [128, 1], F32)
    p.cshb = p.sb("cshb", [128, 1], F32)

    p.do("gpsimd", lambda g: g.dma_start(out=p.it128[:], in_=ident[:, :]), inc="ld", incn=16)
    for g in range(NG):
        p.do("sync", lambda sy, g=g: sy.dma_start(
            out=resid[:, g * 1024:(g + 1) * 1024], in_=xmy[g * 128:(g + 1) * 128, :]),
            inc="ld", incn=16)
    p.do("vector", lambda v: v.memset(onesbf[:], 1.0), inc="vec")
    p.do("vector", lambda v: v.memset(p.epsb[:], EPS_LN), inc="vec")
    p.do("vector", lambda v: v.memset(p.cshb[:], -CSHIFT), inc="vec")
    p.do("vector", lambda v: v.memset(onesf32[:], 1.0), inc="vec")

    with ExitStack() as e12:
        qbuf = p.sb("qbuf", [128, DC * TPC], BF16, e12)
        attn = p.sb("attn", [128, DC * TPC], BF16, e12)

        with ExitStack() as e1:
            wqs = p.sb("wqs", [128, DC * 1024], BF16, e1)
            wks = p.sb("wks", [128, DC * 1024], BF16, e1)
            wvs = p.sb("wvs", [128, DC * 1024], BF16, e1)
            xg = [p.sb(f"xg{i}", [128, 1024], F32, e1) for i in range(2)]
            xfmA = p.sb("xfm1", [128, DC * 512], BF16, e1)
            kev = [p.sb(f"kev{i}", [128, 512], BF16, e1) for i in range(2)]
            vev = [p.sb(f"vev{i}", [128, 512], BF16, e1) for i in range(2)]
            psT1 = [p.ps(f"psT1{i}", [128, 128], BF16, e1) for i in range(2)]
            psO1 = [p.ps(f"psO1{i}", [128, 512], F32, e1) for i in range(4)]

            for w, t_ in ((wq0, wqs), (wk0, wks), (wv0, wvs)):
                p.do("gpsimd", lambda g, w=w, t_=t_: g.dma_start(
                    out=t_[:].rearrange("p (dc c) -> p dc c", dc=DC),
                    in_=w.rearrange("(dc p) c -> p dc c", p=128)), inc="lw", incn=16)
            lw_w = p.cnt["lw"]

            def proj_fm(wbuf, dst_fn, extra_wait):
                for cc in range(DC):
                    bank = cc % 2
                    for dc in range(DC):
                        p.do("tensor", lambda t, cc=cc, dc=dc, bank=bank, wbuf=wbuf: t.matmul(
                            psO1[bank][:], wbuf[:, dc * 1024 + cc * 128: dc * 1024 + cc * 128 + 128],
                            xfmA[:, dc * 512:(dc + 1) * 512],
                            start=(dc == 0), stop=(dc == DC - 1)),
                            wait=(([("lw", lw_w), ("vec", p.cnt["vec"])] + list(extra_wait))
                                  if dc == 0 else None),
                            inc=("mm" if dc == DC - 1 else None))
                    dst_fn(cc, bank)

            if debug:
                p.do("sync", lambda sy: sy.dma_start(out=wqso[:, :], in_=wqs[:]),
                     wait=[("lw", p.cnt["lw"])], inc="st", incn=16)
                p.do("sync", lambda sy: sy.dma_start(out=wkso[:, :], in_=wks[:]),
                     inc="st", incn=16)
            for bt in range(16):
                ld0 = p.cnt["ld"]
                for g in range(4):
                    xgb = xg[g % 2]
                    p.do("sync", lambda sy, bt=bt, g=g, xgb=xgb: sy.dma_start(
                        out=xgb[:], in_=xb[bt * 512 + g * 128: bt * 512 + (g + 1) * 128, :]),
                        wait=[("act", p.cnt["act"]), ("vec", p.cnt["vec"])], inc="ld", incn=16)
                    emit_ln(p, xgb[:], p.xlg[:],
                            w0=[("ld", p.cnt["ld"]), ("mm", p.cnt["mm"])])
                    emit_transpose(p, psT1, [
                        (p.xlg[:, dc * 128:(dc + 1) * 128],
                         xfmA[:, dc * 512 + g * 128: dc * 512 + g * 128 + 128]) for dc in range(DC)])

                def k_dst(cc, bank):
                    kb = kev[cc % 2]
                    p.do("vector", lambda v, kb=kb, bank=bank: v.tensor_copy(
                        out=kb[:], in_=psO1[bank][:]),
                        wait=["mm", ("st", p.cnt["st"])], inc="vec")
                    p.do("sync", lambda sy, kb=kb, cc=cc, bt=bt: sy.dma_start(
                        out=kfull[cc * 128:(cc + 1) * 128, bt * 512:(bt + 1) * 512], in_=kb[:]),
                        wait="vec", inc="st", incn=16)
                proj_fm(wks, k_dst, [])
                for ts in range(4):
                    for half in range(2):
                        bank = 2 + (ts * 2 + half) % 2
                        vb = vev[(ts * 2 + half) % 2]
                        for dc in range(DC):
                            p.do("tensor", lambda t, ts=ts, half=half, dc=dc, bank=bank: t.matmul(
                                psO1[bank][:],
                                xfmA[:, dc * 512 + ts * 128: dc * 512 + ts * 128 + 128],
                                wvs[:, dc * 1024 + half * 512: dc * 1024 + half * 512 + 512],
                                start=(dc == 0), stop=(dc == DC - 1)),
                                wait=([("vec", p.cnt["vec"])] if dc == 0 else None),
                                inc=("mm" if dc == DC - 1 else None))
                        p.do("vector", lambda v, vb=vb, bank=bank: v.tensor_copy(
                            out=vb[:], in_=psO1[bank][:]),
                            wait=["mm", ("st", p.cnt["st"])], inc="vec")
                        p.do("sync", lambda sy, vb=vb, ts=ts, half=half, bt=bt: sy.dma_start(
                            out=vfull[bt * 512 + ts * 128: bt * 512 + ts * 128 + 128,
                                      half * 512:(half + 1) * 512], in_=vb[:]),
                            wait="vec", inc="st", incn=16)

            if debug:
                p.do("sync", lambda sy: sy.dma_start(out=wqso2[:, :], in_=wqs[:]),
                     wait=[("vec", p.cnt["vec"]), ("mm", p.cnt["mm"]),
                           ("act", p.cnt["act"]), ("st", p.cnt["st"])], inc="st", incn=16)
            for qt in range(4):
                if debug and qt == 1:
                    for _cc in range(DC):
                        p.do("sync", lambda sy, _cc=_cc: sy.dma_start(
                            out=xfmo[:, _cc * 512:(_cc + 1) * 512],
                            in_=xfmA[:, _cc * 512:(_cc + 1) * 512]),
                            wait=[("vec", p.cnt["vec"]), ("mm", p.cnt["mm"])], inc="st", incn=16)
                    p.do("vector", lambda v: v.nop(), wait=[("st", p.cnt["st"])], inc="vec")
                emit_ln_tile(p, psT1,
                             [resid[:, (qt * 4 + g) * 1024:(qt * 4 + g + 1) * 1024] for g in range(4)],
                             xfmA)

                def q_dst(cc, bank, qt=qt):
                    if debug and qt == 0 and cc == 0:
                        p.do("vector", lambda v, bank=bank: v.tensor_copy(
                            out=p.scr[:, 0:512], in_=psO1[bank][:]), wait="mm", inc="vec")
                        p.do("sync", lambda sy: sy.dma_start(out=psoo[:, :], in_=p.scr[:, 0:512]),
                             wait="vec", inc="st", incn=16)
                        p.do("vector", lambda v: v.nop(), wait=[("st", p.cnt["st"])], inc="vec")
                    p.do("vector", lambda v, cc=cc, bank=bank, qt=qt: v.tensor_copy(
                        out=qbuf[:, cc * TPC + qt * 512: cc * TPC + qt * 512 + 512],
                        in_=psO1[bank][:]), wait="mm", inc="vec")
                proj_fm(wqs, q_dst, [])

        if debug:
            for cc in range(DC):
                p.do("sync", lambda sy, cc=cc: sy.dma_start(
                    out=qbo[cc * 128:(cc + 1) * 128, :], in_=qbuf[:, cc * TPC:(cc + 1) * TPC]),
                    wait=[("vec", p.cnt["vec"])], inc="st", incn=16)
        if stages < 1.5:
            p.do("sync", lambda sy: sy.nop(), wait=[("st", p.cnt["st"])])
            return p
        with ExitStack() as e2:
            kpair = p.sb("kpair", [128, S], BF16, e2)
            vones = [p.sb(f"vones{i}", [128, 64 * 65], BF16, e2) for i in range(2)]
            pT = [p.sb(f"pT{i}", [128, 1024], BF16, e2) for i in range(2)]
            znorm = p.sb("znorm", [128, 512], F32, e2)
            attnS = p.sb("attnS", [128, 512], F32, e2)
            psS = [[p.ps(f"psS{i}_{j}", [128, 512], F32, e2) for j in range(2)] for i in range(2)]
            psAV = p.ps("psAV", [128, 512], F32, e2)
            psN = p.ps("psN", [128, 512], F32, e2)

            for i in range(2):
                p.do("vector", lambda v, i=i: v.memset(
                    vones[i][:].rearrange("p (j m) -> p j m", m=65)[:, :, 64:65], 1.0), inc="vec")

            hp_list = range(8) if stages >= 2.0 else ([0] if stages >= 1.5 else [])
            for hp in hp_list:
                p.do("sync", lambda sy, hp=hp: sy.dma_start(
                    out=kpair[0:64, :], in_=kfull[(2 * hp) * 64:(2 * hp) * 64 + 64, :]),
                    wait=[("mm", p.cnt["mm"])], inc="ld", incn=16)
                p.do("sync", lambda sy, hp=hp: sy.dma_start(
                    out=kpair[64:128, :], in_=kfull[(2 * hp + 1) * 64:(2 * hp + 1) * 64 + 64, :]),
                    inc="ld", incn=16)
                for i in range(2):
                    h = 2 * hp + i
                    p.do("sync", lambda sy, h=h, i=i: sy.dma_start(
                        out=vones[i][:].rearrange("p (j m) -> p j m", m=65)[:, :, 0:64],
                        in_=vfull.rearrange("(j p) c -> p j c", p=128)[:, :, h * 64:(h + 1) * 64]),
                        inc="ld", incn=16)
                ld_hp = p.cnt["ld"]
                for i in (range(2) if stages >= 2.0 else []):
                    hbase = i * 64
                    for qt in range(4):
                        act_base = p.cnt["act"]
                        for g in range(32):
                            for j in range(2):
                                kt = 2 * g + j
                                w = None
                                if j == 0:
                                    w = [("act", p.cnt["act"]), ("vec", p.cnt["vec"]),
                                         ("ld", ld_hp)]
                                p.do("tensor", lambda t, g=g, j=j, kt=kt, hbase=hbase, hp=hp, qt=qt:
                                     t.matmul(psS[g % 2][j][:],
                                              kpair[hbase:hbase + 64, kt * 128:(kt + 1) * 128],
                                              qbuf[hbase:hbase + 64,
                                                   hp * 2048 + qt * 512: hp * 2048 + qt * 512 + 512],
                                              start=True, stop=True),
                                     wait=w, inc=("mm" if j == 1 else None))
                            if g >= 1:
                                for j in range(2):
                                    kt = 2 * (g - 1) + j
                                    p.do("tensor", lambda t, g=g, j=j, kt=kt, i=i:
                                         t.matmul(psAV[0:65, :],
                                                  vones[i][:, kt * 65: kt * 65 + 65],
                                                  pT[(g - 1) % 2][:, j * 512:(j + 1) * 512],
                                                  start=(kt == 0), stop=False),
                                         wait=([("act", p.cnt["act"])] if j == 0 else None),
                                         inc=("mm" if j == 1 else None))
                            p.do("scalar", lambda s, g=g: s.activation(
                                out=pT[g % 2][:, 0:512], in_=psS[g % 2][0][:], func=AF.Exp,
                                scale=0.125, bias=p.cshb[:]),
                                wait=[("mm", p.cnt["mm"])])
                            p.do("scalar", lambda s, g=g: s.activation(
                                out=pT[g % 2][:, 512:1024], in_=psS[g % 2][1][:], func=AF.Exp,
                                scale=0.125, bias=p.cshb[:]), inc="act")
                        for j in range(2):
                            kt = 62 + j
                            p.do("tensor", lambda t, j=j, kt=kt, i=i:
                                 t.matmul(psAV[0:65, :], vones[i][:, kt * 65: kt * 65 + 65],
                                          pT[1][:, j * 512:(j + 1) * 512],
                                          start=False, stop=(j == 1)),
                                 wait=([("act", p.cnt["act"])] if j == 0 else None),
                                 inc=("mm" if j == 1 else None))
                        p.do("vector", lambda v: v.reciprocal(znorm[64:65, :], psAV[64:65, :]),
                             wait="mm", inc="vec")
                        p.do("tensor", lambda t: t.matmul(psN[0:64, :], onesf32[64:65, 0:64],
                                                          znorm[64:65, :], start=True, stop=True),
                             wait="vec", inc="mm")
                        p.do("vector", lambda v: v.tensor_copy(out=attnS[0:64, :], in_=psAV[0:64, :]),
                             inc="vec")
                        p.do("vector", lambda v, i=i, hp=hp, qt=qt: v.tensor_tensor(
                            attn[i * 64:(i + 1) * 64,
                                 hp * 2048 + qt * 512: hp * 2048 + qt * 512 + 512],
                            attnS[0:64, :], psN[0:64, :], OP.mult),
                            wait="mm", inc="vec")

        if stages < 2.0:
            p.do("sync", lambda sy: sy.nop(), wait=[("st", p.cnt["st"]), ("ld", p.cnt["ld"])])
            return p
        with ExitStack() as e2b:
            wos = p.sb("wos", [128, DC * 1024], BF16, e2b)
            psW = [p.ps(f"psW{i}", [128, 512], F32, e2b) for i in range(2)]
            p.do("gpsimd", lambda g: g.dma_start(
                out=wos[:].rearrange("p (dc c) -> p dc c", dc=DC),
                in_=wout.rearrange("(dc p) c -> p dc c", p=128)),
                wait=[("mm", p.cnt["mm"]), ("act", p.cnt["act"]), ("vec", p.cnt["vec"]), ("st", p.cnt["st"])], inc="lw", incn=16)
            lw_o = p.cnt["lw"]
            p.do("vector", lambda v: v.nop(),
                 wait=[("mm", p.cnt["mm"]), ("act", p.cnt["act"]), ("st", p.cnt["st"]),
                       ("ld", p.cnt["ld"]), ("lw", p.cnt["lw"])], inc="vec")
            if debug:
                for cc in range(DC):
                    p.do("sync", lambda sy, cc=cc: sy.dma_start(
                        out=attnpre[cc * 128:(cc + 1) * 128, :],
                        in_=attn[:, cc * TPC:(cc + 1) * TPC]),
                        wait=[("vec", p.cnt["vec"])], inc="st", incn=16)
                    p.do("sync", lambda sy, cc=cc: sy.dma_start(
                        out=qbo2[cc * 128:(cc + 1) * 128, :], in_=qbuf[:, cc * TPC:(cc + 1) * TPC]),
                        inc="st", incn=16)
            for tg in range(NG):
                for half in range(2):
                    bank = (tg * 2 + half) % 2
                    for hp in range(8):
                        p.do("tensor", lambda t, tg=tg, half=half, hp=hp, bank=bank: t.matmul(
                            psW[bank][:],
                            attn[:, hp * 2048 + tg * 128: hp * 2048 + tg * 128 + 128],
                            wos[:, hp * 1024 + half * 512: hp * 1024 + half * 512 + 512],
                            start=(hp == 0), stop=(hp == 7)),
                            wait=([("lw", lw_o), ("vec", p.cnt["vec"])] if hp == 0 else None),
                            inc=("mm" if hp == 7 else None))
                    if debug and half == 0:
                        p.do("vector", lambda v, bank=bank: v.tensor_copy(
                            out=p.scr[:, 0:512], in_=psW[bank][:]), wait="mm", inc="vec")
                        p.do("sync", lambda sy, tg=tg: sy.dma_start(
                            out=pswo[tg], in_=p.scr[:, 0:512]), wait="vec", inc="st", incn=16)
                    p.do("vector", lambda v, tg=tg, half=half, bank=bank: v.tensor_tensor(
                        resid[:, tg * 1024 + half * 512: tg * 1024 + half * 512 + 512],
                        resid[:, tg * 1024 + half * 512: tg * 1024 + half * 512 + 512],
                        psW[bank][:], OP.add),
                        wait=["mm", ("st", p.cnt["st"])], inc="vec")
            if debug:
                p.do("sync", lambda sy: sy.dma_start(
                    out=x1o.rearrange("(g p) d -> p g d", p=128),
                    in_=resid[:].rearrange("p (g d) -> p g d", g=NG)),
                    wait=[("vec", p.cnt["vec"])], inc="st", incn=16)
                for cc in range(DC):
                    p.do("sync", lambda sy, cc=cc: sy.dma_start(
                        out=attno[cc * 128:(cc + 1) * 128, :], in_=attn[:, cc * TPC:(cc + 1) * TPC]),
                        wait=[("vec", p.cnt["vec"])], inc="st", incn=16)

    if stages < 3:
        p.do("sync", lambda sy: sy.nop(), wait=[("st", p.cnt["st"])])
        return p
    with ExitStack() as e3:
        xfmF = p.sb("xfm3", [128, DC * 512], BF16, e3)
        abuf = p.sb("abuf", [128, 32 * 512], BF16, e3)
        bbuf = p.sb("bbuf", [128, 32 * 512], BF16, e3)
        t1 = p.sb("t1", [128, 512], F32, e3)
        t2 = p.sb("t2", [128, 512], F32, e3)
        w3tmp = p.sb("w3tmp", [128, 512], BF16, e3)
        tf32 = p.sb("tf32", [128, 128], F32, e3)
        w12s = [p.sb(f"w12s{i}", [128, DC * 256], BF16, e3) for i in range(2)]
        whs = [p.sb(f"whs{i}", [128, 32 * 128], BF16, e3) for i in range(2)]
        scl1 = p.sb("scl1", [128, 32], F32, e3)
        scl2 = p.sb("scl2", [128, 32], F32, e3)
        sclh1 = p.sb("sclh1", [128, 32], F32, e3)
        sclh2 = p.sb("sclh2", [128, 32], F32, e3)
        scl3 = p.sb("scl3", [128, 8], F32, e3)
        psT3v = [p.ps(f"psT3{i}", [128, 128], BF16, e3) for i in range(2)]
        psA = [p.ps(f"psA{i}", [128, 512], F32, e3) for i in range(4)]

        for src, dst in ((s1, scl1), (s2, scl2), (sh1, sclh1), (sh2, sclh2), (s3, scl3)):
            p.do("sync", lambda sy, src=src, dst=dst: sy.dma_start(
                out=dst[:], in_=src[:, :]),
                wait=[("mm", p.cnt["mm"]), ("act", p.cnt["act"]), ("vec", p.cnt["vec"]), ("st", p.cnt["st"])], inc="ld", incn=16)

        def h_stage(inbuf, outbuf, wsrc, sclbuf, func, nout=32, nin=32):
            for oc in range(nout):
                hbuf = whs[oc % 2]
                p.do("gpsimd", lambda g, oc=oc, hbuf=hbuf, wsrc=wsrc, nin=nin: g.dma_start(
                    out=hbuf[:, 0:nin * 128].rearrange("p (ic c) -> p ic c", ic=nin),
                    in_=wsrc.rearrange("(ic p) o -> p ic o", p=128)[:, :, oc * 128:(oc + 1) * 128]),
                    wait=[("mm", p.cnt["mm"])], inc="lw", incn=16)
                bank = oc % 2
                for ic in range(nin):
                    p.do("tensor", lambda t, oc=oc, ic=ic, bank=bank, hbuf=hbuf, inbuf=inbuf: t.matmul(
                        psA[bank][:], hbuf[:, ic * 128:(ic + 1) * 128],
                        inbuf[:, ic * 512:(ic + 1) * 512],
                        start=(ic == 0), stop=(ic == nin - 1)),
                        wait=([("lw", p.cnt["lw"]), ("act", p.cnt["act"]),
                               ("vec", p.cnt["vec"])] if ic == 0 else None),
                        inc=("mm" if ic == nin - 1 else None))
                yield oc, bank

        for st in range(4):
            emit_ln_tile(p, psT3v,
                         [resid[:, (st * 4 + g) * 1024:(st * 4 + g + 1) * 1024] for g in range(4)],
                         xfmF)
            for oc in range(32):
                buf = w12s[oc % 2]
                for wi, wsrc in ((0, w1t), (1, w2t)):
                    p.do("gpsimd", lambda g, wsrc=wsrc, wi=wi, oc=oc, buf=buf: g.dma_start(
                        out=buf[:].rearrange("p (w dc c) -> p w dc c", w=2, dc=DC)[:, wi],
                        in_=wsrc.rearrange("(dc p) o -> p dc o", p=128)[:, :, oc * 128:(oc + 1) * 128]),
                        wait=[("mm", p.cnt["mm"])], inc="lw", incn=16)
                for wi in range(2):
                    bank = 2 * (oc % 2) + wi
                    for dc in range(DC):
                        p.do("tensor", lambda t, oc=oc, wi=wi, dc=dc, bank=bank, buf=buf: t.matmul(
                            psA[bank][:],
                            buf[:].rearrange("p (w dc c) -> p w dc c", w=2, dc=DC)[:, wi, dc],
                            xfmF[:, dc * 512:(dc + 1) * 512],
                            start=(dc == 0), stop=(dc == DC - 1)),
                            wait=([("lw", p.cnt["lw"]), ("vec", p.cnt["vec"]),
                                   ("act", p.cnt["act"])] if dc == 0 else None),
                            inc=("mm" if dc == DC - 1 else None))
                p.do("scalar", lambda s, oc=oc: s.activation(
                    out=t1[:], in_=psA[2 * (oc % 2)][:], func=AF.Silu,
                    scale=scl1[:, oc:oc + 1]), wait=["mm", ("vec", p.cnt["vec"])], inc="act")
                p.do("scalar", lambda s, oc=oc: s.activation(
                    out=t2[:], in_=psA[2 * (oc % 2) + 1][:], func=AF.Identity,
                    scale=scl2[:, oc:oc + 1]), inc="act")
                p.do("vector", lambda v, oc=oc: v.tensor_tensor(
                    abuf[:, oc * 512:(oc + 1) * 512], t1[:], t2[:], OP.mult),
                    wait="act", inc="vec")
            for oc, bank in h_stage(abuf, bbuf, h1t, sclh1, AF.Gelu):
                p.do("scalar", lambda s, oc=oc, bank=bank: s.activation(
                    out=bbuf[:, oc * 512:(oc + 1) * 512], in_=psA[bank][:],
                    func=AF.Gelu, scale=sclh1[:, oc:oc + 1]), wait="mm", inc="act")
            for oc, bank in h_stage(bbuf, abuf, h2t, sclh2, AF.Gelu):
                p.do("scalar", lambda s, oc=oc, bank=bank: s.activation(
                    out=abuf[:, oc * 512:(oc + 1) * 512], in_=psA[bank][:],
                    func=AF.Gelu, scale=sclh2[:, oc:oc + 1]), wait="mm", inc="act")
            for oc, bank in h_stage(abuf, None, w3t, scl3, None, nout=8):
                p.do("scalar", lambda s, oc=oc, bank=bank: s.activation(
                    out=w3tmp[:], in_=psA[bank][:], func=AF.Identity,
                    scale=scl3[:, oc:oc + 1]), wait=["mm", ("vec", p.cnt["vec"])], inc="act")
                for tsub in range(4):
                    tb = 2 + tsub % 2
                    p.do("tensor", lambda t, tsub=tsub, tb=tb: t.transpose(
                        psT3v[tb - 2][:], w3tmp[:, tsub * 128:(tsub + 1) * 128], p.it128[:]),
                        wait=[("act", p.cnt["act"]), ("vec", p.cnt["vec"])], inc="mm")
                    p.do("vector", lambda v, tb=tb: v.tensor_copy(out=tf32[:], in_=psT3v[tb - 2][:]),
                         wait="mm", inc="vec")
                    gg = st * 4 + tsub
                    p.do("vector", lambda v, gg=gg, oc=oc: v.tensor_tensor(
                        resid[:, gg * 1024 + oc * 128: gg * 1024 + oc * 128 + 128],
                        resid[:, gg * 1024 + oc * 128: gg * 1024 + oc * 128 + 128],
                        tf32[:], OP.add), inc="vec")

    if stages < 4:
        p.do("sync", lambda sy: sy.nop(), wait=[("st", p.cnt["st"])])
        return p
    with ExitStack() as e4:
        qf4 = p.sb("qf4", [128, DC * TPC], BF16, e4)
        xfmL = p.sb("xfm4", [128, DC * 512], BF16, e4)
        wq1s = p.sb("wq1s", [128, DC * 1024], BF16, e4)
        wk1s = p.sb("wk1s", [128, DC * 1024], BF16, e4)
        wv1s = p.sb("wv1s", [128, DC * 1024], BF16, e4)
        bqs = p.sb("bqs", [1, D], BF16, e4)
        bks = p.sb("bks", [1, D], BF16, e4)
        bvs = p.sb("bvs", [1, D], BF16, e4)
        kf = p.sb("kf", [128, 4 * 1024], BF16, e4)
        vf = p.sb("vf", [128, 4 * 16 * 65], BF16, e4)
        ta = p.sb("ta", [128, 512], F32, e4)
        tc = p.sb("tc", [128, 512], F32, e4)
        tb_ = p.sb("tb", [128, 512], F32, e4)
        kvsb = p.sb("kvsb", [128, 16 * 65], F32, e4)
        psT4v = [p.ps(f"psT4{i}", [128, 128], BF16, e4) for i in range(2)]
        psOL = [p.ps(f"psO4{i}", [128, 512], F32, e4) for i in range(3)]
        psKV = p.ps("psKV", [128, 1536], F32, e4)

        for w, t_ in ((wq1, wq1s), (wk1, wk1s), (wv1, wv1s)):
            p.do("gpsimd", lambda g, w=w, t_=t_: g.dma_start(
                out=t_[:].rearrange("p (dc c) -> p dc c", dc=DC),
                in_=w.rearrange("(dc p) c -> p dc c", p=128)),
                wait=[("mm", p.cnt["mm"]), ("act", p.cnt["act"]), ("vec", p.cnt["vec"]), ("st", p.cnt["st"])], inc="lw", incn=16)
        for bsrc, bdst in ((bq1, bqs), (bk1, bks), (bv1, bvs)):
            p.do("gpsimd", lambda g, bsrc=bsrc, bdst=bdst: g.dma_start(
                out=bdst[:], in_=bsrc[:, :]), inc="lw", incn=16)
        lw4 = p.cnt["lw"]

        kv_started = set()
        for st in range(4):
            emit_ln_tile(p, psT4v,
                         [resid[:, (st * 4 + g) * 1024:(st * 4 + g + 1) * 1024] for g in range(4)],
                         xfmL)
            p.do("vector", lambda v: v.memset(
                vf[:].rearrange("p (ts h m) -> p ts h m", ts=4, m=65)[:, :, :, 64:65], 1.0),
                wait=[("mm", p.cnt["mm"])], inc="vec")
            for cc in range(DC):
                bank = 0
                for dc in range(DC):
                    p.do("tensor", lambda t, cc=cc, dc=dc, bank=bank: t.matmul(
                        psOL[bank][:], wq1s[:, dc * 1024 + cc * 128: dc * 1024 + cc * 128 + 128],
                        xfmL[:, dc * 512:(dc + 1) * 512],
                        start=(dc == 0), stop=False),
                        wait=([("lw", lw4), ("vec", p.cnt["vec"]),
                               ("act", p.cnt["act"])] if dc == 0 else None))
                p.do("tensor", lambda t, cc=cc, bank=bank: t.matmul(
                    psOL[bank][:], bqs[0:1, cc * 128:(cc + 1) * 128], onesbf[0:1, :],
                    start=False, stop=True), inc="mm")
                p.do("vector", lambda v, bank=bank: v.tensor_scalar_min(ta[:], psOL[bank][:], 0.0),
                     wait=["mm", ("act", p.cnt["act"])], inc="vec")
                p.do("scalar", lambda s: s.activation(out=tc[:], in_=ta[:], func=AF.Exp),
                     wait="vec", inc="act")
                p.do("vector", lambda v, bank=bank: v.tensor_scalar_max(tb_[:], psOL[bank][:], 0.0),
                     inc="vec")
                p.do("vector", lambda v, cc=cc, st=st: v.tensor_tensor(
                    qf4[:, cc * TPC + st * 512: cc * TPC + st * 512 + 512], tc[:], tb_[:], OP.add),
                    wait=["act", ("vec", p.cnt["vec"])], inc="vec")
            for ts in range(4):
                for half in range(2):
                    for bank, wbuf, brow in ((1, wk1s, bks), (2, wv1s, bvs)):
                        for dc in range(DC):
                            p.do("tensor", lambda t, ts=ts, half=half, dc=dc, bank=bank, wbuf=wbuf:
                                 t.matmul(
                                     psOL[bank][:],
                                     xfmL[:, dc * 512 + ts * 128: dc * 512 + ts * 128 + 128],
                                     wbuf[:, dc * 1024 + half * 512: dc * 1024 + half * 512 + 512],
                                     start=(dc == 0), stop=False),
                                 wait=([("vec", p.cnt["vec"]),
                                        ("act", p.cnt["act"])] if dc == 0 else None))
                        p.do("tensor", lambda t, ts=ts, half=half, bank=bank, brow=brow: t.matmul(
                            psOL[bank][:], onesbf[0:1, ts * 128: ts * 128 + 128],
                            brow[0:1, half * 512:(half + 1) * 512],
                            start=False, stop=True), inc="mm")
                    p.do("vector", lambda v: v.tensor_scalar_min(ta[:], psOL[1][:], 0.0),
                         wait=["mm", ("act", p.cnt["act"])], inc="vec")
                    p.do("scalar", lambda s: s.activation(out=tc[:], in_=ta[:], func=AF.Exp),
                         wait="vec", inc="act")
                    p.do("vector", lambda v: v.tensor_scalar_max(tb_[:], psOL[1][:], 0.0), inc="vec")
                    p.do("vector", lambda v, ts=ts, half=half: v.tensor_tensor(
                        kf[:, ts * 1024 + half * 512: ts * 1024 + half * 512 + 512],
                        tc[:], tb_[:], OP.add), wait=["act", ("vec", p.cnt["vec"])], inc="vec")
                    p.do("vector", lambda v, ts=ts, half=half: v.tensor_copy(
                        out=vf[:].rearrange("p (ts h m) -> p ts h m", ts=4, m=65)
                        [:, ts, half * 8:(half + 1) * 8, 0:64],
                        in_=psOL[2][:].rearrange("p (h m) -> p h m", m=64)), inc="vec")
            for h in range(H):
                for ts in range(4):
                    bank_first = (st == 0 and ts == 0 and h in (0, 7, 14))
                    last = (st == 3 and ts == 3)
                    kc = (h // 7) * 512 + (h % 7) * 65
                    p.do("tensor", lambda t, h=h, ts=ts, bf=bank_first, last=last, kc=kc:
                         t.matmul(psKV[0:64, kc:kc + 65],
                                  kf[:, ts * 1024 + h * 64: ts * 1024 + h * 64 + 64],
                                  vf[:, ts * 1040 + h * 65: ts * 1040 + h * 65 + 65],
                                  start=bf, stop=last,
                                  skip_group_check=True),
                         wait=([("vec", p.cnt["vec"])] if (ts == 0 and h == 0) else None),
                         inc=("mm" if (h == H - 1 and ts == 3) else None))
        for h in range(H):
            kc = (h // 7) * 512 + (h % 7) * 65
            p.do("vector", lambda v, h=h, kc=kc: v.tensor_copy(
                out=kvsb[0:64, h * 65: h * 65 + 65],
                in_=psKV[0:64, kc:kc + 65]),
                wait="mm", inc="vec")
            p.do("sync", lambda sy, h=h: sy.dma_start(
                out=kvo[h], in_=kvsb[0:64, h * 65: h * 65 + 65]),
                wait="vec", inc="st", incn=16)
        for cc in range(DC):
            p.do("sync", lambda sy, cc=cc: sy.dma_start(
                out=qfo[cc * 128:(cc + 1) * 128, :], in_=qf4[:, cc * TPC:(cc + 1) * TPC]),
                wait=[("vec", p.cnt["vec"])], inc="st", incn=16)
        p.do("sync", lambda sy: sy.dma_start(
            out=x2o.rearrange("(g p) d -> p g d", p=128),
            in_=resid[:].rearrange("p (g d) -> p g d", g=NG)),
            wait=[("vec", p.cnt["vec"])], inc="st", incn=16)
        if debug:
            p.do("sync", lambda sy: sy.dma_start(out=ta[:].bitcast(BF16)[:, 0:1024],
                                                 in_=kfull[0:128, 0:1024]),
                 wait=[("st", p.cnt["st"])], inc="ld", incn=16)
            p.do("sync", lambda sy: sy.dma_start(out=kchk[:, :], in_=ta[:].bitcast(BF16)[:, 0:1024]),
                 wait=[("ld", p.cnt["ld"])], inc="st", incn=16)
        p.do("sync", lambda sy: sy.nop(), wait=[("st", p.cnt["st"])])

    return p



def build_launch_B():
    p = Prog("B")
    x2i = p.din("x2i", [TPC, D])
    qfi = p.din("qfi", [D, TPC], BF16)
    kvp = p.din("kvp", [4, H, HD, 65])
    wo1 = p.din("wo1", [D, D])
    bo1 = p.din("bo1", [1, D])
    rt = p.din("rt", [D, 64])
    ident = p.din("ident", [128, 128])
    x3o = p.dout("x3o", [TPC, D])
    xl4o = p.dout("xl4o", [TPC, D], BF16)
    idxo = p.dout("idxo", [TPC, 8])
    wno = p.dout("wno", [TPC, 8])
    toto = p.dout("toto", [1, 64])

    resid = p.sb("residB", [128, NG * 1024], F32)
    p.it128 = p.sb("it128B", [128, 128], BF16)
    onesbfB = p.sb("onesbfB", [128, 512], BF16)
    p.scr = p.sb("scrB", [128, 1024], F32)
    p.mu = p.sb("muB", [128, 1]); p.sq = p.sb("sqB", [128, 1])
    p.var = p.sb("varB", [128, 1]); p.rstd = p.sb("rstdB", [128, 1])
    p.musum = p.sb("musumB", [128, 1]); p.musq = p.sb("musqB", [128, 1])
    p.sd = p.sb("sdB", [128, 1])
    p.xlg = p.sb("xlgB", [128, 1024], BF16)
    p.epsb = p.sb("epsbB", [128, 1], F32)
    hif = p.sb("hifB", [128, 1024], F32)
    xlf = p.sb("xlfB", [128, 1024], F32)
    iot64 = p.sb("iot64B", [128, 64], F32)
    iomb = p.sb("iombB", [128, 64], F32)
    probs = p.sb("probsB", [128, 64], F32)
    pn = p.sb("pnB", [128, 64], F32)
    nmax = p.sb("nmaxB", [128, 1], F32)
    rsum = p.sb("rsumB", [128, 1], F32)
    ssum = p.sb("ssumB", [128, 1], F32)
    wv8 = p.sb("wv8B", [128, 8], F32)
    iv8 = p.sb("iv8B", [128, 8], F32)
    t64 = p.sb("t64B", [128, 64], F32)
    obf = p.sb("obfB", [128, 64], BF16)
    onesf1 = p.sb("onesf1B", [128, 1], BF16)
    eB1 = ExitStack()
    qfB = p.sb("qfB", [128, DC * TPC], BF16, eB1)
    kvl = [p.sb(f"kvl{i}", [128, 8 * 65], F32, eB1) for i in range(2)]
    kvs = p.sb("kvs", [128, 8 * 65], F32, eB1)
    kvsbf = p.sb("kvsbf", [128, 8 * 65], BF16, eB1)
    ytok = p.sb("ytok", [128, 1024], BF16, eB1)
    zcol = p.sb("zcolB", [128, 1], F32, eB1)
    yfm = p.sb("yfmB", [128, DC * TPC], BF16, eB1)
    wo1s = p.sb("wo1s", [128, DC * 1024], BF16, eB1)
    bo1s = p.sb("bo1s", [1, D], BF16, eB1)
    psT = [p.ps(f"psTB{i}", [128, 128], BF16) for i in range(2)]
    psY = [p.ps(f"psYB{i}", [128, 512], F32) for i in range(2)]
    psW = [p.ps(f"psWB{i}", [128, 512], F32) for i in range(2)]
    psL = p.ps("psLB", [128, 64], F32)
    psTT = p.ps("psTTB", [128, 64], F32)

    p.do("gpsimd", lambda g: g.dma_start(out=p.it128[:], in_=ident[:, :]), inc="ld", incn=16)
    p.do("gpsimd", lambda g: g.dma_start(
        out=wo1s[:].rearrange("p (dc c) -> p dc c", dc=DC),
        in_=wo1.rearrange("(dc p) c -> p dc c", p=128)), inc="lw", incn=16)
    p.do("gpsimd", lambda g: g.dma_start(out=bo1s[:], in_=bo1[:, :]), inc="lw", incn=16)
    p.do("gpsimd", lambda g: g.dma_start(out=qfB[:].rearrange("p (dc t) -> p dc t", dc=DC),
                                         in_=qfi.rearrange("(dc p) t -> p dc t", p=128)),
         inc="lw", incn=16)
    for g2 in range(NG):
        p.do("sync", lambda sy, g2=g2: sy.dma_start(
            out=resid[:, g2 * 1024:(g2 + 1) * 1024], in_=x2i[g2 * 128:(g2 + 1) * 128, :]),
            inc="ld", incn=16)
    p.do("vector", lambda v: v.memset(onesbfB[:], 1.0), inc="vec")
    p.do("vector", lambda v: v.memset(onesf1[:], 1.0), inc="vec")
    p.do("vector", lambda v: v.memset(p.epsb[:], EPS_LN), inc="vec")
    p.do("gpsimd", lambda g: g.iota(iot64[:], pattern=[[1, 64]], base=0, channel_multiplier=0,
                                    allow_small_or_imprecise_dtypes=True), inc="gp")
    p.do("vector", lambda v: v.tensor_scalar_add(iomb[:], iot64[:], -BIG),
         wait=[("gp", p.cnt["gp"])], inc="vec")
    for gi in range(4):
        buf = kvl[gi % 2]
        p.do("sync", lambda sy, gi=gi, buf=buf: sy.dma_start(
            out=buf[:, :].rearrange("p (hp c) -> p hp c", hp=8),
            in_=kvp[gi].rearrange("(hp i) d c -> (i d) hp c", i=2)),
            wait=[("vec", p.cnt["vec"])], inc="ld", incn=16)
        if gi == 0:
            p.dov(lambda v, buf=buf: v.tensor_copy(out=kvs[:], in_=buf[:]), wait="ld")
        else:
            p.dov(lambda v, buf=buf: v.tensor_tensor(kvs[:], kvs[:], buf[:], OP.add), wait="ld")
    p.do("vector", lambda v: v.tensor_copy(out=kvsbf[:], in_=kvs[:]), wait="vec", inc="vec")
    for tg in range(NG):
        for hp in range(DC):
            for i in range(2):
                hb = i * 64
                p.do("tensor", lambda t, tg=tg, hp=hp, i=i, hb=hb: t.matmul(
                    psY[i][:, 0:65],
                    qfB[hb:hb + 64, hp * TPC + tg * 128: hp * TPC + tg * 128 + 128],
                    kvsbf[hb:hb + 64, hp * 65: hp * 65 + 65],
                    start=True, stop=True),
                    wait=([("lw", p.cnt["lw"]), ("vec", p.cnt["vec"])] if i == 0 else None),
                    inc=("mm" if i == 1 else None))
            for i in range(2):
                p.do("vector", lambda v, i=i: v.reciprocal(zcol[:], psY[i][:, 64:65]),
                     wait=(["mm"] if i == 0 else [("vec", p.cnt["vec"])]), inc="vec")
                p.do("vector", lambda v, i=i, hp=hp: v.tensor_scalar(
                    ytok[:, (2 * hp + i) * 64:(2 * hp + i) * 64 + 64],
                    psY[i][:, 0:64], zcol[:, 0:1], None, OP.mult),
                    wait="vec", inc="vec")
        emit_transpose(p, psT, [
            (ytok[:, hp * 128:(hp + 1) * 128],
             yfm[:, hp * TPC + tg * 128: hp * TPC + tg * 128 + 128]) for hp in range(DC)])
    for tg in range(NG):
        for half in range(2):
            bank = (tg * 2 + half) % 2
            for hp in range(DC):
                p.do("tensor", lambda t, tg=tg, half=half, hp=hp, bank=bank: t.matmul(
                    psW[bank][:], yfm[:, hp * TPC + tg * 128: hp * TPC + tg * 128 + 128],
                    wo1s[:, hp * 1024 + half * 512: hp * 1024 + half * 512 + 512],
                    start=(hp == 0), stop=False),
                    wait=([("vec", p.cnt["vec"])] if hp == 0 else None))
            p.do("tensor", lambda t, tg=tg, half=half, bank=bank: t.matmul(
                psW[bank][:], onesbfB[0:1, 0:128],
                bo1s[0:1, half * 512:(half + 1) * 512],
                start=False, stop=True), inc="mm")
            p.do("vector", lambda v, tg=tg, half=half, bank=bank: v.tensor_tensor(
                resid[:, tg * 1024 + half * 512: tg * 1024 + half * 512 + 512],
                resid[:, tg * 1024 + half * 512: tg * 1024 + half * 512 + 512],
                psW[bank][:], OP.add), wait="mm", inc="vec")
    eB1.close()
    xhi = p.sb("xhiB", [128, DC * TPC], BF16)
    xlo = p.sb("xloB", [128, DC * TPC], BF16)
    rts = p.sb("rtsB", [128, 8 * 64], F32)
    rthi = p.sb("rthiB", [128, 8 * 64], BF16)
    rtlo = p.sb("rtloB", [128, 8 * 64], BF16)
    p.do("sync", lambda sy: sy.dma_start(out=rts[:].rearrange("p (dc c) -> p dc c", dc=DC),
                                         in_=rt.rearrange("(dc p) c -> p dc c", p=128)),
         wait=[("mm", p.cnt["mm"]), ("vec", p.cnt["vec"]), ("act", p.cnt["act"]),
               ("st", p.cnt["st"])], inc="ld", incn=16)
    p.do("vector", lambda v: v.tensor_copy(out=rthi[:], in_=rts[:]), wait="ld", inc="vec")
    p.do("vector", lambda v: v.tensor_tensor(rtlo[:], rts[:], rthi[:], OP.subtract),
         wait="vec", inc="vec")
    for g2 in range(NG):
        emit_ln(p, resid[:, g2 * 1024:(g2 + 1) * 1024], p.xlg[:],
                w0=[("vec", p.cnt["vec"]), ("mm", p.cnt["mm"]), ("st", p.cnt["st"])],
                out_f32=xlf[:])
        p.do("sync", lambda sy, g2=g2: sy.dma_start(
            out=xl4o[g2 * 128:(g2 + 1) * 128, :], in_=p.xlg[:]), wait="vec", inc="st", incn=16)
        p.dov(lambda v: v.tensor_copy(out=hif[:], in_=p.xlg[:]))
        p.do("vector", lambda v: v.tensor_tensor(p.scr[:], xlf[:], hif[:], OP.subtract),
             wait="vec", inc="vec")
        emit_transpose(p, psT, [
            (p.xlg[:, dc * 128:(dc + 1) * 128],
             xhi[:, dc * TPC + g2 * 128: dc * TPC + g2 * 128 + 128]) for dc in range(DC)])
        p.do("vector", lambda v: v.tensor_copy(out=p.xlg[:], in_=p.scr[:]),
             wait=[("mm", p.cnt["mm"]), ("st", p.cnt["st"])], inc="vec")
        emit_transpose(p, psT, [
            (p.xlg[:, dc * 128:(dc + 1) * 128],
             xlo[:, dc * TPC + g2 * 128: dc * TPC + g2 * 128 + 128]) for dc in range(DC)])
    for tg in range(NG):
        combos = [(xhi, rthi), (xlo, rthi), (xhi, rtlo), (xlo, rtlo)]
        nmm = 0
        for xb_, rb_ in combos:
            for dc in range(DC):
                nmm += 1
                p.do("tensor", lambda t, tg=tg, dc=dc, xb_=xb_, rb_=rb_, nmm=nmm: t.matmul(
                    psL[:], xb_[:, dc * TPC + tg * 128: dc * TPC + tg * 128 + 128],
                    rb_[:, dc * 64:(dc + 1) * 64],
                    start=(nmm == 1), stop=(nmm == 32)),
                    wait=([("act", p.cnt["act"]), ("vec", p.cnt["vec"])] if nmm == 1 else None),
                    inc=("mm" if nmm == 32 else None))
        p.do("vector", lambda v: v.reduce_max(nmax[:], psL[:], axis=AX.X, negate=True),
             wait="mm", inc="vec")
        p.do("scalar", lambda s: s.activation(out=probs[:], in_=psL[:], func=AF.Exp,
                                              bias=nmax[:, 0:1], accum_out=ssum[:]),
             wait="vec", inc="act")
        p.do("vector", lambda v: v.reciprocal(rsum[:], ssum[:]), wait="act", inc="vec")
        p.do("vector", lambda v: v.tensor_scalar(pn[:], probs[:], rsum[:, 0:1], None, OP.mult),
             wait="vec", inc="vec")
        for k in range(8):
            p.do("vector", lambda v, k=k: v.reduce_max(wv8[:, k:k + 1], pn[:], axis=AX.X),
                 wait=(["vec", ("st", p.cnt["st"])] if k == 0 else ["vec"]), inc="vec")
            p.dov(lambda v, k=k: v.tensor_scalar(t64[:], pn[:], wv8[:, k:k + 1],
                                                 None, OP.is_equal))
            p.dov(lambda v: v.tensor_mul(t64[:], t64[:], iomb[:]))
            p.dov(lambda v: v.tensor_scalar_add(t64[:], t64[:], BIG))
            p.dov(lambda v, k=k: v.tensor_reduce(iv8[:, k:k + 1], t64[:], axis=AX.X,
                                                 op=OP.min))
            p.dov(lambda v, k=k: v.tensor_scalar(t64[:], iot64[:], iv8[:, k:k + 1],
                                                 None, OP.is_equal))
            p.dov(lambda v: v.scalar_tensor_tensor(pn[:], t64[:], -BIG, pn[:],
                                                   OP.mult, OP.add))
            p.dov(lambda v: v.tensor_copy(out=obf[:], in_=t64[:]))
            p.do("tensor", lambda t, tg=tg, k=k: t.matmul(
                psTT[0:1, :], onesf1[:, 0:1], obf[:],
                start=(tg == 0 and k == 0), stop=(tg == NG - 1 and k == 7),
                skip_group_check=True),
                wait=[("vec", p.cnt["vec"])], inc="mm")
        p.do("vector", lambda v: v.reduce_sum(p.musum[:], wv8[:], axis=AX.X),
             wait=[("vec", p.cnt["vec"])], inc="vec")
        p.do("vector", lambda v: v.reciprocal(p.sd[:], p.musum[:]), wait="vec", inc="vec")
        p.do("vector", lambda v: v.tensor_scalar(t64[:, 0:8], wv8[:], p.sd[:, 0:1], None,
                                                 OP.mult), wait="vec", inc="vec")
        p.do("sync", lambda sy, tg=tg: sy.dma_start(
            out=idxo[tg * 128:(tg + 1) * 128, :], in_=iv8[:]), wait="vec", inc="st", incn=16)
        p.do("sync", lambda sy, tg=tg: sy.dma_start(
            out=wno[tg * 128:(tg + 1) * 128, :], in_=t64[:, 0:8]), wait="vec", inc="st", incn=16)
    p.do("vector", lambda v: v.tensor_copy(out=p.scr[0:1, 0:64], in_=psTT[0:1, :]),
         wait="mm", inc="vec")
    p.do("sync", lambda sy: sy.dma_start(out=toto[:, :], in_=p.scr[0:1, 0:64]),
         wait="vec", inc="st", incn=16)
    p.do("sync", lambda sy: sy.dma_start(
        out=x3o.rearrange("(g p) d -> p g d", p=128),
        in_=resid[:].rearrange("p (g d) -> p g d", g=NG)),
        wait=[("vec", p.cnt["vec"])], inc="st", incn=16)
    p.do("sync", lambda sy: sy.nop(), wait=[("st", p.cnt["st"])])
    return p



def build_launch_B2(debug=False, scat=True):
    p = Prog("B2")
    idxi = p.din("idxi", [TPC, 8])
    wni = p.din("wni", [TPC, 8])
    tota = p.din("tota", [8, 64])
    premask = p.din("premask", [8, 1])
    gtok = p.din("gtok", [TPC, 1])
    uts = p.din("uts", [128, 128])
    plisto = p.dout("plisto", [E_EXP * CAP + 128, 1])
    goffo = p.dout("goffo", [TPC, 8])
    wvo = p.dout("wvo", [TPC, 8])
    zpl = p.sb("zpl2", [128, 257], F32)
    posmo = p.dout("posmo", [TPC, 8]) if debug else None
    mfo = p.dout("mfo", [TPC, 64]) if debug else None
    crowo = p.dout("crowo", [NG, 64]) if debug else None
    basebo = p.dout("basebo", [128, 64]) if debug else None
    sallo = p.dout("sallo", [TPC, 64]) if debug else None
    idxto = p.dout("idxto", [TPC, 8]) if debug else None

    totas = p.sb("totas2", [8, 64], F32)
    pres = p.sb("pres2", [8, 1], F32)
    utsb = p.sb("utsb2", [128, 128], BF16)
    onesb2 = p.sb("onesb2", [128, 1], BF16)
    onesf2 = p.sb("onesf2", [128, 128], F32)
    baseb = p.sb("baseb2", [128, 64], F32)
    brow = p.sb("brow2", [1, 64], F32)
    crow = p.sb("crow2", [1, 64], F32)
    iot64 = p.sb("iot642", [128, 64], F32)
    idxt = p.sb("idxt2", [128, 8], F32)
    wnt = p.sb("wnt2", [128, 8], F32)
    gtt = p.sb("gtt2", [128, 1], F32)
    Ok = [p.sb(f"Ok2_{k}", [128, 64], F32) for k in range(8)]
    Sall = p.sb("Sall2", [128, 64], F32)
    Sbf = p.sb("Sbf2", [128, 64], BF16)
    M1 = p.sb("M12", [128, 64], F32)
    Mfull = p.sb("Mfull2", [128, 64], F32)
    posm = p.sb("posm2", [128, 8], F32)
    valid = p.sb("valid2", [128, 8], F32)
    goff = p.sb("goff2", [128, 8], F32)
    gint = p.sb("gint2", [128, 8], I32)
    t8 = p.sb("t82", [128, 8], F32)
    psP = p.ps("psP2", [128, 64], F32)
    psC = p.ps("psC2", [128, 64], F32)
    psB = p.ps("psB2", [128, 64], F32)
    psBC = p.ps("psBC2", [128, 64], F32)

    p.dov(lambda v: v.memset(zpl[:], 0.0))
    NZ = (E_EXP * CAP + 128) // 128
    for j in range(NZ):
        p.do("sync", lambda sy, j=j: sy.dma_start(out=plisto[j * 128:(j + 1) * 128, :],
                                                  in_=zpl[:, j:j + 1]),
             wait=(["vec"] if j == 0 else None), inc="st", incn=16)
    p.do("sync", lambda sy: sy.dma_start(out=totas[:], in_=tota[:, :]), inc="ld", incn=16)
    p.do("sync", lambda sy: sy.dma_start(out=pres[:], in_=premask[:, :]), inc="ld", incn=16)
    p.do("gpsimd", lambda g: g.dma_start(out=utsb[:], in_=uts[:, :]), inc="lw", incn=16)
    p.dov(lambda v: v.memset(onesb2[:], 1.0))
    p.dov(lambda v: v.memset(onesf2[:], 1.0))
    p.do("gpsimd", lambda g: g.iota(iot64[:], pattern=[[1, 64]], base=0, channel_multiplier=0,
                                    allow_small_or_imprecise_dtypes=True), inc="gp")
    p.do("tensor", lambda t: t.matmul(psB[0:1, :], pres[0:8, 0:1], totas[0:8, :],
                                      start=True, stop=True), wait=["ld"], inc="mm")
    p.dov(lambda v: v.tensor_copy(out=brow[:], in_=psB[0:1, :]), wait="mm")
    p.do("tensor", lambda t: t.matmul(psBC[:], onesf2[0:1, 0:128], brow[0:1, :],
                                      start=True, stop=True), wait="vec", inc="mm")
    p.dov(lambda v: v.tensor_copy(out=baseb[:], in_=psBC[:]), wait="mm")
    if debug:
        p.do("sync", lambda sy: sy.dma_start(out=basebo[:, :], in_=baseb[:]),
             wait="vec", inc="st", incn=16)

    for tg in range(NG):
        p.do("sync", lambda sy, tg=tg: sy.dma_start(out=idxt[:], in_=idxi[tg * 128:(tg + 1) * 128, :]),
             wait=[("vec", p.cnt["vec"]), ("st", p.cnt["st"]), ("gp", p.cnt["gp"])], inc="ld", incn=16)
        p.do("sync", lambda sy, tg=tg: sy.dma_start(out=wnt[:], in_=wni[tg * 128:(tg + 1) * 128, :]),
             inc="ld", incn=16)
        p.do("sync", lambda sy, tg=tg: sy.dma_start(out=gtt[:], in_=gtok[tg * 128:(tg + 1) * 128, :]),
             inc="ld", incn=16)
        for k in range(8):
            p.dov(lambda v, k=k: v.tensor_scalar(Ok[k][:], iot64[:], idxt[:, k:k + 1],
                                                          None, OP.is_equal),
                 wait=[("ld", p.cnt["ld"])])
        p.dov(lambda v: v.tensor_copy(out=Sall[:], in_=Ok[0][:]))
        for k in range(1, 8):
            p.dov(lambda v, k=k: v.tensor_add(Sall[:], Sall[:], Ok[k][:]))
        p.dov(lambda v: v.tensor_copy(out=Sbf[:], in_=Sall[:]), wait="vec")
        if debug:
            p.do("sync", lambda sy, tg=tg: sy.dma_start(out=sallo[tg * 128:(tg + 1) * 128, :],
                                                        in_=Sall[:]), wait="vec", inc="st", incn=16)
            p.do("sync", lambda sy, tg=tg: sy.dma_start(out=idxto[tg * 128:(tg + 1) * 128, :],
                                                        in_=idxt[:]), inc="st", incn=16)
        if tg == 0:
            p.dov(lambda v: v.memset(crow[:], 0.0))
        else:
            p.dov(lambda v: v.tensor_copy(out=crow[:], in_=psC[0:1, :]),
                 wait=[("mm", p.cnt["mm"])])
        p.do("tensor", lambda t: t.matmul(psP[:], utsb[:], Sbf[:], start=True, stop=True),
             wait=[("vec", p.cnt["vec"]), ("lw", p.cnt["lw"])], inc="mm")
        p.do("tensor", lambda t, tg=tg: t.matmul(psC[0:1, :], onesb2[:, 0:1], Sbf[:],
                                                 start=(tg == 0), stop=(tg == NG - 1),
                                                 skip_group_check=True),
             wait=[("vec", p.cnt["vec"])], inc="mm")
        p.do("tensor", lambda t: t.matmul(psBC[:], onesf2[0:1, 0:128], crow[0:1, :],
                                          start=True, stop=True),
             wait=[("vec", p.cnt["vec"])], inc="mm")
        p.dov(lambda v: v.tensor_copy(out=M1[:], in_=psP[:]), wait="mm")
        p.dov(lambda v: v.tensor_tensor(M1[:], M1[:], psBC[:], OP.add))
        p.dov(lambda v: v.tensor_tensor(Mfull[:], M1[:], baseb[:], OP.add))
        if debug:
            p.do("sync", lambda sy, tg=tg: sy.dma_start(out=crowo[tg:tg + 1, :], in_=crow[:]),
                 wait="vec", inc="st", incn=16)
            p.do("sync", lambda sy, tg=tg: sy.dma_start(out=mfo[tg * 128:(tg + 1) * 128, :],
                                                        in_=Mfull[:]), inc="st", incn=16)
        for k in range(8):
            p.dov(lambda v, k=k: v.tensor_mul(Ok[k][:], Ok[k][:], Mfull[:]))
            p.dov(lambda v, k=k: v.reduce_sum(posm[:, k:k + 1], Ok[k][:], axis=AX.X))
        if debug:
            p.do("sync", lambda sy, tg=tg: sy.dma_start(out=posmo[tg * 128:(tg + 1) * 128, :],
                                                        in_=posm[:]), wait="vec", inc="st", incn=16)
        p.dov(lambda v: v.tensor_scalar(valid[:], posm[:], float(CAP), None, OP.is_lt),
             wait="vec")
        p.dov(lambda v: v.tensor_tensor(t8[:], wnt[:], valid[:], OP.mult))
        p.do("sync", lambda sy, tg=tg: sy.dma_start(out=wvo[tg * 128:(tg + 1) * 128, :], in_=t8[:]),
             wait="vec", inc="st", incn=16)
        p.dov(lambda v: v.tensor_scalar(goff[:], idxt[:], float(CAP), None, OP.mult))
        p.dov(lambda v: v.tensor_add(goff[:], goff[:], posm[:]))
        p.dov(lambda v: v.tensor_mul(t8[:], goff[:], valid[:]),
             wait=[("st", p.cnt["st"])])
        p.do("sync", lambda sy, tg=tg: sy.dma_start(out=goffo[tg * 128:(tg + 1) * 128, :],
                                                    in_=t8[:]), wait="vec", inc="st", incn=16)
        TRASH = float(E_EXP * CAP)
        p.dov(lambda v: v.tensor_scalar_add(goff[:], goff[:], -TRASH))
        p.dov(lambda v: v.tensor_mul(goff[:], goff[:], valid[:]))
        p.dov(lambda v: v.tensor_scalar_add(goff[:], goff[:], TRASH))
        p.dov(lambda v: v.tensor_scalar(goff[:], goff[:], TRASH, None, OP.min))
        p.dov(lambda v: v.tensor_scalar(goff[:], goff[:], 0.0, None, OP.max))
        p.dov(lambda v: v.tensor_copy(out=gint[:], in_=goff[:]))
        for k in range(8):
            if not scat:
                break
            p.do("gpsimd", lambda g, k=k: g.indirect_dma_start(
                plisto[0:128, :], IndirectOffsetOnAxis(gint[:, k:k + 1], 0),
                gtt[:], None),
                wait=[("vec", p.cnt["vec"]), ("st", NZ * 16)], inc="st", incn=16)
    p.do("sync", lambda sy: sy.nop(), wait=[("st", p.cnt["st"])])
    return p


def build_launch_C():
    p = Prog("C")
    plists = p.din("plists", [8, 128, 32])
    xall = p.din("xall", [B * S, D], BF16)
    wgs_in = p.din("wgi", [EPPC, D, FEXP])
    wus_in = p.din("wui", [EPPC, D, FEXP])
    wds_in = p.din("wdi", [EPPC, FEXP, D])
    ident = p.din("identC", [128, 128])
    yo = p.dout("yo", [EPPC * CAP, D], BF16)

    p.it128 = p.sb("it128C", [128, 128], BF16)
    pls = [p.sb(f"pls{i}", [128, 32], F32) for i in range(2)]
    fl = p.sb("flC", [128, 32], F32)
    t32 = p.sb("t32C", [128, 32], F32)
    gint = p.sb("gintC", [128, 32], I32)
    xbg = [p.sb(f"xbg{i}", [128, 1024], BF16) for i in range(4)]
    xbfm = p.sb("xbfmC", [128, DC * 512], BF16)
    wgs = p.sb("wgsC", [128, DC * FEXP], BF16)
    wus = p.sb("wusC", [128, DC * FEXP], BF16)
    wds = p.sb("wdsC", [128, 6 * D], BF16)
    t1 = p.sb("t1C", [128, 512], F32)
    t2 = p.sb("t2C", [128, 512], F32)
    hfm = p.sb("hfmC", [128, 6 * 512], BF16)
    yev = p.sb("yevC", [128, 512], BF16)
    psT = [p.ps(f"psTC{i}", [128, 128], BF16) for i in range(2)]
    psG = [p.ps(f"psGC{i}", [128, 512], F32) for i in range(4)]
    psD = [p.ps(f"psDC{i}", [128, 512], F32) for i in range(2)]

    p.do("gpsimd", lambda g: g.dma_start(out=p.it128[:], in_=ident[:, :]), inc="ld", incn=16)
    for s_ in range(8):
        buf = pls[s_ % 2]
        p.do("sync", lambda sy, s_=s_, buf=buf: sy.dma_start(out=buf[:], in_=plists[s_]),
             wait=[("vec", p.cnt["vec"])], inc="ld", incn=16)
        if s_ == 0:
            p.dov(lambda v, buf=buf: v.tensor_copy(out=fl[:], in_=buf[:]), wait="ld")
        else:
            p.dov(lambda v, buf=buf: v.tensor_add(fl[:], fl[:], buf[:]), wait="ld")
    p.dov(lambda v: v.tensor_scalar_add(t32[:], fl[:], -1.0))
    p.dov(lambda v: v.tensor_scalar_max(t32[:], t32[:], 0.0))
    p.dov(lambda v: v.tensor_copy(out=gint[:], in_=t32[:]))

    for e in range(EPPC):
        p.do("gpsimd", lambda g, e=e: g.dma_start(
            out=wgs[:].rearrange("p (dc f) -> p dc f", dc=DC),
            in_=wgs_in[e].rearrange("(dc p) f -> p dc f", p=128)),
            wait=[("mm", p.cnt["mm"])], inc="lw", incn=16)
        p.do("gpsimd", lambda g, e=e: g.dma_start(
            out=wus[:].rearrange("p (dc f) -> p dc f", dc=DC),
            in_=wus_in[e].rearrange("(dc p) f -> p dc f", p=128)), inc="lw", incn=16)
        p.do("gpsimd", lambda g, e=e: g.dma_start(
            out=wds[:].rearrange("p (fc c) -> p fc c", fc=6),
            in_=wds_in[e].rearrange("(fc p) c -> p fc c", p=128)), inc="lw", incn=16)
        lwc = p.cnt["lw"]
        for j4 in range(4):
            jj = e * 4 + j4
            p.do("gpsimd", lambda g, jj=jj, j4=j4: g.indirect_dma_start(
                xbg[j4][:], None, xall[0:128, :], IndirectOffsetOnAxis(gint[:, jj:jj + 1], 0)),
                wait=[("mm", p.cnt["mm"]), ("vec", p.cnt["vec"])], inc="ld", incn=16)
        p.do("vector", lambda v: v.nop(), wait=[("ld", p.cnt["ld"])], inc="vec")
        for j4 in range(4):
            emit_transpose(p, psT, [
                (xbg[j4][:, dc * 128:(dc + 1) * 128],
                 xbfm[:, dc * 512 + j4 * 128: dc * 512 + j4 * 128 + 128]) for dc in range(DC)])
        for fc in range(6):
            for wi, wbuf in ((0, wgs), (1, wus)):
                bank = 2 * (fc % 2) + wi
                for dc in range(DC):
                    p.do("tensor", lambda t, fc=fc, dc=dc, bank=bank, wbuf=wbuf: t.matmul(
                        psG[bank][:], wbuf[:, dc * FEXP + fc * 128: dc * FEXP + fc * 128 + 128],
                        xbfm[:, dc * 512:(dc + 1) * 512],
                        start=(dc == 0), stop=(dc == DC - 1)),
                        wait=(([("lw", lwc), ("vec", p.cnt["vec"]), ("act", p.cnt["act"])])
                              if dc == 0 else None),
                        inc=("mm" if dc == DC - 1 else None))
            p.do("scalar", lambda s, fc=fc: s.activation(
                out=t1[:], in_=psG[2 * (fc % 2)][:], func=AF.Silu),
                wait=["mm", ("vec", p.cnt["vec"])], inc="act")
            p.do("scalar", lambda s, fc=fc: s.activation(
                out=t2[:], in_=psG[2 * (fc % 2) + 1][:], func=AF.Identity), inc="act")
            p.do("vector", lambda v, fc=fc: v.tensor_tensor(
                hfm[:, fc * 512:(fc + 1) * 512], t1[:], t2[:], OP.mult),
                wait="act", inc="vec")
        for tsub in range(4):
            for half in range(2):
                bank = (tsub * 2 + half) % 2
                for fc in range(6):
                    p.do("tensor", lambda t, tsub=tsub, half=half, fc=fc, bank=bank: t.matmul(
                        psD[bank][:],
                        hfm[:, fc * 512 + tsub * 128: fc * 512 + tsub * 128 + 128],
                        wds[:, fc * 1024 + half * 512: fc * 1024 + half * 512 + 512],
                        start=(fc == 0), stop=(fc == 5)),
                        wait=([("vec", p.cnt["vec"]), ("st", p.cnt["st"])] if fc == 0 else None),
                        inc=("mm" if fc == 5 else None))
                p.do("vector", lambda v, bank=bank: v.tensor_copy(out=yev[:], in_=psD[bank][:]),
                     wait=["mm", ("st", p.cnt["st"])], inc="vec")
                p.do("sync", lambda sy, e=e, tsub=tsub, half=half: sy.dma_start(
                    out=yo[e * CAP + tsub * 128: e * CAP + tsub * 128 + 128,
                           half * 512:(half + 1) * 512], in_=yev[:]),
                    wait="vec", inc="st", incn=16)
    p.do("sync", lambda sy: sy.nop(), wait=[("st", p.cnt["st"])])
    return p


def build_launch_E():
    p = Prog("E")
    x3i = p.din("x3i", [TPC, D])
    xl4i = p.din("xl4i", [TPC, D], BF16)
    goffi = p.din("goffi", [TPC, 8])
    wvi = p.din("wvi", [TPC, 8])
    yall = p.din("yall", [E_EXP * CAP, D], BF16)
    wsgu = p.din("wsgu", [D, 2 * FEXP])
    wsd = p.din("wsd", [FEXP, D])
    ident = p.din("identE", [128, 128])
    x4o = p.dout("x4o", [TPC, D])

    resid = p.sb("residE", [128, NG * 1024], F32)
    p.it128 = p.sb("it128E", [128, 128], BF16)
    xl4t = p.sb("xl4tE", [128, 1024], BF16)
    x4fm = p.sb("x4fmE", [128, DC * TPC], BF16)
    wgus = p.sb("wgusE", [128, DC * 2 * FEXP], BF16)
    wsds = p.sb("wsdsE", [128, 6 * D], BF16)
    t1 = p.sb("t1E", [128, 512], F32)
    t2 = p.sb("t2E", [128, 512], F32)
    hsh = p.sb("hshE", [128, 6 * 512], BF16)
    goffs = p.sb("goffsE", [128, 8], F32)
    wvs = p.sb("wvsE", [128, 8], F32)
    gint = p.sb("gintE", [128, 8], I32)
    rows = [p.sb(f"rowsE{k}", [128, 1024], BF16) for k in range(8)]
    acc = p.sb("accE", [128, 1024], F32)
    psT = [p.ps(f"psTE{i}", [128, 128], BF16) for i in range(2)]
    psG = [p.ps(f"psGE{i}", [128, 512], F32) for i in range(4)]
    psD = [p.ps(f"psDE{i}", [128, 512], F32) for i in range(2)]

    p.do("gpsimd", lambda g: g.dma_start(out=p.it128[:], in_=ident[:, :]), inc="ld", incn=16)
    p.do("gpsimd", lambda g: g.dma_start(
        out=wgus[:].rearrange("p (dc f) -> p dc f", dc=DC),
        in_=wsgu.rearrange("(dc p) f -> p dc f", p=128)), inc="lw", incn=16)
    p.do("gpsimd", lambda g: g.dma_start(
        out=wsds[:].rearrange("p (fc c) -> p fc c", fc=6),
        in_=wsd.rearrange("(fc p) c -> p fc c", p=128)), inc="lw", incn=16)
    for g2 in range(NG):
        p.do("sync", lambda sy, g2=g2: sy.dma_start(
            out=resid[:, g2 * 1024:(g2 + 1) * 1024], in_=x3i[g2 * 128:(g2 + 1) * 128, :]),
            inc="ld", incn=16)
    ld_resid = p.cnt["ld"]
    for k in range(8):
        p.do("vector", lambda v, k=k: v.memset(rows[k][:], 0.0), inc="vec")
    for g2 in range(NG):
        p.do("sync", lambda sy, g2=g2: sy.dma_start(
            out=xl4t[:], in_=xl4i[g2 * 128:(g2 + 1) * 128, :]),
            wait=[("mm", p.cnt["mm"])], inc="ld", incn=16)
        p.do("vector", lambda v: v.nop(), wait=[("ld", p.cnt["ld"])], inc="vec")
        emit_transpose(p, psT, [
            (xl4t[:, dc * 128:(dc + 1) * 128],
             x4fm[:, dc * TPC + g2 * 128: dc * TPC + g2 * 128 + 128]) for dc in range(DC)])
    for st4 in range(4):
        for gc in range(6):
            for wi in range(2):
                bank = 2 * (gc % 2) + wi
                for dc in range(DC):
                    p.do("tensor", lambda t, st4=st4, gc=gc, wi=wi, dc=dc, bank=bank: t.matmul(
                        psG[bank][:],
                        wgus[:, dc * 2 * FEXP + wi * FEXP + gc * 128:
                             dc * 2 * FEXP + wi * FEXP + gc * 128 + 128],
                        x4fm[:, dc * TPC + st4 * 512: dc * TPC + st4 * 512 + 512],
                        start=(dc == 0), stop=(dc == DC - 1)),
                        wait=([("lw", p.cnt["lw"]), ("vec", p.cnt["vec"]),
                               ("act", p.cnt["act"])] if dc == 0 else None),
                        inc=("mm" if dc == DC - 1 else None))
            p.do("scalar", lambda s, gc=gc: s.activation(
                out=t1[:], in_=psG[2 * (gc % 2)][:], func=AF.Silu),
                wait=["mm", ("vec", p.cnt["vec"])], inc="act")
            p.do("scalar", lambda s, gc=gc: s.activation(
                out=t2[:], in_=psG[2 * (gc % 2) + 1][:], func=AF.Identity), inc="act")
            p.do("vector", lambda v, gc=gc: v.tensor_tensor(
                hsh[:, gc * 512:(gc + 1) * 512], t1[:], t2[:], OP.mult),
                wait="act", inc="vec")
        for tsub in range(4):
            for half in range(2):
                bank = (tsub * 2 + half) % 2
                for fc in range(6):
                    p.do("tensor", lambda t, st4=st4, tsub=tsub, half=half, fc=fc, bank=bank:
                         t.matmul(psD[bank][:],
                                  hsh[:, fc * 512 + tsub * 128: fc * 512 + tsub * 128 + 128],
                                  wsds[:, fc * 1024 + half * 512: fc * 1024 + half * 512 + 512],
                                  start=(fc == 0), stop=(fc == 5)),
                         wait=([("vec", p.cnt["vec"])] if fc == 0 else None),
                         inc=("mm" if fc == 5 else None))
                g2 = st4 * 4 + tsub
                p.do("vector", lambda v, g2=g2, half=half, bank=bank: v.tensor_tensor(
                    resid[:, g2 * 1024 + half * 512: g2 * 1024 + half * 512 + 512],
                    resid[:, g2 * 1024 + half * 512: g2 * 1024 + half * 512 + 512],
                    psD[bank][:], OP.add), wait=["mm", ("ld", ld_resid)], inc="vec")
    for tg in range(NG):
        p.do("sync", lambda sy, tg=tg: sy.dma_start(out=goffs[:], in_=goffi[tg * 128:(tg + 1) * 128, :]),
             wait=[("vec", p.cnt["vec"]), ("gp", p.cnt["gp"])], inc="ld", incn=16)
        p.do("sync", lambda sy, tg=tg: sy.dma_start(out=wvs[:], in_=wvi[tg * 128:(tg + 1) * 128, :]),
             inc="ld", incn=16)
        p.do("vector", lambda v: v.tensor_copy(out=gint[:], in_=goffs[:]),
             wait=[("ld", p.cnt["ld"])], inc="vec")
        p.do("vector", lambda v: v.nop(), inc="gp")
        for k in range(8):
            p.do("gpsimd", lambda g, k=k: g.indirect_dma_start(
                rows[k][:], None, yall[0:128, :],
                IndirectOffsetOnAxis(gint[:, k:k + 1], 0)),
                wait=[("gp", p.cnt["gp"]), ("vec", p.cnt["vec"])], inc="ld", incn=16)
        p.do("vector", lambda v: v.tensor_scalar(acc[:], rows[0][:], wvs[:, 0:1], None,
                                                 OP.mult), wait=[("ld", p.cnt["ld"])], inc="vec")
        for k in range(1, 8):
            p.do("vector", lambda v, k=k: v.scalar_tensor_tensor(
                acc[:], rows[k][:], wvs[:, k:k + 1], acc[:],
                OP.mult, OP.add), wait="vec", inc="vec")
        p.do("vector", lambda v, tg=tg: v.tensor_tensor(
            resid[:, tg * 1024:(tg + 1) * 1024], resid[:, tg * 1024:(tg + 1) * 1024],
            acc[:], OP.add), wait="vec", inc="vec")
    p.do("sync", lambda sy: sy.dma_start(
        out=x4o.rearrange("(g p) d -> p g d", p=128),
        in_=resid[:].rearrange("p (g d) -> p g d", g=NG)),
        wait=[("vec", p.cnt["vec"])], inc="st", incn=16)
    p.do("sync", lambda sy: sy.nop(), wait=[("st", p.cnt["st"])])
    return p


def host_layouts(inputs):
    inp = {k: np.asarray(v) for k, v in inputs.items()}
    L = {}
    wqkv = np.ascontiguousarray(inp["L0_attn_W_qkv"].reshape(D, H, HD, 3))
    L["wq0"] = np.ascontiguousarray(wqkv[..., 0].reshape(D, D))
    L["wk0"] = np.ascontiguousarray(wqkv[..., 1].reshape(D, D))
    L["wv0"] = np.ascontiguousarray(wqkv[..., 2].reshape(D, D))
    L["wout"] = np.ascontiguousarray(inp["L0_attn_W_out"])
    L["w1t"] = np.ascontiguousarray(inp["L0_ffn_w1_w"].T)
    L["w2t"] = np.ascontiguousarray(inp["L0_ffn_w2_w"].T)
    L["h1t"] = np.ascontiguousarray(inp["L0_ffn_h1_w"].T)
    L["h2t"] = np.ascontiguousarray(inp["L0_ffn_h2_w"].T)
    L["w3t"] = np.ascontiguousarray(inp["L0_ffn_w3_w"].T)
    for nm, key in (("s1", "L0_ffn_w1_s"), ("s2", "L0_ffn_w2_s"),
                    ("sh1", "L0_ffn_h1_s"), ("sh2", "L0_ffn_h2_s"), ("s3", "L0_ffn_w3_s")):
        L[nm] = np.ascontiguousarray(inp[key].reshape(-1, 128).T)
    L["wq1"] = np.ascontiguousarray(inp["L1_attn_Wq"].reshape(D, D))
    L["wk1"] = np.ascontiguousarray(inp["L1_attn_Wk"].reshape(D, D))
    L["wv1"] = np.ascontiguousarray(inp["L1_attn_Wv"].reshape(D, D))
    L["wo1"] = np.ascontiguousarray(inp["L1_attn_Wo"].reshape(D, D))
    L["bq1"] = np.ascontiguousarray(inp["L1_attn_bq"].reshape(1, D))
    L["bk1"] = np.ascontiguousarray(inp["L1_attn_bk"].reshape(1, D))
    L["bv1"] = np.ascontiguousarray(inp["L1_attn_bv"].reshape(1, D))
    L["bo1"] = np.ascontiguousarray(inp["L1_attn_bo"].reshape(1, D))
    L["rt"] = np.ascontiguousarray(inp["L1_ffn_router_w"].T)
    L["wsgu"] = np.ascontiguousarray(inp["L1_ffn_ws_gate_up"])
    L["wsd"] = np.ascontiguousarray(inp["L1_ffn_ws_down"])
    L["wg"] = inp["L1_ffn_w_gate"]
    L["wu"] = inp["L1_ffn_w_up"]
    L["wd"] = inp["L1_ffn_w_down"]
    L["X"] = np.ascontiguousarray(inp["L0_attn_X"].reshape(B * S, D))
    L["ident"] = np.eye(128, dtype=np.float32)
    return L


def run_A(L, debug=False, stages=4):
    p = build_launch_A(debug, stages)
    nc = p.finalize()
    in_maps = []
    for c in range(NCORE):
        bi = c // 4
        im = dict(
            xb=L["X"][bi * S:(bi + 1) * S],
            xmy=L["X"][c * TPC:(c + 1) * TPC],
            wq0=L["wq0"], wk0=L["wk0"], wv0=L["wv0"], wout=L["wout"],
            w1t=L["w1t"], w2t=L["w2t"], h1t=L["h1t"], h2t=L["h2t"], w3t=L["w3t"],
            s1=L["s1"], s2=L["s2"], sh1=L["sh1"], sh2=L["sh2"], s3=L["s3"],
            wq1=L["wq1"], wk1=L["wk1"], wv1=L["wv1"],
            bq1=L["bq1"], bk1=L["bk1"], bv1=L["bv1"],
            ident=L["ident"],
        )
        in_maps.append(im)
    res = run_bass_kernel_spmd(nc, in_maps, core_ids=list(range(NCORE)))
    return res.results


def _run(build_fn, in_maps, _cache={}):
    key = build_fn.__name__
    if key not in _cache:
        p = build_fn() if build_fn is not build_launch_A else build_launch_A(False, 4)
        _cache[key] = p.finalize()
    nc = _cache[key]
    try:
        res = run_bass_kernel_spmd(nc, in_maps, core_ids=list(range(NCORE)))
    except Exception:
        res = run_bass_kernel_spmd(nc, in_maps, core_ids=list(range(NCORE)))
    return res.results


def kernel(**inputs):
    L = host_layouts(inputs)
    strict_upper = np.triu(np.ones((128, 128), np.float32), 1)

    inA = []
    for c in range(NCORE):
        bi = c // 4
        inA.append(dict(
            xb=L["X"][bi * S:(bi + 1) * S], xmy=L["X"][c * TPC:(c + 1) * TPC],
            wq0=L["wq0"], wk0=L["wk0"], wv0=L["wv0"], wout=L["wout"],
            w1t=L["w1t"], w2t=L["w2t"], h1t=L["h1t"], h2t=L["h2t"], w3t=L["w3t"],
            s1=L["s1"], s2=L["s2"], sh1=L["sh1"], sh2=L["sh2"], s3=L["s3"],
            wq1=L["wq1"], wk1=L["wk1"], wv1=L["wv1"],
            bq1=L["bq1"], bk1=L["bk1"], bv1=L["bv1"], ident=L["ident"]))
    rA = _run(build_launch_A, inA)

    inB = []
    for c in range(NCORE):
        grp = range(0, 4) if c < 4 else range(4, 8)
        kvp = np.stack([rA[g]["kvo"] for g in grp], 0)
        inB.append(dict(x2i=rA[c]["x2o"], qfi=rA[c]["qfo"], kvp=kvp,
                        wo1=L["wo1"], bo1=L["bo1"], rt=L["rt"], ident=L["ident"]))
    rB = _run(build_launch_B, inB)

    tota = np.concatenate([rB[c]["toto"] for c in range(NCORE)], 0)
    inB2 = []
    for c in range(NCORE):
        premask = (np.arange(NCORE) < c).astype(np.float32).reshape(8, 1)
        gtok = (np.arange(TPC, dtype=np.float32) + c * TPC + 1.0).reshape(TPC, 1)
        inB2.append(dict(idxi=rB[c]["idxo"], wni=rB[c]["wno"], tota=tota,
                         premask=premask, gtok=gtok, uts=strict_upper))
    rB2 = _run(build_launch_B2, inB2)

    xall = np.concatenate([rB[c]["xl4o"] for c in range(NCORE)], 0)
    plist_all = [np.asarray(rB2[c]["plisto"])[:E_EXP * CAP].reshape(E_EXP, CAP)
                 for c in range(NCORE)]
    inC = []
    for c in range(NCORE):
        mine = np.stack([pl[c * EPPC:(c + 1) * EPPC].reshape(-1).reshape(32, 128).T.copy()
                         for pl in plist_all], 0)
        inC.append(dict(plists=np.ascontiguousarray(mine), xall=xall,
                        wgi=L["wg"][c * EPPC:(c + 1) * EPPC],
                        wui=L["wu"][c * EPPC:(c + 1) * EPPC],
                        wdi=L["wd"][c * EPPC:(c + 1) * EPPC], identC=L["ident"]))
    rC = _run(build_launch_C, inC)

    yall = np.concatenate([rC[c]["yo"] for c in range(NCORE)], 0)
    inE = []
    for c in range(NCORE):
        inE.append(dict(x3i=rB[c]["x3o"], xl4i=rB[c]["xl4o"], goffi=rB2[c]["goffo"],
                        wvi=rB2[c]["wvo"], yall=yall, wsgu=L["wsgu"], wsd=L["wsd"],
                        identE=L["ident"]))
    rE = _run(build_launch_E, inE)

    out = np.concatenate([rE[c]["x4o"] for c in range(NCORE)], 0).reshape(B, S, D)
    return out.astype(np.float32)


# version 02
# speedup vs baseline: 1.2504x; 1.2504x over previous
import math
from contextlib import ExitStack

import numpy as np

import concourse.bass as bass
import concourse.mybir as mybir
from concourse.bass_core_rust import IndirectOffsetOnAxis
from concourse.bass_utils import run_bass_kernel_spmd

F32 = mybir.dt.float32
BF16 = mybir.dt.bfloat16
I32 = mybir.dt.int32
AF = mybir.ActivationFunctionType
OP = mybir.AluOpType
AX = mybir.AxisListType

NCORE = 8
B, S, D = 2, 8192, 1024
TPC = 2048
NG = TPC // 128
DC = D // 128
H, HD = 16, 64
CSHIFT = 16.0
EPS_LN = 1e-5
E_EXP, TOPK, CAP, FEXP = 64, 8, 512, 768
EPPC = E_EXP // NCORE
BIG = 65536.0

ENGINES = ("sync", "gpsimd", "tensor", "scalar", "vector")


class Prog:

    def __init__(self, name):
        self.nc = bass.Bass("TRN2", target_bir_lowering=False)
        self.es = ExitStack()
        self.streams = {e: [] for e in ENGINES}
        self.sem = {}
        self.cnt = {}
        for n in ("ld", "lw", "mm", "act", "vec", "st", "gp"):
            self.sem[n] = self.es.enter_context(self.nc.semaphore(n))
            self.cnt[n] = 0
        self.ap = {}

    def din(self, name, shape, dt=F32):
        self.ap[name] = self.nc.dram_tensor(name, shape, dt, kind="ExternalInput").ap()
        return self.ap[name]

    def dout(self, name, shape, dt=F32):
        self.ap[name] = self.nc.dram_tensor(name, shape, dt, kind="ExternalOutput").ap()
        return self.ap[name]

    def dtmp(self, name, shape, dt=F32):
        self.ap[name] = self.nc.dram_tensor(name, shape, dt).ap()
        return self.ap[name]

    def sb(self, name, shape, dt=F32, es=None):
        return (es or self.es).enter_context(self.nc.sbuf_tensor(name, shape, dt))

    def ps(self, name, shape, dt=F32, es=None):
        return (es or self.es).enter_context(self.nc.psum_tensor(name, shape, dt))

    def dov(self, emit, wait=None):
        w = [("vec", self.cnt["vec"])]
        if wait is not None:
            w += wait if isinstance(wait, list) else [wait]
        self.do("vector", emit, wait=w, inc="vec")

    def do(self, eng, emit, wait=None, inc=None, incn=1):
        w = []
        if wait is not None:
            for it in (wait if isinstance(wait, list) else [wait]):
                if isinstance(it, tuple):
                    w.append(it)
                else:
                    w.append((it, self.cnt[it]))
        self.streams[eng].append((w, emit, inc, incn))
        if inc is not None:
            self.cnt[inc] += incn

    def finalize(self):
        with self.es, self.nc.Block() as blk:
            for engname in ENGINES:
                items = self.streams[engname]
                if not items:
                    continue
                deco = getattr(blk, engname)

                def make(items):
                    def body(eng):
                        for w, emit, inc, incn in items:
                            for s, v in w:
                                if v > 0:
                                    eng.wait_ge(self.sem[s], v)
                            instr = emit(eng)
                            if inc is not None:
                                instr.then_inc(self.sem[inc], incn)
                    return body

                deco(make(items))
        return self.nc


def emit_ln(p, src, out_bf, w0=None, out_f32=None):
    p.do("scalar", lambda s, a=src: s.activation(
        out=p.scr[:], in_=a, func=AF.Square, accum_out=p.sq[:]),
        wait=(list(w0) if w0 else []) + [("vec", p.cnt["vec"])], inc="act")
    p.do("vector", lambda v, a=src: v.reduce_sum(p.musum[:], a, axis=AX.X),
         wait=(list(w0) if w0 else None), inc="vec")
    p.do("vector", lambda v: v.tensor_scalar(p.mu[:], p.musum[:], 1.0 / 1024.0, None, OP.mult),
         wait="vec", inc="vec")
    p.do("vector", lambda v: v.tensor_mul(p.musq[:], p.mu[:], p.mu[:]),
         wait=[("vec", p.cnt["vec"]), ("act", p.cnt["act"])], inc="vec")
    p.do("vector", lambda v: v.scalar_tensor_tensor(
        p.var[:], p.sq[:], 1.0 / 1024.0, p.musq[:], OP.mult, OP.subtract),
        wait=["vec", ("act", p.cnt["act"])], inc="vec")
    p.do("scalar", lambda s: s.activation(out=p.sd[:], in_=p.var[:], func=AF.Sqrt,
                                          bias=p.epsb[:]), wait="vec", inc="act")
    p.do("vector", lambda v: v.reciprocal(p.rstd[:], p.sd[:]), wait="act", inc="vec")
    p.do("vector", lambda v, a=src, o=out_bf: v.tensor_scalar(
        o, a, p.mu[:, 0:1], p.rstd[:, 0:1], OP.subtract, OP.mult), wait="vec", inc="vec")
    if out_f32 is not None:
        p.do("vector", lambda v, a=src, o=out_f32: v.tensor_scalar(
            o, a, p.mu[:, 0:1], p.rstd[:, 0:1], OP.subtract, OP.mult), inc="vec")


def emit_transpose(p, psT, pairs):
    for i, (srcs, dsts) in enumerate(pairs):
        bank = i % 2
        p.do("tensor", lambda t, s=srcs, b=bank: t.transpose(psT[b][:], s, p.it128[:]),
             wait=[("vec", p.cnt["vec"])], inc="mm")
        p.do("vector", lambda v, d=dsts, b=bank: v.tensor_copy(out=d, in_=psT[b][:]),
             wait="mm", inc="vec")


def emit_ln_tile(p, psT, srcs, xfm, gbase=0, w0s=None, out_f32_slices=None):
    for g in range(4):
        src = srcs[g]
        w0 = (w0s[g] if w0s else None)
        ww = list(w0) if w0 else []
        ww.append(("mm", p.cnt["mm"]))
        emit_ln(p, src, p.xlg[:], w0=ww,
                out_f32=(out_f32_slices[g] if out_f32_slices else None))
        emit_transpose(p, psT, [
            (p.xlg[:, dc * 128:(dc + 1) * 128],
             xfm[:, dc * 512 + g * 128: dc * 512 + g * 128 + 128]) for dc in range(DC)])


def build_launch_A(debug=False, stages=4):
    p = Prog("A")
    xb = p.din("xb", [S, D])
    xmy = p.din("xmy", [TPC, D])
    wq0 = p.din("wq0", [D, D]); wk0 = p.din("wk0", [D, D]); wv0 = p.din("wv0", [D, D])
    wout = p.din("wout", [D, D])
    w1t = p.din("w1t", [D, 4096]); w2t = p.din("w2t", [D, 4096])
    h1t = p.din("h1t", [4096, 4096]); h2t = p.din("h2t", [4096, 4096])
    w3t = p.din("w3t", [4096, D])
    s1 = p.din("s1", [128, 32]); s2 = p.din("s2", [128, 32])
    sh1 = p.din("sh1", [128, 32]); sh2 = p.din("sh2", [128, 32])
    s3 = p.din("s3", [128, 8])
    wq1 = p.din("wq1", [D, D]); wk1 = p.din("wk1", [D, D]); wv1 = p.din("wv1", [D, D])
    bq1 = p.din("bq1", [1, D]); bk1 = p.din("bk1", [1, D]); bv1 = p.din("bv1", [1, D])
    ident = p.din("ident", [128, 128])

    mk = p.dout if debug else p.dtmp
    kfull = mk("kfull", [D, S], BF16)
    vfull = mk("vfull", [S, D], BF16)
    h1b = p.dtmp("h1b", [32, 128, 32 * 128], BF16)
    h2b = p.dtmp("h2b", [32, 128, 32 * 128], BF16)
    w12b = p.dtmp("w12b", [32, 128, 2 * DC * 128], BF16)
    w3b = p.dtmp("w3b", [8, 128, 32 * 128], BF16)

    x2o = p.dout("x2o", [TPC, D])
    x1o = p.dout("x1o", [TPC, D]) if debug else None
    kchk = p.dout("kchk", [128, 1024], BF16) if debug else None
    attnpre = p.dout("attnpre", [D, TPC], BF16) if debug else None
    qbo = p.dout("qbo", [D, TPC], BF16) if debug else None
    xfmo = p.dout("xfmo", [128, DC * 512], BF16) if debug else None
    wqso = p.dout("wqso", [128, DC * 1024], BF16) if debug else None
    wqso2 = p.dout("wqso2", [128, DC * 1024], BF16) if debug else None
    wkso = p.dout("wkso", [128, DC * 1024], BF16) if debug else None
    xfmo2 = p.dout("xfmo2", [128, DC * 512], BF16) if debug else None
    psoo = p.dout("psoo", [128, 512]) if debug else None
    qbo2 = p.dout("qbo2", [D, TPC], BF16) if debug else None
    pswo = p.dout("pswo", [NG, 128, 512]) if debug else None
    attno = p.dout("attno", [D, TPC], BF16) if debug else None
    qfo = p.dout("qfo", [D, TPC], BF16)
    kvo = p.dout("kvo", [H, HD, 65])

    resid = p.sb("resid", [128, NG * 1024], F32)
    p.it128 = p.sb("it128", [128, 128], BF16)
    onesbf = p.sb("onesbf", [128, 512], BF16)
    onesf32 = p.sb("onesf32", [128, 64], F32)
    p.scr = p.sb("scr", [128, 1024], F32)
    p.mu = p.sb("mu", [128, 1]); p.sq = p.sb("sq", [128, 1])
    p.var = p.sb("var", [128, 1]); p.rstd = p.sb("rstd", [128, 1])
    p.musum = p.sb("musum", [128, 1]); p.musq = p.sb("musq", [128, 1])
    p.sd = p.sb("sd", [128, 1])
    p.xlg = p.sb("xlg", [128, 1024], BF16)
    p.epsb = p.sb("epsb", [128, 1], F32)
    p.cshb = p.sb("cshb", [128, 1], F32)

    p.do("gpsimd", lambda g: g.dma_start(out=p.it128[:], in_=ident[:, :]), inc="ld", incn=16)
    for g in range(NG):
        p.do("sync", lambda sy, g=g: sy.dma_start(
            out=resid[:, g * 1024:(g + 1) * 1024], in_=xmy[g * 128:(g + 1) * 128, :]),
            inc="ld", incn=16)
    p.do("vector", lambda v: v.memset(onesbf[:], 1.0), inc="vec")
    p.do("vector", lambda v: v.memset(p.epsb[:], EPS_LN), inc="vec")
    p.do("vector", lambda v: v.memset(p.cshb[:], -CSHIFT), inc="vec")
    p.do("vector", lambda v: v.memset(onesf32[:], 1.0), inc="vec")

    with ExitStack() as e12:
        qbuf = p.sb("qbuf", [128, DC * TPC], BF16, e12)
        attn = p.sb("attn", [128, DC * TPC], BF16, e12)

        with ExitStack() as e1:
            wqs = p.sb("wqs", [128, DC * 1024], BF16, e1)
            wks = p.sb("wks", [128, DC * 1024], BF16, e1)
            wvs = p.sb("wvs", [128, DC * 1024], BF16, e1)
            xg = [p.sb(f"xg{i}", [128, 1024], F32, e1) for i in range(2)]
            xfmA = p.sb("xfm1", [128, DC * 512], BF16, e1)
            kev = [p.sb(f"kev{i}", [128, 512], BF16, e1) for i in range(2)]
            vev = [p.sb(f"vev{i}", [128, 512], BF16, e1) for i in range(2)]
            psT1 = [p.ps(f"psT1{i}", [128, 128], BF16, e1) for i in range(2)]
            psO1 = [p.ps(f"psO1{i}", [128, 512], F32, e1) for i in range(4)]

            for w, t_ in ((wq0, wqs), (wk0, wks), (wv0, wvs)):
                p.do("gpsimd", lambda g, w=w, t_=t_: g.dma_start(
                    out=t_[:].rearrange("p (dc c) -> p dc c", dc=DC),
                    in_=w.rearrange("(dc p) c -> p dc c", p=128)), inc="lw", incn=16)
            lw_w = p.cnt["lw"]
            for oc in range(32):
                for hsrc, hdst in ((h1t, h1b), (h2t, h2b)):
                    p.do("gpsimd", lambda g, oc=oc, hsrc=hsrc, hdst=hdst: g.dma_start(
                        out=hdst[oc].rearrange("p (ic c) -> p ic c", ic=32),
                        in_=hsrc.rearrange("(ic p) o -> p ic o", p=128)[:, :, oc * 128:(oc + 1) * 128]),
                        inc="lw", incn=16)
                for wi, wsrc in ((0, w1t), (1, w2t)):
                    p.do("gpsimd", lambda g, oc=oc, wi=wi, wsrc=wsrc: g.dma_start(
                        out=w12b[oc].rearrange("p (w dc c) -> p w dc c", w=2, dc=DC)[:, wi],
                        in_=wsrc.rearrange("(dc p) o -> p dc o", p=128)[:, :, oc * 128:(oc + 1) * 128]),
                        inc="lw", incn=16)
                if oc < 8:
                    p.do("gpsimd", lambda g, oc=oc: g.dma_start(
                        out=w3b[oc].rearrange("p (ic c) -> p ic c", ic=32),
                        in_=w3t.rearrange("(ic p) o -> p ic o", p=128)[:, :, oc * 128:(oc + 1) * 128]),
                        inc="lw", incn=16)
            lw_cast = p.cnt["lw"]

            def proj_fm(wbuf, dst_fn, extra_wait):
                for cc in range(DC):
                    bank = cc % 2
                    for dc in range(DC):
                        p.do("tensor", lambda t, cc=cc, dc=dc, bank=bank, wbuf=wbuf: t.matmul(
                            psO1[bank][:], wbuf[:, dc * 1024 + cc * 128: dc * 1024 + cc * 128 + 128],
                            xfmA[:, dc * 512:(dc + 1) * 512],
                            start=(dc == 0), stop=(dc == DC - 1)),
                            wait=(([("lw", lw_w), ("vec", p.cnt["vec"])] + list(extra_wait))
                                  if dc == 0 else None),
                            inc=("mm" if dc == DC - 1 else None))
                    dst_fn(cc, bank)

            if debug:
                p.do("sync", lambda sy: sy.dma_start(out=wqso[:, :], in_=wqs[:]),
                     wait=[("lw", p.cnt["lw"])], inc="st", incn=16)
                p.do("sync", lambda sy: sy.dma_start(out=wkso[:, :], in_=wks[:]),
                     inc="st", incn=16)
            for bt in range(16):
                ld0 = p.cnt["ld"]
                for g in range(4):
                    xgb = xg[g % 2]
                    p.do("sync", lambda sy, bt=bt, g=g, xgb=xgb: sy.dma_start(
                        out=xgb[:], in_=xb[bt * 512 + g * 128: bt * 512 + (g + 1) * 128, :]),
                        wait=[("act", p.cnt["act"]), ("vec", p.cnt["vec"])], inc="ld", incn=16)
                    emit_ln(p, xgb[:], p.xlg[:],
                            w0=[("ld", p.cnt["ld"]), ("mm", p.cnt["mm"])])
                    emit_transpose(p, psT1, [
                        (p.xlg[:, dc * 128:(dc + 1) * 128],
                         xfmA[:, dc * 512 + g * 128: dc * 512 + g * 128 + 128]) for dc in range(DC)])

                def k_dst(cc, bank):
                    kb = kev[cc % 2]
                    p.do("vector", lambda v, kb=kb, bank=bank: v.tensor_copy(
                        out=kb[:], in_=psO1[bank][:]),
                        wait=["mm", ("st", p.cnt["st"])], inc="vec")
                    p.do("sync", lambda sy, kb=kb, cc=cc, bt=bt: sy.dma_start(
                        out=kfull[cc * 128:(cc + 1) * 128, bt * 512:(bt + 1) * 512], in_=kb[:]),
                        wait="vec", inc="st", incn=16)
                proj_fm(wks, k_dst, [])
                for ts in range(4):
                    for half in range(2):
                        bank = 2 + (ts * 2 + half) % 2
                        vb = vev[(ts * 2 + half) % 2]
                        for dc in range(DC):
                            p.do("tensor", lambda t, ts=ts, half=half, dc=dc, bank=bank: t.matmul(
                                psO1[bank][:],
                                xfmA[:, dc * 512 + ts * 128: dc * 512 + ts * 128 + 128],
                                wvs[:, dc * 1024 + half * 512: dc * 1024 + half * 512 + 512],
                                start=(dc == 0), stop=(dc == DC - 1)),
                                wait=([("vec", p.cnt["vec"])] if dc == 0 else None),
                                inc=("mm" if dc == DC - 1 else None))
                        p.do("vector", lambda v, vb=vb, bank=bank: v.tensor_copy(
                            out=vb[:], in_=psO1[bank][:]),
                            wait=["mm", ("st", p.cnt["st"])], inc="vec")
                        p.do("sync", lambda sy, vb=vb, ts=ts, half=half, bt=bt: sy.dma_start(
                            out=vfull[bt * 512 + ts * 128: bt * 512 + ts * 128 + 128,
                                      half * 512:(half + 1) * 512], in_=vb[:]),
                            wait="vec", inc="st", incn=16)

            if debug:
                p.do("sync", lambda sy: sy.dma_start(out=wqso2[:, :], in_=wqs[:]),
                     wait=[("vec", p.cnt["vec"]), ("mm", p.cnt["mm"]),
                           ("act", p.cnt["act"]), ("st", p.cnt["st"])], inc="st", incn=16)
            for qt in range(4):
                if debug and qt == 1:
                    for _cc in range(DC):
                        p.do("sync", lambda sy, _cc=_cc: sy.dma_start(
                            out=xfmo[:, _cc * 512:(_cc + 1) * 512],
                            in_=xfmA[:, _cc * 512:(_cc + 1) * 512]),
                            wait=[("vec", p.cnt["vec"]), ("mm", p.cnt["mm"])], inc="st", incn=16)
                    p.do("vector", lambda v: v.nop(), wait=[("st", p.cnt["st"])], inc="vec")
                emit_ln_tile(p, psT1,
                             [resid[:, (qt * 4 + g) * 1024:(qt * 4 + g + 1) * 1024] for g in range(4)],
                             xfmA)

                def q_dst(cc, bank, qt=qt):
                    if debug and qt == 0 and cc == 0:
                        p.do("vector", lambda v, bank=bank: v.tensor_copy(
                            out=p.scr[:, 0:512], in_=psO1[bank][:]), wait="mm", inc="vec")
                        p.do("sync", lambda sy: sy.dma_start(out=psoo[:, :], in_=p.scr[:, 0:512]),
                             wait="vec", inc="st", incn=16)
                        p.do("vector", lambda v: v.nop(), wait=[("st", p.cnt["st"])], inc="vec")
                    p.do("vector", lambda v, cc=cc, bank=bank, qt=qt: v.tensor_copy(
                        out=qbuf[:, cc * TPC + qt * 512: cc * TPC + qt * 512 + 512],
                        in_=psO1[bank][:]), wait="mm", inc="vec")
                proj_fm(wqs, q_dst, [])

        if debug:
            for cc in range(DC):
                p.do("sync", lambda sy, cc=cc: sy.dma_start(
                    out=qbo[cc * 128:(cc + 1) * 128, :], in_=qbuf[:, cc * TPC:(cc + 1) * TPC]),
                    wait=[("vec", p.cnt["vec"])], inc="st", incn=16)
        if stages < 1.5:
            p.do("sync", lambda sy: sy.nop(), wait=[("st", p.cnt["st"])])
            return p
        with ExitStack() as e2:
            kpair = p.sb("kpair", [128, S], BF16, e2)
            vones = [p.sb(f"vones{i}", [128, 64 * 65], BF16, e2) for i in range(2)]
            pT = [p.sb(f"pT{i}", [128, 1024], BF16, e2) for i in range(2)]
            znorm = p.sb("znorm", [128, 512], F32, e2)
            attnS = p.sb("attnS", [128, 512], F32, e2)
            psS = [[p.ps(f"psS{i}_{j}", [128, 512], F32, e2) for j in range(2)] for i in range(2)]
            psAV = p.ps("psAV", [128, 512], F32, e2)
            psN = p.ps("psN", [128, 512], F32, e2)

            for i in range(2):
                p.do("vector", lambda v, i=i: v.memset(
                    vones[i][:].rearrange("p (j m) -> p j m", m=65)[:, :, 64:65], 1.0), inc="vec")

            hp_list = range(8) if stages >= 2.0 else ([0] if stages >= 1.5 else [])
            for hp in hp_list:
                p.do("sync", lambda sy, hp=hp: sy.dma_start(
                    out=kpair[0:64, :], in_=kfull[(2 * hp) * 64:(2 * hp) * 64 + 64, :]),
                    wait=[("mm", p.cnt["mm"])], inc="ld", incn=16)
                p.do("sync", lambda sy, hp=hp: sy.dma_start(
                    out=kpair[64:128, :], in_=kfull[(2 * hp + 1) * 64:(2 * hp + 1) * 64 + 64, :]),
                    inc="ld", incn=16)
                for i in range(2):
                    h = 2 * hp + i
                    p.do("sync", lambda sy, h=h, i=i: sy.dma_start(
                        out=vones[i][:].rearrange("p (j m) -> p j m", m=65)[:, :, 0:64],
                        in_=vfull.rearrange("(j p) c -> p j c", p=128)[:, :, h * 64:(h + 1) * 64]),
                        inc="ld", incn=16)
                ld_hp = p.cnt["ld"]
                for i in (range(2) if stages >= 2.0 else []):
                    hbase = i * 64
                    for qt in range(4):
                        act_base = p.cnt["act"]
                        for g in range(32):
                            for j in range(2):
                                kt = 2 * g + j
                                w = None
                                if j == 0:
                                    w = [("act", p.cnt["act"]), ("vec", p.cnt["vec"]),
                                         ("ld", ld_hp)]
                                p.do("tensor", lambda t, g=g, j=j, kt=kt, hbase=hbase, hp=hp, qt=qt:
                                     t.matmul(psS[g % 2][j][:],
                                              kpair[hbase:hbase + 64, kt * 128:(kt + 1) * 128],
                                              qbuf[hbase:hbase + 64,
                                                   hp * 2048 + qt * 512: hp * 2048 + qt * 512 + 512],
                                              start=True, stop=True),
                                     wait=w, inc=("mm" if j == 1 else None))
                            if g >= 1:
                                for j in range(2):
                                    kt = 2 * (g - 1) + j
                                    p.do("tensor", lambda t, g=g, j=j, kt=kt, i=i:
                                         t.matmul(psAV[0:65, :],
                                                  vones[i][:, kt * 65: kt * 65 + 65],
                                                  pT[(g - 1) % 2][:, j * 512:(j + 1) * 512],
                                                  start=(kt == 0), stop=False),
                                         wait=([("act", p.cnt["act"])] if j == 0 else None),
                                         inc=("mm" if j == 1 else None))
                            p.do("scalar", lambda s, g=g: s.activation(
                                out=pT[g % 2][:, 0:512], in_=psS[g % 2][0][:], func=AF.Exp,
                                scale=0.125, bias=p.cshb[:]),
                                wait=[("mm", p.cnt["mm"])])
                            p.do("scalar", lambda s, g=g: s.activation(
                                out=pT[g % 2][:, 512:1024], in_=psS[g % 2][1][:], func=AF.Exp,
                                scale=0.125, bias=p.cshb[:]), inc="act")
                        for j in range(2):
                            kt = 62 + j
                            p.do("tensor", lambda t, j=j, kt=kt, i=i:
                                 t.matmul(psAV[0:65, :], vones[i][:, kt * 65: kt * 65 + 65],
                                          pT[1][:, j * 512:(j + 1) * 512],
                                          start=False, stop=(j == 1)),
                                 wait=([("act", p.cnt["act"])] if j == 0 else None),
                                 inc=("mm" if j == 1 else None))
                        p.do("vector", lambda v: v.reciprocal(znorm[64:65, :], psAV[64:65, :]),
                             wait="mm", inc="vec")
                        p.do("tensor", lambda t: t.matmul(psN[0:64, :], onesf32[64:65, 0:64],
                                                          znorm[64:65, :], start=True, stop=True),
                             wait="vec", inc="mm")
                        p.do("vector", lambda v: v.tensor_copy(out=attnS[0:64, :], in_=psAV[0:64, :]),
                             inc="vec")
                        p.do("vector", lambda v, i=i, hp=hp, qt=qt: v.tensor_tensor(
                            attn[i * 64:(i + 1) * 64,
                                 hp * 2048 + qt * 512: hp * 2048 + qt * 512 + 512],
                            attnS[0:64, :], psN[0:64, :], OP.mult),
                            wait="mm", inc="vec")

        if stages < 2.0:
            p.do("sync", lambda sy: sy.nop(), wait=[("st", p.cnt["st"]), ("ld", p.cnt["ld"])])
            return p
        with ExitStack() as e2b:
            wos = p.sb("wos", [128, DC * 1024], BF16, e2b)
            psW = [p.ps(f"psW{i}", [128, 512], F32, e2b) for i in range(2)]
            p.do("gpsimd", lambda g: g.dma_start(
                out=wos[:].rearrange("p (dc c) -> p dc c", dc=DC),
                in_=wout.rearrange("(dc p) c -> p dc c", p=128)),
                wait=[("mm", p.cnt["mm"]), ("act", p.cnt["act"]), ("vec", p.cnt["vec"]), ("st", p.cnt["st"])], inc="lw", incn=16)
            lw_o = p.cnt["lw"]
            p.do("vector", lambda v: v.nop(),
                 wait=[("mm", p.cnt["mm"]), ("act", p.cnt["act"]), ("st", p.cnt["st"]),
                       ("ld", p.cnt["ld"]), ("lw", p.cnt["lw"])], inc="vec")
            if debug:
                for cc in range(DC):
                    p.do("sync", lambda sy, cc=cc: sy.dma_start(
                        out=attnpre[cc * 128:(cc + 1) * 128, :],
                        in_=attn[:, cc * TPC:(cc + 1) * TPC]),
                        wait=[("vec", p.cnt["vec"])], inc="st", incn=16)
                    p.do("sync", lambda sy, cc=cc: sy.dma_start(
                        out=qbo2[cc * 128:(cc + 1) * 128, :], in_=qbuf[:, cc * TPC:(cc + 1) * TPC]),
                        inc="st", incn=16)
            for tg in range(NG):
                for half in range(2):
                    bank = (tg * 2 + half) % 2
                    for hp in range(8):
                        p.do("tensor", lambda t, tg=tg, half=half, hp=hp, bank=bank: t.matmul(
                            psW[bank][:],
                            attn[:, hp * 2048 + tg * 128: hp * 2048 + tg * 128 + 128],
                            wos[:, hp * 1024 + half * 512: hp * 1024 + half * 512 + 512],
                            start=(hp == 0), stop=(hp == 7)),
                            wait=([("lw", lw_o), ("vec", p.cnt["vec"])] if hp == 0 else None),
                            inc=("mm" if hp == 7 else None))
                    if debug and half == 0:
                        p.do("vector", lambda v, bank=bank: v.tensor_copy(
                            out=p.scr[:, 0:512], in_=psW[bank][:]), wait="mm", inc="vec")
                        p.do("sync", lambda sy, tg=tg: sy.dma_start(
                            out=pswo[tg], in_=p.scr[:, 0:512]), wait="vec", inc="st", incn=16)
                    p.do("vector", lambda v, tg=tg, half=half, bank=bank: v.tensor_tensor(
                        resid[:, tg * 1024 + half * 512: tg * 1024 + half * 512 + 512],
                        resid[:, tg * 1024 + half * 512: tg * 1024 + half * 512 + 512],
                        psW[bank][:], OP.add),
                        wait=["mm", ("st", p.cnt["st"])], inc="vec")
            if debug:
                p.do("sync", lambda sy: sy.dma_start(
                    out=x1o.rearrange("(g p) d -> p g d", p=128),
                    in_=resid[:].rearrange("p (g d) -> p g d", g=NG)),
                    wait=[("vec", p.cnt["vec"])], inc="st", incn=16)
                for cc in range(DC):
                    p.do("sync", lambda sy, cc=cc: sy.dma_start(
                        out=attno[cc * 128:(cc + 1) * 128, :], in_=attn[:, cc * TPC:(cc + 1) * TPC]),
                        wait=[("vec", p.cnt["vec"])], inc="st", incn=16)

    if stages < 3:
        p.do("sync", lambda sy: sy.nop(), wait=[("st", p.cnt["st"])])
        return p
    with ExitStack() as e3:
        xfmF = p.sb("xfm3", [128, DC * 512], BF16, e3)
        abuf = p.sb("abuf", [128, 32 * 512], BF16, e3)
        bbuf = p.sb("bbuf", [128, 32 * 512], BF16, e3)
        t1 = p.sb("t1", [128, 512], F32, e3)
        t2 = p.sb("t2", [128, 512], F32, e3)
        w3tmp = p.sb("w3tmp", [128, 512], BF16, e3)
        tf32 = p.sb("tf32", [128, 128], F32, e3)
        w12s = [p.sb(f"w12s{i}", [128, DC * 256], BF16, e3) for i in range(2)]
        whs = [p.sb(f"whs{i}", [128, 32 * 128], BF16, e3) for i in range(2)]
        scl1 = p.sb("scl1", [128, 32], F32, e3)
        scl2 = p.sb("scl2", [128, 32], F32, e3)
        sclh1 = p.sb("sclh1", [128, 32], F32, e3)
        sclh2 = p.sb("sclh2", [128, 32], F32, e3)
        scl3 = p.sb("scl3", [128, 8], F32, e3)
        psT3v = [p.ps(f"psT3{i}", [128, 128], BF16, e3) for i in range(2)]
        psA = [p.ps(f"psA{i}", [128, 512], F32, e3) for i in range(4)]

        for src, dst in ((s1, scl1), (s2, scl2), (sh1, sclh1), (sh2, sclh2), (s3, scl3)):
            p.do("sync", lambda sy, src=src, dst=dst: sy.dma_start(
                out=dst[:], in_=src[:, :]),
                wait=[("mm", p.cnt["mm"]), ("act", p.cnt["act"]), ("vec", p.cnt["vec"]), ("st", p.cnt["st"])], inc="ld", incn=16)

        def h_stage(inbuf, outbuf, wsrc_b, sclbuf, func, nout=32, nin=32):
            mm_entry = p.cnt["mm"]
            mm_after = []
            for oc in range(nout):
                hbuf = whs[oc % 2]
                qeng, qsem = (("gpsimd", "lw") if oc % 2 == 0 else ("sync", "ld"))
                mmw = mm_after[oc - 2] if oc >= 2 else mm_entry
                p.do(qeng, lambda g, oc=oc, hbuf=hbuf, wsrc_b=wsrc_b, nin=nin: g.dma_start(
                    out=hbuf[:, 0:nin * 128], in_=wsrc_b[oc]),
                    wait=[("mm", mmw), ("lw", lw_cast)], inc=qsem, incn=16)
                ldw = (qsem, p.cnt[qsem])
                bank = oc % 2
                for ic in range(nin):
                    p.do("tensor", lambda t, oc=oc, ic=ic, bank=bank, hbuf=hbuf, inbuf=inbuf: t.matmul(
                        psA[bank][:], hbuf[:, ic * 128:(ic + 1) * 128],
                        inbuf[:, ic * 512:(ic + 1) * 512],
                        start=(ic == 0), stop=(ic == nin - 1)),
                        wait=([ldw, ("act", p.cnt["act"]),
                               ("vec", p.cnt["vec"])] if ic == 0 else None),
                        inc=("mm" if ic == nin - 1 else None))
                mm_after.append(p.cnt["mm"])
                yield oc, bank

        for st in range(4):
            emit_ln_tile(p, psT3v,
                         [resid[:, (st * 4 + g) * 1024:(st * 4 + g + 1) * 1024] for g in range(4)],
                         xfmF)
            mm_e12 = p.cnt["mm"]
            mm_a12 = []
            for oc in range(32):
                buf = w12s[oc % 2]
                qeng, qsem = (("gpsimd", "lw") if oc % 2 == 0 else ("sync", "ld"))
                mmw = mm_a12[oc - 2] if oc >= 2 else mm_e12
                p.do(qeng, lambda g, oc=oc, buf=buf: g.dma_start(
                    out=buf[:], in_=w12b[oc]),
                    wait=[("mm", mmw), ("lw", lw_cast)], inc=qsem, incn=16)
                ldw = (qsem, p.cnt[qsem])
                for wi in range(2):
                    bank = 2 * (oc % 2) + wi
                    for dc in range(DC):
                        p.do("tensor", lambda t, oc=oc, wi=wi, dc=dc, bank=bank, buf=buf: t.matmul(
                            psA[bank][:],
                            buf[:].rearrange("p (w dc c) -> p w dc c", w=2, dc=DC)[:, wi, dc],
                            xfmF[:, dc * 512:(dc + 1) * 512],
                            start=(dc == 0), stop=(dc == DC - 1)),
                            wait=([ldw, ("vec", p.cnt["vec"]),
                                   ("act", p.cnt["act"])] if (dc == 0 and wi == 0) else None),
                            inc=("mm" if dc == DC - 1 else None))
                mm_a12.append(p.cnt["mm"])
                p.do("scalar", lambda s, oc=oc: s.activation(
                    out=t1[:], in_=psA[2 * (oc % 2)][:], func=AF.Silu,
                    scale=scl1[:, oc:oc + 1]), wait=["mm", ("vec", p.cnt["vec"])], inc="act")
                p.do("scalar", lambda s, oc=oc: s.activation(
                    out=t2[:], in_=psA[2 * (oc % 2) + 1][:], func=AF.Identity,
                    scale=scl2[:, oc:oc + 1]), inc="act")
                p.do("vector", lambda v, oc=oc: v.tensor_tensor(
                    abuf[:, oc * 512:(oc + 1) * 512], t1[:], t2[:], OP.mult),
                    wait="act", inc="vec")
            for oc, bank in h_stage(abuf, bbuf, h1b, sclh1, AF.Gelu):
                p.do("scalar", lambda s, oc=oc, bank=bank: s.activation(
                    out=bbuf[:, oc * 512:(oc + 1) * 512], in_=psA[bank][:],
                    func=AF.Gelu, scale=sclh1[:, oc:oc + 1]), wait="mm", inc="act")
            for oc, bank in h_stage(bbuf, abuf, h2b, sclh2, AF.Gelu):
                p.do("scalar", lambda s, oc=oc, bank=bank: s.activation(
                    out=abuf[:, oc * 512:(oc + 1) * 512], in_=psA[bank][:],
                    func=AF.Gelu, scale=sclh2[:, oc:oc + 1]), wait="mm", inc="act")
            for oc, bank in h_stage(abuf, None, w3b, scl3, None, nout=8):
                p.do("scalar", lambda s, oc=oc, bank=bank: s.activation(
                    out=w3tmp[:], in_=psA[bank][:], func=AF.Identity,
                    scale=scl3[:, oc:oc + 1]), wait=["mm", ("vec", p.cnt["vec"])], inc="act")
                for tsub in range(4):
                    tb = 2 + tsub % 2
                    p.do("tensor", lambda t, tsub=tsub, tb=tb: t.transpose(
                        psT3v[tb - 2][:], w3tmp[:, tsub * 128:(tsub + 1) * 128], p.it128[:]),
                        wait=[("act", p.cnt["act"]), ("vec", p.cnt["vec"])], inc="mm")
                    p.do("vector", lambda v, tb=tb: v.tensor_copy(out=tf32[:], in_=psT3v[tb - 2][:]),
                         wait="mm", inc="vec")
                    gg = st * 4 + tsub
                    p.do("vector", lambda v, gg=gg, oc=oc: v.tensor_tensor(
                        resid[:, gg * 1024 + oc * 128: gg * 1024 + oc * 128 + 128],
                        resid[:, gg * 1024 + oc * 128: gg * 1024 + oc * 128 + 128],
                        tf32[:], OP.add), inc="vec")

    if stages < 4:
        p.do("sync", lambda sy: sy.nop(), wait=[("st", p.cnt["st"])])
        return p
    with ExitStack() as e4:
        qf4 = p.sb("qf4", [128, DC * TPC], BF16, e4)
        xfmL = p.sb("xfm4", [128, DC * 512], BF16, e4)
        wq1s = p.sb("wq1s", [128, DC * 1024], BF16, e4)
        wk1s = p.sb("wk1s", [128, DC * 1024], BF16, e4)
        wv1s = p.sb("wv1s", [128, DC * 1024], BF16, e4)
        bqs = p.sb("bqs", [1, D], BF16, e4)
        bks = p.sb("bks", [1, D], BF16, e4)
        bvs = p.sb("bvs", [1, D], BF16, e4)
        kf = p.sb("kf", [128, 4 * 1024], BF16, e4)
        vf = p.sb("vf", [128, 4 * 16 * 65], BF16, e4)
        ta = p.sb("ta", [128, 512], F32, e4)
        tc = p.sb("tc", [128, 512], F32, e4)
        tb_ = p.sb("tb", [128, 512], F32, e4)
        kvsb = p.sb("kvsb", [128, 16 * 65], F32, e4)
        psT4v = [p.ps(f"psT4{i}", [128, 128], BF16, e4) for i in range(2)]
        psOL = [p.ps(f"psO4{i}", [128, 512], F32, e4) for i in range(3)]
        psKV = p.ps("psKV", [128, 1536], F32, e4)

        for w, t_ in ((wq1, wq1s), (wk1, wk1s), (wv1, wv1s)):
            p.do("gpsimd", lambda g, w=w, t_=t_: g.dma_start(
                out=t_[:].rearrange("p (dc c) -> p dc c", dc=DC),
                in_=w.rearrange("(dc p) c -> p dc c", p=128)),
                wait=[("mm", p.cnt["mm"]), ("act", p.cnt["act"]), ("vec", p.cnt["vec"]), ("st", p.cnt["st"])], inc="lw", incn=16)
        for bsrc, bdst in ((bq1, bqs), (bk1, bks), (bv1, bvs)):
            p.do("gpsimd", lambda g, bsrc=bsrc, bdst=bdst: g.dma_start(
                out=bdst[:], in_=bsrc[:, :]), inc="lw", incn=16)
        lw4 = p.cnt["lw"]

        kv_started = set()
        for st in range(4):
            emit_ln_tile(p, psT4v,
                         [resid[:, (st * 4 + g) * 1024:(st * 4 + g + 1) * 1024] for g in range(4)],
                         xfmL)
            p.do("vector", lambda v: v.memset(
                vf[:].rearrange("p (ts h m) -> p ts h m", ts=4, m=65)[:, :, :, 64:65], 1.0),
                wait=[("mm", p.cnt["mm"])], inc="vec")
            for cc in range(DC):
                bank = 0
                for dc in range(DC):
                    p.do("tensor", lambda t, cc=cc, dc=dc, bank=bank: t.matmul(
                        psOL[bank][:], wq1s[:, dc * 1024 + cc * 128: dc * 1024 + cc * 128 + 128],
                        xfmL[:, dc * 512:(dc + 1) * 512],
                        start=(dc == 0), stop=False),
                        wait=([("lw", lw4), ("vec", p.cnt["vec"]),
                               ("act", p.cnt["act"])] if dc == 0 else None))
                p.do("tensor", lambda t, cc=cc, bank=bank: t.matmul(
                    psOL[bank][:], bqs[0:1, cc * 128:(cc + 1) * 128], onesbf[0:1, :],
                    start=False, stop=True), inc="mm")
                p.do("vector", lambda v, bank=bank: v.tensor_scalar_min(ta[:], psOL[bank][:], 0.0),
                     wait=["mm", ("act", p.cnt["act"])], inc="vec")
                p.do("scalar", lambda s: s.activation(out=tc[:], in_=ta[:], func=AF.Exp),
                     wait="vec", inc="act")
                p.do("vector", lambda v, bank=bank: v.tensor_scalar_max(tb_[:], psOL[bank][:], 0.0),
                     inc="vec")
                p.do("vector", lambda v, cc=cc, st=st: v.tensor_tensor(
                    qf4[:, cc * TPC + st * 512: cc * TPC + st * 512 + 512], tc[:], tb_[:], OP.add),
                    wait=["act", ("vec", p.cnt["vec"])], inc="vec")
            for ts in range(4):
                for half in range(2):
                    for bank, wbuf, brow in ((1, wk1s, bks), (2, wv1s, bvs)):
                        for dc in range(DC):
                            p.do("tensor", lambda t, ts=ts, half=half, dc=dc, bank=bank, wbuf=wbuf:
                                 t.matmul(
                                     psOL[bank][:],
                                     xfmL[:, dc * 512 + ts * 128: dc * 512 + ts * 128 + 128],
                                     wbuf[:, dc * 1024 + half * 512: dc * 1024 + half * 512 + 512],
                                     start=(dc == 0), stop=False),
                                 wait=([("vec", p.cnt["vec"]),
                                        ("act", p.cnt["act"])] if dc == 0 else None))
                        p.do("tensor", lambda t, ts=ts, half=half, bank=bank, brow=brow: t.matmul(
                            psOL[bank][:], onesbf[0:1, ts * 128: ts * 128 + 128],
                            brow[0:1, half * 512:(half + 1) * 512],
                            start=False, stop=True), inc="mm")
                    p.do("vector", lambda v: v.tensor_scalar_min(ta[:], psOL[1][:], 0.0),
                         wait=["mm", ("act", p.cnt["act"])], inc="vec")
                    p.do("scalar", lambda s: s.activation(out=tc[:], in_=ta[:], func=AF.Exp),
                         wait="vec", inc="act")
                    p.do("vector", lambda v: v.tensor_scalar_max(tb_[:], psOL[1][:], 0.0), inc="vec")
                    p.do("vector", lambda v, ts=ts, half=half: v.tensor_tensor(
                        kf[:, ts * 1024 + half * 512: ts * 1024 + half * 512 + 512],
                        tc[:], tb_[:], OP.add), wait=["act", ("vec", p.cnt["vec"])], inc="vec")
                    p.do("vector", lambda v, ts=ts, half=half: v.tensor_copy(
                        out=vf[:].rearrange("p (ts h m) -> p ts h m", ts=4, m=65)
                        [:, ts, half * 8:(half + 1) * 8, 0:64],
                        in_=psOL[2][:].rearrange("p (h m) -> p h m", m=64)), inc="vec")
            for h in range(H):
                for ts in range(4):
                    bank_first = (st == 0 and ts == 0 and h in (0, 7, 14))
                    last = (st == 3 and ts == 3)
                    kc = (h // 7) * 512 + (h % 7) * 65
                    p.do("tensor", lambda t, h=h, ts=ts, bf=bank_first, last=last, kc=kc:
                         t.matmul(psKV[0:64, kc:kc + 65],
                                  kf[:, ts * 1024 + h * 64: ts * 1024 + h * 64 + 64],
                                  vf[:, ts * 1040 + h * 65: ts * 1040 + h * 65 + 65],
                                  start=bf, stop=last,
                                  skip_group_check=True),
                         wait=([("vec", p.cnt["vec"])] if (ts == 0 and h == 0) else None),
                         inc=("mm" if (h == H - 1 and ts == 3) else None))
        for h in range(H):
            kc = (h // 7) * 512 + (h % 7) * 65
            p.do("vector", lambda v, h=h, kc=kc: v.tensor_copy(
                out=kvsb[0:64, h * 65: h * 65 + 65],
                in_=psKV[0:64, kc:kc + 65]),
                wait="mm", inc="vec")
            p.do("sync", lambda sy, h=h: sy.dma_start(
                out=kvo[h], in_=kvsb[0:64, h * 65: h * 65 + 65]),
                wait="vec", inc="st", incn=16)
        for cc in range(DC):
            p.do("sync", lambda sy, cc=cc: sy.dma_start(
                out=qfo[cc * 128:(cc + 1) * 128, :], in_=qf4[:, cc * TPC:(cc + 1) * TPC]),
                wait=[("vec", p.cnt["vec"])], inc="st", incn=16)
        p.do("sync", lambda sy: sy.dma_start(
            out=x2o.rearrange("(g p) d -> p g d", p=128),
            in_=resid[:].rearrange("p (g d) -> p g d", g=NG)),
            wait=[("vec", p.cnt["vec"])], inc="st", incn=16)
        if debug:
            p.do("sync", lambda sy: sy.dma_start(out=ta[:].bitcast(BF16)[:, 0:1024],
                                                 in_=kfull[0:128, 0:1024]),
                 wait=[("st", p.cnt["st"])], inc="ld", incn=16)
            p.do("sync", lambda sy: sy.dma_start(out=kchk[:, :], in_=ta[:].bitcast(BF16)[:, 0:1024]),
                 wait=[("ld", p.cnt["ld"])], inc="st", incn=16)
        p.do("sync", lambda sy: sy.nop(), wait=[("st", p.cnt["st"])])

    return p



def build_launch_B():
    p = Prog("B")
    x2i = p.din("x2i", [TPC, D])
    qfi = p.din("qfi", [D, TPC], BF16)
    kvp = p.din("kvp", [4, H, HD, 65])
    wo1 = p.din("wo1", [D, D])
    bo1 = p.din("bo1", [1, D])
    rt = p.din("rt", [D, 64])
    ident = p.din("ident", [128, 128])
    x3o = p.dout("x3o", [TPC, D])
    xl4o = p.dout("xl4o", [TPC, D], BF16)
    idxo = p.dout("idxo", [TPC, 8])
    wno = p.dout("wno", [TPC, 8])
    toto = p.dout("toto", [1, 64])

    resid = p.sb("residB", [128, NG * 1024], F32)
    p.it128 = p.sb("it128B", [128, 128], BF16)
    onesbfB = p.sb("onesbfB", [128, 512], BF16)
    p.scr = p.sb("scrB", [128, 1024], F32)
    p.mu = p.sb("muB", [128, 1]); p.sq = p.sb("sqB", [128, 1])
    p.var = p.sb("varB", [128, 1]); p.rstd = p.sb("rstdB", [128, 1])
    p.musum = p.sb("musumB", [128, 1]); p.musq = p.sb("musqB", [128, 1])
    p.sd = p.sb("sdB", [128, 1])
    p.xlg = p.sb("xlgB", [128, 1024], BF16)
    p.epsb = p.sb("epsbB", [128, 1], F32)
    hif = p.sb("hifB", [128, 1024], F32)
    xlf = p.sb("xlfB", [128, 1024], F32)
    iot64 = p.sb("iot64B", [128, 64], F32)
    iomb = p.sb("iombB", [128, 64], F32)
    probs = p.sb("probsB", [128, 64], F32)
    pn = p.sb("pnB", [128, 64], F32)
    nmax = p.sb("nmaxB", [128, 1], F32)
    rsum = p.sb("rsumB", [128, 1], F32)
    ssum = p.sb("ssumB", [128, 1], F32)
    wv8 = p.sb("wv8B", [128, 8], F32)
    iv8 = p.sb("iv8B", [128, 8], F32)
    t64 = p.sb("t64B", [128, 64], F32)
    obf = p.sb("obfB", [128, 64], BF16)
    onesf1 = p.sb("onesf1B", [128, 1], BF16)
    eB1 = ExitStack()
    qfB = p.sb("qfB", [128, DC * TPC], BF16, eB1)
    kvl = [p.sb(f"kvl{i}", [128, 8 * 65], F32, eB1) for i in range(2)]
    kvs = p.sb("kvs", [128, 8 * 65], F32, eB1)
    kvsbf = p.sb("kvsbf", [128, 8 * 65], BF16, eB1)
    ytok = p.sb("ytok", [128, 1024], BF16, eB1)
    zcol = p.sb("zcolB", [128, 1], F32, eB1)
    yfm = p.sb("yfmB", [128, DC * TPC], BF16, eB1)
    wo1s = p.sb("wo1s", [128, DC * 1024], BF16, eB1)
    bo1s = p.sb("bo1s", [1, D], BF16, eB1)
    psT = [p.ps(f"psTB{i}", [128, 128], BF16) for i in range(2)]
    psY = [p.ps(f"psYB{i}", [128, 512], F32) for i in range(2)]
    psW = [p.ps(f"psWB{i}", [128, 512], F32) for i in range(2)]
    psL = p.ps("psLB", [128, 64], F32)
    psTT = p.ps("psTTB", [128, 64], F32)

    p.do("gpsimd", lambda g: g.dma_start(out=p.it128[:], in_=ident[:, :]), inc="ld", incn=16)
    p.do("gpsimd", lambda g: g.dma_start(
        out=wo1s[:].rearrange("p (dc c) -> p dc c", dc=DC),
        in_=wo1.rearrange("(dc p) c -> p dc c", p=128)), inc="lw", incn=16)
    p.do("gpsimd", lambda g: g.dma_start(out=bo1s[:], in_=bo1[:, :]), inc="lw", incn=16)
    p.do("gpsimd", lambda g: g.dma_start(out=qfB[:].rearrange("p (dc t) -> p dc t", dc=DC),
                                         in_=qfi.rearrange("(dc p) t -> p dc t", p=128)),
         inc="lw", incn=16)
    for g2 in range(NG):
        p.do("sync", lambda sy, g2=g2: sy.dma_start(
            out=resid[:, g2 * 1024:(g2 + 1) * 1024], in_=x2i[g2 * 128:(g2 + 1) * 128, :]),
            inc="ld", incn=16)
    p.do("vector", lambda v: v.memset(onesbfB[:], 1.0), inc="vec")
    p.do("vector", lambda v: v.memset(onesf1[:], 1.0), inc="vec")
    p.do("vector", lambda v: v.memset(p.epsb[:], EPS_LN), inc="vec")
    p.do("gpsimd", lambda g: g.iota(iot64[:], pattern=[[1, 64]], base=0, channel_multiplier=0,
                                    allow_small_or_imprecise_dtypes=True), inc="gp")
    p.do("vector", lambda v: v.tensor_scalar_add(iomb[:], iot64[:], -BIG),
         wait=[("gp", p.cnt["gp"])], inc="vec")
    for gi in range(4):
        buf = kvl[gi % 2]
        p.do("sync", lambda sy, gi=gi, buf=buf: sy.dma_start(
            out=buf[:, :].rearrange("p (hp c) -> p hp c", hp=8),
            in_=kvp[gi].rearrange("(hp i) d c -> (i d) hp c", i=2)),
            wait=[("vec", p.cnt["vec"])], inc="ld", incn=16)
        if gi == 0:
            p.dov(lambda v, buf=buf: v.tensor_copy(out=kvs[:], in_=buf[:]), wait="ld")
        else:
            p.dov(lambda v, buf=buf: v.tensor_tensor(kvs[:], kvs[:], buf[:], OP.add), wait="ld")
    p.do("vector", lambda v: v.tensor_copy(out=kvsbf[:], in_=kvs[:]), wait="vec", inc="vec")
    for tg in range(NG):
        for hp in range(DC):
            for i in range(2):
                hb = i * 64
                p.do("tensor", lambda t, tg=tg, hp=hp, i=i, hb=hb: t.matmul(
                    psY[i][:, 0:65],
                    qfB[hb:hb + 64, hp * TPC + tg * 128: hp * TPC + tg * 128 + 128],
                    kvsbf[hb:hb + 64, hp * 65: hp * 65 + 65],
                    start=True, stop=True),
                    wait=([("lw", p.cnt["lw"]), ("vec", p.cnt["vec"])] if i == 0 else None),
                    inc=("mm" if i == 1 else None))
            for i in range(2):
                p.do("vector", lambda v, i=i: v.reciprocal(zcol[:], psY[i][:, 64:65]),
                     wait=(["mm"] if i == 0 else [("vec", p.cnt["vec"])]), inc="vec")
                p.do("vector", lambda v, i=i, hp=hp: v.tensor_scalar(
                    ytok[:, (2 * hp + i) * 64:(2 * hp + i) * 64 + 64],
                    psY[i][:, 0:64], zcol[:, 0:1], None, OP.mult),
                    wait="vec", inc="vec")
        emit_transpose(p, psT, [
            (ytok[:, hp * 128:(hp + 1) * 128],
             yfm[:, hp * TPC + tg * 128: hp * TPC + tg * 128 + 128]) for hp in range(DC)])
    for tg in range(NG):
        for half in range(2):
            bank = (tg * 2 + half) % 2
            for hp in range(DC):
                p.do("tensor", lambda t, tg=tg, half=half, hp=hp, bank=bank: t.matmul(
                    psW[bank][:], yfm[:, hp * TPC + tg * 128: hp * TPC + tg * 128 + 128],
                    wo1s[:, hp * 1024 + half * 512: hp * 1024 + half * 512 + 512],
                    start=(hp == 0), stop=False),
                    wait=([("vec", p.cnt["vec"])] if hp == 0 else None))
            p.do("tensor", lambda t, tg=tg, half=half, bank=bank: t.matmul(
                psW[bank][:], onesbfB[0:1, 0:128],
                bo1s[0:1, half * 512:(half + 1) * 512],
                start=False, stop=True), inc="mm")
            p.do("vector", lambda v, tg=tg, half=half, bank=bank: v.tensor_tensor(
                resid[:, tg * 1024 + half * 512: tg * 1024 + half * 512 + 512],
                resid[:, tg * 1024 + half * 512: tg * 1024 + half * 512 + 512],
                psW[bank][:], OP.add), wait="mm", inc="vec")
    eB1.close()
    xhi = p.sb("xhiB", [128, DC * TPC], BF16)
    xlo = p.sb("xloB", [128, DC * TPC], BF16)
    rts = p.sb("rtsB", [128, 8 * 64], F32)
    rthi = p.sb("rthiB", [128, 8 * 64], BF16)
    rtlo = p.sb("rtloB", [128, 8 * 64], BF16)
    p.do("sync", lambda sy: sy.dma_start(out=rts[:].rearrange("p (dc c) -> p dc c", dc=DC),
                                         in_=rt.rearrange("(dc p) c -> p dc c", p=128)),
         wait=[("mm", p.cnt["mm"]), ("vec", p.cnt["vec"]), ("act", p.cnt["act"]),
               ("st", p.cnt["st"])], inc="ld", incn=16)
    p.do("vector", lambda v: v.tensor_copy(out=rthi[:], in_=rts[:]), wait="ld", inc="vec")
    p.do("vector", lambda v: v.tensor_tensor(rtlo[:], rts[:], rthi[:], OP.subtract),
         wait="vec", inc="vec")
    for g2 in range(NG):
        emit_ln(p, resid[:, g2 * 1024:(g2 + 1) * 1024], p.xlg[:],
                w0=[("vec", p.cnt["vec"]), ("mm", p.cnt["mm"]), ("st", p.cnt["st"])],
                out_f32=xlf[:])
        p.do("sync", lambda sy, g2=g2: sy.dma_start(
            out=xl4o[g2 * 128:(g2 + 1) * 128, :], in_=p.xlg[:]), wait="vec", inc="st", incn=16)
        p.dov(lambda v: v.tensor_copy(out=hif[:], in_=p.xlg[:]))
        p.do("vector", lambda v: v.tensor_tensor(p.scr[:], xlf[:], hif[:], OP.subtract),
             wait="vec", inc="vec")
        emit_transpose(p, psT, [
            (p.xlg[:, dc * 128:(dc + 1) * 128],
             xhi[:, dc * TPC + g2 * 128: dc * TPC + g2 * 128 + 128]) for dc in range(DC)])
        p.do("vector", lambda v: v.tensor_copy(out=p.xlg[:], in_=p.scr[:]),
             wait=[("mm", p.cnt["mm"]), ("st", p.cnt["st"])], inc="vec")
        emit_transpose(p, psT, [
            (p.xlg[:, dc * 128:(dc + 1) * 128],
             xlo[:, dc * TPC + g2 * 128: dc * TPC + g2 * 128 + 128]) for dc in range(DC)])
    for tg in range(NG):
        combos = [(xhi, rthi), (xlo, rthi), (xhi, rtlo), (xlo, rtlo)]
        nmm = 0
        for xb_, rb_ in combos:
            for dc in range(DC):
                nmm += 1
                p.do("tensor", lambda t, tg=tg, dc=dc, xb_=xb_, rb_=rb_, nmm=nmm: t.matmul(
                    psL[:], xb_[:, dc * TPC + tg * 128: dc * TPC + tg * 128 + 128],
                    rb_[:, dc * 64:(dc + 1) * 64],
                    start=(nmm == 1), stop=(nmm == 32)),
                    wait=([("act", p.cnt["act"]), ("vec", p.cnt["vec"])] if nmm == 1 else None),
                    inc=("mm" if nmm == 32 else None))
        p.do("vector", lambda v: v.reduce_max(nmax[:], psL[:], axis=AX.X, negate=True),
             wait="mm", inc="vec")
        p.do("scalar", lambda s: s.activation(out=probs[:], in_=psL[:], func=AF.Exp,
                                              bias=nmax[:, 0:1], accum_out=ssum[:]),
             wait="vec", inc="act")
        p.do("vector", lambda v: v.reciprocal(rsum[:], ssum[:]), wait="act", inc="vec")
        p.do("vector", lambda v: v.tensor_scalar(pn[:], probs[:], rsum[:, 0:1], None, OP.mult),
             wait="vec", inc="vec")
        for k in range(8):
            p.do("vector", lambda v, k=k: v.reduce_max(wv8[:, k:k + 1], pn[:], axis=AX.X),
                 wait=(["vec", ("st", p.cnt["st"])] if k == 0 else ["vec"]), inc="vec")
            p.dov(lambda v, k=k: v.tensor_scalar(t64[:], pn[:], wv8[:, k:k + 1],
                                                 None, OP.is_equal))
            p.dov(lambda v: v.tensor_mul(t64[:], t64[:], iomb[:]))
            p.dov(lambda v: v.tensor_scalar_add(t64[:], t64[:], BIG))
            p.dov(lambda v, k=k: v.tensor_reduce(iv8[:, k:k + 1], t64[:], axis=AX.X,
                                                 op=OP.min))
            p.dov(lambda v, k=k: v.tensor_scalar(t64[:], iot64[:], iv8[:, k:k + 1],
                                                 None, OP.is_equal))
            p.dov(lambda v: v.scalar_tensor_tensor(pn[:], t64[:], -BIG, pn[:],
                                                   OP.mult, OP.add))
            p.dov(lambda v: v.tensor_copy(out=obf[:], in_=t64[:]))
            p.do("tensor", lambda t, tg=tg, k=k: t.matmul(
                psTT[0:1, :], onesf1[:, 0:1], obf[:],
                start=(tg == 0 and k == 0), stop=(tg == NG - 1 and k == 7),
                skip_group_check=True),
                wait=[("vec", p.cnt["vec"])], inc="mm")
        p.do("vector", lambda v: v.reduce_sum(p.musum[:], wv8[:], axis=AX.X),
             wait=[("vec", p.cnt["vec"])], inc="vec")
        p.do("vector", lambda v: v.reciprocal(p.sd[:], p.musum[:]), wait="vec", inc="vec")
        p.do("vector", lambda v: v.tensor_scalar(t64[:, 0:8], wv8[:], p.sd[:, 0:1], None,
                                                 OP.mult), wait="vec", inc="vec")
        p.do("sync", lambda sy, tg=tg: sy.dma_start(
            out=idxo[tg * 128:(tg + 1) * 128, :], in_=iv8[:]), wait="vec", inc="st", incn=16)
        p.do("sync", lambda sy, tg=tg: sy.dma_start(
            out=wno[tg * 128:(tg + 1) * 128, :], in_=t64[:, 0:8]), wait="vec", inc="st", incn=16)
    p.do("vector", lambda v: v.tensor_copy(out=p.scr[0:1, 0:64], in_=psTT[0:1, :]),
         wait="mm", inc="vec")
    p.do("sync", lambda sy: sy.dma_start(out=toto[:, :], in_=p.scr[0:1, 0:64]),
         wait="vec", inc="st", incn=16)
    p.do("sync", lambda sy: sy.dma_start(
        out=x3o.rearrange("(g p) d -> p g d", p=128),
        in_=resid[:].rearrange("p (g d) -> p g d", g=NG)),
        wait=[("vec", p.cnt["vec"])], inc="st", incn=16)
    p.do("sync", lambda sy: sy.nop(), wait=[("st", p.cnt["st"])])
    return p



def build_launch_B2(debug=False, scat=True):
    p = Prog("B2")
    idxi = p.din("idxi", [TPC, 8])
    wni = p.din("wni", [TPC, 8])
    tota = p.din("tota", [8, 64])
    premask = p.din("premask", [8, 1])
    gtok = p.din("gtok", [TPC, 1])
    uts = p.din("uts", [128, 128])
    plisto = p.dout("plisto", [E_EXP * CAP + 128, 1])
    goffo = p.dout("goffo", [TPC, 8])
    wvo = p.dout("wvo", [TPC, 8])
    zpl = p.sb("zpl2", [128, 257], F32)
    posmo = p.dout("posmo", [TPC, 8]) if debug else None
    mfo = p.dout("mfo", [TPC, 64]) if debug else None
    crowo = p.dout("crowo", [NG, 64]) if debug else None
    basebo = p.dout("basebo", [128, 64]) if debug else None
    sallo = p.dout("sallo", [TPC, 64]) if debug else None
    idxto = p.dout("idxto", [TPC, 8]) if debug else None

    totas = p.sb("totas2", [8, 64], F32)
    pres = p.sb("pres2", [8, 1], F32)
    utsb = p.sb("utsb2", [128, 128], BF16)
    onesb2 = p.sb("onesb2", [128, 1], BF16)
    onesf2 = p.sb("onesf2", [128, 128], F32)
    baseb = p.sb("baseb2", [128, 64], F32)
    brow = p.sb("brow2", [1, 64], F32)
    crow = p.sb("crow2", [1, 64], F32)
    iot64 = p.sb("iot642", [128, 64], F32)
    idxt = p.sb("idxt2", [128, 8], F32)
    wnt = p.sb("wnt2", [128, 8], F32)
    gtt = p.sb("gtt2", [128, 1], F32)
    Ok = [p.sb(f"Ok2_{k}", [128, 64], F32) for k in range(8)]
    Sall = p.sb("Sall2", [128, 64], F32)
    Sbf = p.sb("Sbf2", [128, 64], BF16)
    M1 = p.sb("M12", [128, 64], F32)
    Mfull = p.sb("Mfull2", [128, 64], F32)
    posm = p.sb("posm2", [128, 8], F32)
    valid = p.sb("valid2", [128, 8], F32)
    goff = p.sb("goff2", [128, 8], F32)
    gint = p.sb("gint2", [128, 8], I32)
    t8 = p.sb("t82", [128, 8], F32)
    psP = p.ps("psP2", [128, 64], F32)
    psC = p.ps("psC2", [128, 64], F32)
    psB = p.ps("psB2", [128, 64], F32)
    psBC = p.ps("psBC2", [128, 64], F32)

    p.dov(lambda v: v.memset(zpl[:], 0.0))
    NZ = (E_EXP * CAP + 128) // 128
    for j in range(NZ):
        p.do("sync", lambda sy, j=j: sy.dma_start(out=plisto[j * 128:(j + 1) * 128, :],
                                                  in_=zpl[:, j:j + 1]),
             wait=(["vec"] if j == 0 else None), inc="st", incn=16)
    p.do("sync", lambda sy: sy.dma_start(out=totas[:], in_=tota[:, :]), inc="ld", incn=16)
    p.do("sync", lambda sy: sy.dma_start(out=pres[:], in_=premask[:, :]), inc="ld", incn=16)
    p.do("gpsimd", lambda g: g.dma_start(out=utsb[:], in_=uts[:, :]), inc="lw", incn=16)
    p.dov(lambda v: v.memset(onesb2[:], 1.0))
    p.dov(lambda v: v.memset(onesf2[:], 1.0))
    p.do("gpsimd", lambda g: g.iota(iot64[:], pattern=[[1, 64]], base=0, channel_multiplier=0,
                                    allow_small_or_imprecise_dtypes=True), inc="gp")
    p.do("tensor", lambda t: t.matmul(psB[0:1, :], pres[0:8, 0:1], totas[0:8, :],
                                      start=True, stop=True), wait=["ld"], inc="mm")
    p.dov(lambda v: v.tensor_copy(out=brow[:], in_=psB[0:1, :]), wait="mm")
    p.do("tensor", lambda t: t.matmul(psBC[:], onesf2[0:1, 0:128], brow[0:1, :],
                                      start=True, stop=True), wait="vec", inc="mm")
    p.dov(lambda v: v.tensor_copy(out=baseb[:], in_=psBC[:]), wait="mm")
    if debug:
        p.do("sync", lambda sy: sy.dma_start(out=basebo[:, :], in_=baseb[:]),
             wait="vec", inc="st", incn=16)

    for tg in range(NG):
        p.do("sync", lambda sy, tg=tg: sy.dma_start(out=idxt[:], in_=idxi[tg * 128:(tg + 1) * 128, :]),
             wait=[("vec", p.cnt["vec"]), ("st", p.cnt["st"]), ("gp", p.cnt["gp"])], inc="ld", incn=16)
        p.do("sync", lambda sy, tg=tg: sy.dma_start(out=wnt[:], in_=wni[tg * 128:(tg + 1) * 128, :]),
             inc="ld", incn=16)
        p.do("sync", lambda sy, tg=tg: sy.dma_start(out=gtt[:], in_=gtok[tg * 128:(tg + 1) * 128, :]),
             inc="ld", incn=16)
        for k in range(8):
            p.dov(lambda v, k=k: v.tensor_scalar(Ok[k][:], iot64[:], idxt[:, k:k + 1],
                                                          None, OP.is_equal),
                 wait=[("ld", p.cnt["ld"])])
        p.dov(lambda v: v.tensor_copy(out=Sall[:], in_=Ok[0][:]))
        for k in range(1, 8):
            p.dov(lambda v, k=k: v.tensor_add(Sall[:], Sall[:], Ok[k][:]))
        p.dov(lambda v: v.tensor_copy(out=Sbf[:], in_=Sall[:]), wait="vec")
        if debug:
            p.do("sync", lambda sy, tg=tg: sy.dma_start(out=sallo[tg * 128:(tg + 1) * 128, :],
                                                        in_=Sall[:]), wait="vec", inc="st", incn=16)
            p.do("sync", lambda sy, tg=tg: sy.dma_start(out=idxto[tg * 128:(tg + 1) * 128, :],
                                                        in_=idxt[:]), inc="st", incn=16)
        if tg == 0:
            p.dov(lambda v: v.memset(crow[:], 0.0))
        else:
            p.dov(lambda v: v.tensor_copy(out=crow[:], in_=psC[0:1, :]),
                 wait=[("mm", p.cnt["mm"])])
        p.do("tensor", lambda t: t.matmul(psP[:], utsb[:], Sbf[:], start=True, stop=True),
             wait=[("vec", p.cnt["vec"]), ("lw", p.cnt["lw"])], inc="mm")
        p.do("tensor", lambda t, tg=tg: t.matmul(psC[0:1, :], onesb2[:, 0:1], Sbf[:],
                                                 start=(tg == 0), stop=(tg == NG - 1),
                                                 skip_group_check=True),
             wait=[("vec", p.cnt["vec"])], inc="mm")
        p.do("tensor", lambda t: t.matmul(psBC[:], onesf2[0:1, 0:128], crow[0:1, :],
                                          start=True, stop=True),
             wait=[("vec", p.cnt["vec"])], inc="mm")
        p.dov(lambda v: v.tensor_copy(out=M1[:], in_=psP[:]), wait="mm")
        p.dov(lambda v: v.tensor_tensor(M1[:], M1[:], psBC[:], OP.add))
        p.dov(lambda v: v.tensor_tensor(Mfull[:], M1[:], baseb[:], OP.add))
        if debug:
            p.do("sync", lambda sy, tg=tg: sy.dma_start(out=crowo[tg:tg + 1, :], in_=crow[:]),
                 wait="vec", inc="st", incn=16)
            p.do("sync", lambda sy, tg=tg: sy.dma_start(out=mfo[tg * 128:(tg + 1) * 128, :],
                                                        in_=Mfull[:]), inc="st", incn=16)
        for k in range(8):
            p.dov(lambda v, k=k: v.tensor_mul(Ok[k][:], Ok[k][:], Mfull[:]))
            p.dov(lambda v, k=k: v.reduce_sum(posm[:, k:k + 1], Ok[k][:], axis=AX.X))
        if debug:
            p.do("sync", lambda sy, tg=tg: sy.dma_start(out=posmo[tg * 128:(tg + 1) * 128, :],
                                                        in_=posm[:]), wait="vec", inc="st", incn=16)
        p.dov(lambda v: v.tensor_scalar(valid[:], posm[:], float(CAP), None, OP.is_lt),
             wait="vec")
        p.dov(lambda v: v.tensor_tensor(t8[:], wnt[:], valid[:], OP.mult))
        p.do("sync", lambda sy, tg=tg: sy.dma_start(out=wvo[tg * 128:(tg + 1) * 128, :], in_=t8[:]),
             wait="vec", inc="st", incn=16)
        p.dov(lambda v: v.tensor_scalar(goff[:], idxt[:], float(CAP), None, OP.mult))
        p.dov(lambda v: v.tensor_add(goff[:], goff[:], posm[:]))
        p.dov(lambda v: v.tensor_mul(t8[:], goff[:], valid[:]),
             wait=[("st", p.cnt["st"])])
        p.do("sync", lambda sy, tg=tg: sy.dma_start(out=goffo[tg * 128:(tg + 1) * 128, :],
                                                    in_=t8[:]), wait="vec", inc="st", incn=16)
        TRASH = float(E_EXP * CAP)
        p.dov(lambda v: v.tensor_scalar_add(goff[:], goff[:], -TRASH))
        p.dov(lambda v: v.tensor_mul(goff[:], goff[:], valid[:]))
        p.dov(lambda v: v.tensor_scalar_add(goff[:], goff[:], TRASH))
        p.dov(lambda v: v.tensor_scalar(goff[:], goff[:], TRASH, None, OP.min))
        p.dov(lambda v: v.tensor_scalar(goff[:], goff[:], 0.0, None, OP.max))
        p.dov(lambda v: v.tensor_copy(out=gint[:], in_=goff[:]))
        for k in range(8):
            if not scat:
                break
            p.do("gpsimd", lambda g, k=k: g.indirect_dma_start(
                plisto[0:128, :], IndirectOffsetOnAxis(gint[:, k:k + 1], 0),
                gtt[:], None),
                wait=[("vec", p.cnt["vec"]), ("st", NZ * 16)], inc="st", incn=16)
    p.do("sync", lambda sy: sy.nop(), wait=[("st", p.cnt["st"])])
    return p


def build_launch_C():
    p = Prog("C")
    plists = p.din("plists", [8, 128, 32])
    xall = p.din("xall", [B * S, D], BF16)
    wgs_in = p.din("wgi", [EPPC, D, FEXP])
    wus_in = p.din("wui", [EPPC, D, FEXP])
    wds_in = p.din("wdi", [EPPC, FEXP, D])
    ident = p.din("identC", [128, 128])
    yo = p.dout("yo", [EPPC * CAP, D], BF16)

    p.it128 = p.sb("it128C", [128, 128], BF16)
    pls = [p.sb(f"pls{i}", [128, 32], F32) for i in range(2)]
    fl = p.sb("flC", [128, 32], F32)
    t32 = p.sb("t32C", [128, 32], F32)
    gint = p.sb("gintC", [128, 32], I32)
    xbg = [p.sb(f"xbg{i}", [128, 1024], BF16) for i in range(4)]
    xbfm = p.sb("xbfmC", [128, DC * 512], BF16)
    wgs = p.sb("wgsC", [128, DC * FEXP], BF16)
    wus = p.sb("wusC", [128, DC * FEXP], BF16)
    wds = p.sb("wdsC", [128, 6 * D], BF16)
    t1 = p.sb("t1C", [128, 512], F32)
    t2 = p.sb("t2C", [128, 512], F32)
    hfm = p.sb("hfmC", [128, 6 * 512], BF16)
    yev = p.sb("yevC", [128, 512], BF16)
    psT = [p.ps(f"psTC{i}", [128, 128], BF16) for i in range(2)]
    psG = [p.ps(f"psGC{i}", [128, 512], F32) for i in range(4)]
    psD = [p.ps(f"psDC{i}", [128, 512], F32) for i in range(2)]

    p.do("gpsimd", lambda g: g.dma_start(out=p.it128[:], in_=ident[:, :]), inc="ld", incn=16)
    for s_ in range(8):
        buf = pls[s_ % 2]
        p.do("sync", lambda sy, s_=s_, buf=buf: sy.dma_start(out=buf[:], in_=plists[s_]),
             wait=[("vec", p.cnt["vec"])], inc="ld", incn=16)
        if s_ == 0:
            p.dov(lambda v, buf=buf: v.tensor_copy(out=fl[:], in_=buf[:]), wait="ld")
        else:
            p.dov(lambda v, buf=buf: v.tensor_add(fl[:], fl[:], buf[:]), wait="ld")
    p.dov(lambda v: v.tensor_scalar_add(t32[:], fl[:], -1.0))
    p.dov(lambda v: v.tensor_scalar_max(t32[:], t32[:], 0.0))
    p.dov(lambda v: v.tensor_copy(out=gint[:], in_=t32[:]))

    for e in range(EPPC):
        p.do("gpsimd", lambda g, e=e: g.dma_start(
            out=wgs[:].rearrange("p (dc f) -> p dc f", dc=DC),
            in_=wgs_in[e].rearrange("(dc p) f -> p dc f", p=128)),
            wait=[("mm", p.cnt["mm"])], inc="lw", incn=16)
        p.do("gpsimd", lambda g, e=e: g.dma_start(
            out=wus[:].rearrange("p (dc f) -> p dc f", dc=DC),
            in_=wus_in[e].rearrange("(dc p) f -> p dc f", p=128)), inc="lw", incn=16)
        p.do("gpsimd", lambda g, e=e: g.dma_start(
            out=wds[:].rearrange("p (fc c) -> p fc c", fc=6),
            in_=wds_in[e].rearrange("(fc p) c -> p fc c", p=128)), inc="lw", incn=16)
        lwc = p.cnt["lw"]
        for j4 in range(4):
            jj = e * 4 + j4
            p.do("gpsimd", lambda g, jj=jj, j4=j4: g.indirect_dma_start(
                xbg[j4][:], None, xall[0:128, :], IndirectOffsetOnAxis(gint[:, jj:jj + 1], 0)),
                wait=[("mm", p.cnt["mm"]), ("vec", p.cnt["vec"])], inc="ld", incn=16)
        p.do("vector", lambda v: v.nop(), wait=[("ld", p.cnt["ld"])], inc="vec")
        for j4 in range(4):
            emit_transpose(p, psT, [
                (xbg[j4][:, dc * 128:(dc + 1) * 128],
                 xbfm[:, dc * 512 + j4 * 128: dc * 512 + j4 * 128 + 128]) for dc in range(DC)])
        for fc in range(6):
            for wi, wbuf in ((0, wgs), (1, wus)):
                bank = 2 * (fc % 2) + wi
                for dc in range(DC):
                    p.do("tensor", lambda t, fc=fc, dc=dc, bank=bank, wbuf=wbuf: t.matmul(
                        psG[bank][:], wbuf[:, dc * FEXP + fc * 128: dc * FEXP + fc * 128 + 128],
                        xbfm[:, dc * 512:(dc + 1) * 512],
                        start=(dc == 0), stop=(dc == DC - 1)),
                        wait=(([("lw", lwc), ("vec", p.cnt["vec"]), ("act", p.cnt["act"])])
                              if dc == 0 else None),
                        inc=("mm" if dc == DC - 1 else None))
            p.do("scalar", lambda s, fc=fc: s.activation(
                out=t1[:], in_=psG[2 * (fc % 2)][:], func=AF.Silu),
                wait=["mm", ("vec", p.cnt["vec"])], inc="act")
            p.do("scalar", lambda s, fc=fc: s.activation(
                out=t2[:], in_=psG[2 * (fc % 2) + 1][:], func=AF.Identity), inc="act")
            p.do("vector", lambda v, fc=fc: v.tensor_tensor(
                hfm[:, fc * 512:(fc + 1) * 512], t1[:], t2[:], OP.mult),
                wait="act", inc="vec")
        for tsub in range(4):
            for half in range(2):
                bank = (tsub * 2 + half) % 2
                for fc in range(6):
                    p.do("tensor", lambda t, tsub=tsub, half=half, fc=fc, bank=bank: t.matmul(
                        psD[bank][:],
                        hfm[:, fc * 512 + tsub * 128: fc * 512 + tsub * 128 + 128],
                        wds[:, fc * 1024 + half * 512: fc * 1024 + half * 512 + 512],
                        start=(fc == 0), stop=(fc == 5)),
                        wait=([("vec", p.cnt["vec"]), ("st", p.cnt["st"])] if fc == 0 else None),
                        inc=("mm" if fc == 5 else None))
                p.do("vector", lambda v, bank=bank: v.tensor_copy(out=yev[:], in_=psD[bank][:]),
                     wait=["mm", ("st", p.cnt["st"])], inc="vec")
                p.do("sync", lambda sy, e=e, tsub=tsub, half=half: sy.dma_start(
                    out=yo[e * CAP + tsub * 128: e * CAP + tsub * 128 + 128,
                           half * 512:(half + 1) * 512], in_=yev[:]),
                    wait="vec", inc="st", incn=16)
    p.do("sync", lambda sy: sy.nop(), wait=[("st", p.cnt["st"])])
    return p


def build_launch_E():
    p = Prog("E")
    x3i = p.din("x3i", [TPC, D])
    xl4i = p.din("xl4i", [TPC, D], BF16)
    goffi = p.din("goffi", [TPC, 8])
    wvi = p.din("wvi", [TPC, 8])
    yall = p.din("yall", [E_EXP * CAP, D], BF16)
    wsgu = p.din("wsgu", [D, 2 * FEXP])
    wsd = p.din("wsd", [FEXP, D])
    ident = p.din("identE", [128, 128])
    x4o = p.dout("x4o", [TPC, D])

    resid = p.sb("residE", [128, NG * 1024], F32)
    p.it128 = p.sb("it128E", [128, 128], BF16)
    xl4t = p.sb("xl4tE", [128, 1024], BF16)
    x4fm = p.sb("x4fmE", [128, DC * TPC], BF16)
    wgus = p.sb("wgusE", [128, DC * 2 * FEXP], BF16)
    wsds = p.sb("wsdsE", [128, 6 * D], BF16)
    t1 = p.sb("t1E", [128, 512], F32)
    t2 = p.sb("t2E", [128, 512], F32)
    hsh = p.sb("hshE", [128, 6 * 512], BF16)
    goffs = p.sb("goffsE", [128, 8], F32)
    wvs = p.sb("wvsE", [128, 8], F32)
    gint = p.sb("gintE", [128, 8], I32)
    rows = [p.sb(f"rowsE{k}", [128, 1024], BF16) for k in range(8)]
    acc = p.sb("accE", [128, 1024], F32)
    psT = [p.ps(f"psTE{i}", [128, 128], BF16) for i in range(2)]
    psG = [p.ps(f"psGE{i}", [128, 512], F32) for i in range(4)]
    psD = [p.ps(f"psDE{i}", [128, 512], F32) for i in range(2)]

    p.do("gpsimd", lambda g: g.dma_start(out=p.it128[:], in_=ident[:, :]), inc="ld", incn=16)
    p.do("gpsimd", lambda g: g.dma_start(
        out=wgus[:].rearrange("p (dc f) -> p dc f", dc=DC),
        in_=wsgu.rearrange("(dc p) f -> p dc f", p=128)), inc="lw", incn=16)
    p.do("gpsimd", lambda g: g.dma_start(
        out=wsds[:].rearrange("p (fc c) -> p fc c", fc=6),
        in_=wsd.rearrange("(fc p) c -> p fc c", p=128)), inc="lw", incn=16)
    for g2 in range(NG):
        p.do("sync", lambda sy, g2=g2: sy.dma_start(
            out=resid[:, g2 * 1024:(g2 + 1) * 1024], in_=x3i[g2 * 128:(g2 + 1) * 128, :]),
            inc="ld", incn=16)
    ld_resid = p.cnt["ld"]
    for k in range(8):
        p.do("vector", lambda v, k=k: v.memset(rows[k][:], 0.0), inc="vec")
    for g2 in range(NG):
        p.do("sync", lambda sy, g2=g2: sy.dma_start(
            out=xl4t[:], in_=xl4i[g2 * 128:(g2 + 1) * 128, :]),
            wait=[("mm", p.cnt["mm"])], inc="ld", incn=16)
        p.do("vector", lambda v: v.nop(), wait=[("ld", p.cnt["ld"])], inc="vec")
        emit_transpose(p, psT, [
            (xl4t[:, dc * 128:(dc + 1) * 128],
             x4fm[:, dc * TPC + g2 * 128: dc * TPC + g2 * 128 + 128]) for dc in range(DC)])
    for st4 in range(4):
        for gc in range(6):
            for wi in range(2):
                bank = 2 * (gc % 2) + wi
                for dc in range(DC):
                    p.do("tensor", lambda t, st4=st4, gc=gc, wi=wi, dc=dc, bank=bank: t.matmul(
                        psG[bank][:],
                        wgus[:, dc * 2 * FEXP + wi * FEXP + gc * 128:
                             dc * 2 * FEXP + wi * FEXP + gc * 128 + 128],
                        x4fm[:, dc * TPC + st4 * 512: dc * TPC + st4 * 512 + 512],
                        start=(dc == 0), stop=(dc == DC - 1)),
                        wait=([("lw", p.cnt["lw"]), ("vec", p.cnt["vec"]),
                               ("act", p.cnt["act"])] if dc == 0 else None),
                        inc=("mm" if dc == DC - 1 else None))
            p.do("scalar", lambda s, gc=gc: s.activation(
                out=t1[:], in_=psG[2 * (gc % 2)][:], func=AF.Silu),
                wait=["mm", ("vec", p.cnt["vec"])], inc="act")
            p.do("scalar", lambda s, gc=gc: s.activation(
                out=t2[:], in_=psG[2 * (gc % 2) + 1][:], func=AF.Identity), inc="act")
            p.do("vector", lambda v, gc=gc: v.tensor_tensor(
                hsh[:, gc * 512:(gc + 1) * 512], t1[:], t2[:], OP.mult),
                wait="act", inc="vec")
        for tsub in range(4):
            for half in range(2):
                bank = (tsub * 2 + half) % 2
                for fc in range(6):
                    p.do("tensor", lambda t, st4=st4, tsub=tsub, half=half, fc=fc, bank=bank:
                         t.matmul(psD[bank][:],
                                  hsh[:, fc * 512 + tsub * 128: fc * 512 + tsub * 128 + 128],
                                  wsds[:, fc * 1024 + half * 512: fc * 1024 + half * 512 + 512],
                                  start=(fc == 0), stop=(fc == 5)),
                         wait=([("vec", p.cnt["vec"])] if fc == 0 else None),
                         inc=("mm" if fc == 5 else None))
                g2 = st4 * 4 + tsub
                p.do("vector", lambda v, g2=g2, half=half, bank=bank: v.tensor_tensor(
                    resid[:, g2 * 1024 + half * 512: g2 * 1024 + half * 512 + 512],
                    resid[:, g2 * 1024 + half * 512: g2 * 1024 + half * 512 + 512],
                    psD[bank][:], OP.add), wait=["mm", ("ld", ld_resid)], inc="vec")
    for tg in range(NG):
        p.do("sync", lambda sy, tg=tg: sy.dma_start(out=goffs[:], in_=goffi[tg * 128:(tg + 1) * 128, :]),
             wait=[("vec", p.cnt["vec"]), ("gp", p.cnt["gp"])], inc="ld", incn=16)
        p.do("sync", lambda sy, tg=tg: sy.dma_start(out=wvs[:], in_=wvi[tg * 128:(tg + 1) * 128, :]),
             inc="ld", incn=16)
        p.do("vector", lambda v: v.tensor_copy(out=gint[:], in_=goffs[:]),
             wait=[("ld", p.cnt["ld"])], inc="vec")
        p.do("vector", lambda v: v.nop(), inc="gp")
        for k in range(8):
            p.do("gpsimd", lambda g, k=k: g.indirect_dma_start(
                rows[k][:], None, yall[0:128, :],
                IndirectOffsetOnAxis(gint[:, k:k + 1], 0)),
                wait=[("gp", p.cnt["gp"]), ("vec", p.cnt["vec"])], inc="ld", incn=16)
        p.do("vector", lambda v: v.tensor_scalar(acc[:], rows[0][:], wvs[:, 0:1], None,
                                                 OP.mult), wait=[("ld", p.cnt["ld"])], inc="vec")
        for k in range(1, 8):
            p.do("vector", lambda v, k=k: v.scalar_tensor_tensor(
                acc[:], rows[k][:], wvs[:, k:k + 1], acc[:],
                OP.mult, OP.add), wait="vec", inc="vec")
        p.do("vector", lambda v, tg=tg: v.tensor_tensor(
            resid[:, tg * 1024:(tg + 1) * 1024], resid[:, tg * 1024:(tg + 1) * 1024],
            acc[:], OP.add), wait="vec", inc="vec")
    p.do("sync", lambda sy: sy.dma_start(
        out=x4o.rearrange("(g p) d -> p g d", p=128),
        in_=resid[:].rearrange("p (g d) -> p g d", g=NG)),
        wait=[("vec", p.cnt["vec"])], inc="st", incn=16)
    p.do("sync", lambda sy: sy.nop(), wait=[("st", p.cnt["st"])])
    return p


def host_layouts(inputs):
    inp = {k: np.asarray(v) for k, v in inputs.items()}
    L = {}
    wqkv = np.ascontiguousarray(inp["L0_attn_W_qkv"].reshape(D, H, HD, 3))
    L["wq0"] = np.ascontiguousarray(wqkv[..., 0].reshape(D, D))
    L["wk0"] = np.ascontiguousarray(wqkv[..., 1].reshape(D, D))
    L["wv0"] = np.ascontiguousarray(wqkv[..., 2].reshape(D, D))
    L["wout"] = np.ascontiguousarray(inp["L0_attn_W_out"])
    L["w1t"] = np.ascontiguousarray(inp["L0_ffn_w1_w"].T)
    L["w2t"] = np.ascontiguousarray(inp["L0_ffn_w2_w"].T)
    L["h1t"] = np.ascontiguousarray(inp["L0_ffn_h1_w"].T)
    L["h2t"] = np.ascontiguousarray(inp["L0_ffn_h2_w"].T)
    L["w3t"] = np.ascontiguousarray(inp["L0_ffn_w3_w"].T)
    for nm, key in (("s1", "L0_ffn_w1_s"), ("s2", "L0_ffn_w2_s"),
                    ("sh1", "L0_ffn_h1_s"), ("sh2", "L0_ffn_h2_s"), ("s3", "L0_ffn_w3_s")):
        L[nm] = np.ascontiguousarray(inp[key].reshape(-1, 128).T)
    L["wq1"] = np.ascontiguousarray(inp["L1_attn_Wq"].reshape(D, D))
    L["wk1"] = np.ascontiguousarray(inp["L1_attn_Wk"].reshape(D, D))
    L["wv1"] = np.ascontiguousarray(inp["L1_attn_Wv"].reshape(D, D))
    L["wo1"] = np.ascontiguousarray(inp["L1_attn_Wo"].reshape(D, D))
    L["bq1"] = np.ascontiguousarray(inp["L1_attn_bq"].reshape(1, D))
    L["bk1"] = np.ascontiguousarray(inp["L1_attn_bk"].reshape(1, D))
    L["bv1"] = np.ascontiguousarray(inp["L1_attn_bv"].reshape(1, D))
    L["bo1"] = np.ascontiguousarray(inp["L1_attn_bo"].reshape(1, D))
    L["rt"] = np.ascontiguousarray(inp["L1_ffn_router_w"].T)
    L["wsgu"] = np.ascontiguousarray(inp["L1_ffn_ws_gate_up"])
    L["wsd"] = np.ascontiguousarray(inp["L1_ffn_ws_down"])
    L["wg"] = inp["L1_ffn_w_gate"]
    L["wu"] = inp["L1_ffn_w_up"]
    L["wd"] = inp["L1_ffn_w_down"]
    L["X"] = np.ascontiguousarray(inp["L0_attn_X"].reshape(B * S, D))
    L["ident"] = np.eye(128, dtype=np.float32)
    return L


def run_A(L, debug=False, stages=4):
    p = build_launch_A(debug, stages)
    nc = p.finalize()
    in_maps = []
    for c in range(NCORE):
        bi = c // 4
        im = dict(
            xb=L["X"][bi * S:(bi + 1) * S],
            xmy=L["X"][c * TPC:(c + 1) * TPC],
            wq0=L["wq0"], wk0=L["wk0"], wv0=L["wv0"], wout=L["wout"],
            w1t=L["w1t"], w2t=L["w2t"], h1t=L["h1t"], h2t=L["h2t"], w3t=L["w3t"],
            s1=L["s1"], s2=L["s2"], sh1=L["sh1"], sh2=L["sh2"], s3=L["s3"],
            wq1=L["wq1"], wk1=L["wk1"], wv1=L["wv1"],
            bq1=L["bq1"], bk1=L["bk1"], bv1=L["bv1"],
            ident=L["ident"],
        )
        in_maps.append(im)
    res = run_bass_kernel_spmd(nc, in_maps, core_ids=list(range(NCORE)))
    return res.results


def _run(build_fn, in_maps, _cache={}):
    key = build_fn.__name__
    if key not in _cache:
        p = build_fn() if build_fn is not build_launch_A else build_launch_A(False, 4)
        _cache[key] = p.finalize()
    nc = _cache[key]
    try:
        res = run_bass_kernel_spmd(nc, in_maps, core_ids=list(range(NCORE)))
    except Exception:
        res = run_bass_kernel_spmd(nc, in_maps, core_ids=list(range(NCORE)))
    return res.results


def kernel(**inputs):
    L = host_layouts(inputs)
    strict_upper = np.triu(np.ones((128, 128), np.float32), 1)

    inA = []
    for c in range(NCORE):
        bi = c // 4
        inA.append(dict(
            xb=L["X"][bi * S:(bi + 1) * S], xmy=L["X"][c * TPC:(c + 1) * TPC],
            wq0=L["wq0"], wk0=L["wk0"], wv0=L["wv0"], wout=L["wout"],
            w1t=L["w1t"], w2t=L["w2t"], h1t=L["h1t"], h2t=L["h2t"], w3t=L["w3t"],
            s1=L["s1"], s2=L["s2"], sh1=L["sh1"], sh2=L["sh2"], s3=L["s3"],
            wq1=L["wq1"], wk1=L["wk1"], wv1=L["wv1"],
            bq1=L["bq1"], bk1=L["bk1"], bv1=L["bv1"], ident=L["ident"]))
    rA = _run(build_launch_A, inA)

    inB = []
    for c in range(NCORE):
        grp = range(0, 4) if c < 4 else range(4, 8)
        kvp = np.stack([rA[g]["kvo"] for g in grp], 0)
        inB.append(dict(x2i=rA[c]["x2o"], qfi=rA[c]["qfo"], kvp=kvp,
                        wo1=L["wo1"], bo1=L["bo1"], rt=L["rt"], ident=L["ident"]))
    rB = _run(build_launch_B, inB)

    tota = np.concatenate([rB[c]["toto"] for c in range(NCORE)], 0)
    inB2 = []
    for c in range(NCORE):
        premask = (np.arange(NCORE) < c).astype(np.float32).reshape(8, 1)
        gtok = (np.arange(TPC, dtype=np.float32) + c * TPC + 1.0).reshape(TPC, 1)
        inB2.append(dict(idxi=rB[c]["idxo"], wni=rB[c]["wno"], tota=tota,
                         premask=premask, gtok=gtok, uts=strict_upper))
    rB2 = _run(build_launch_B2, inB2)

    xall = np.concatenate([rB[c]["xl4o"] for c in range(NCORE)], 0)
    plist_all = [np.asarray(rB2[c]["plisto"])[:E_EXP * CAP].reshape(E_EXP, CAP)
                 for c in range(NCORE)]
    inC = []
    for c in range(NCORE):
        mine = np.stack([pl[c * EPPC:(c + 1) * EPPC].reshape(-1).reshape(32, 128).T.copy()
                         for pl in plist_all], 0)
        inC.append(dict(plists=np.ascontiguousarray(mine), xall=xall,
                        wgi=L["wg"][c * EPPC:(c + 1) * EPPC],
                        wui=L["wu"][c * EPPC:(c + 1) * EPPC],
                        wdi=L["wd"][c * EPPC:(c + 1) * EPPC], identC=L["ident"]))
    rC = _run(build_launch_C, inC)

    yall = np.concatenate([rC[c]["yo"] for c in range(NCORE)], 0)
    inE = []
    for c in range(NCORE):
        inE.append(dict(x3i=rB[c]["x3o"], xl4i=rB[c]["xl4o"], goffi=rB2[c]["goffo"],
                        wvi=rB2[c]["wvo"], yall=yall, wsgu=L["wsgu"], wsd=L["wsd"],
                        identE=L["ident"]))
    rE = _run(build_launch_E, inE)

    out = np.concatenate([rE[c]["x4o"] for c in range(NCORE)], 0).reshape(B, S, D)
    return out.astype(np.float32)


# version 03 (final)
# speedup vs baseline: 1.2649x; 1.0115x over previous
import math
from contextlib import ExitStack

import numpy as np

import concourse.bass as bass
import concourse.mybir as mybir
from concourse.bass_core_rust import IndirectOffsetOnAxis
from concourse.bass_utils import run_bass_kernel_spmd

F32 = mybir.dt.float32
BF16 = mybir.dt.bfloat16
I32 = mybir.dt.int32
AF = mybir.ActivationFunctionType
OP = mybir.AluOpType
AX = mybir.AxisListType

NCORE = 8
B, S, D = 2, 8192, 1024
TPC = 2048
NG = TPC // 128
DC = D // 128
H, HD = 16, 64
CSHIFT = 16.0
EPS_LN = 1e-5
E_EXP, TOPK, CAP, FEXP = 64, 8, 512, 768
EPPC = E_EXP // NCORE
BIG = 65536.0

ENGINES = ("sync", "gpsimd", "tensor", "scalar", "vector")


class Prog:

    def __init__(self, name):
        self.nc = bass.Bass("TRN2", target_bir_lowering=False)
        self.es = ExitStack()
        self.streams = {e: [] for e in ENGINES}
        self.sem = {}
        self.cnt = {}
        for n in ("ld", "lw", "mm", "act", "vec", "st", "gp"):
            self.sem[n] = self.es.enter_context(self.nc.semaphore(n))
            self.cnt[n] = 0
        self.ap = {}

    def din(self, name, shape, dt=F32):
        self.ap[name] = self.nc.dram_tensor(name, shape, dt, kind="ExternalInput").ap()
        return self.ap[name]

    def dout(self, name, shape, dt=F32):
        self.ap[name] = self.nc.dram_tensor(name, shape, dt, kind="ExternalOutput").ap()
        return self.ap[name]

    def dtmp(self, name, shape, dt=F32):
        self.ap[name] = self.nc.dram_tensor(name, shape, dt).ap()
        return self.ap[name]

    def sb(self, name, shape, dt=F32, es=None):
        return (es or self.es).enter_context(self.nc.sbuf_tensor(name, shape, dt))

    def ps(self, name, shape, dt=F32, es=None):
        return (es or self.es).enter_context(self.nc.psum_tensor(name, shape, dt))

    def dov(self, emit, wait=None):
        w = [("vec", self.cnt["vec"])]
        if wait is not None:
            w += wait if isinstance(wait, list) else [wait]
        self.do("vector", emit, wait=w, inc="vec")

    def do(self, eng, emit, wait=None, inc=None, incn=1):
        w = []
        if wait is not None:
            for it in (wait if isinstance(wait, list) else [wait]):
                if isinstance(it, tuple):
                    w.append(it)
                else:
                    w.append((it, self.cnt[it]))
        self.streams[eng].append((w, emit, inc, incn))
        if inc is not None:
            self.cnt[inc] += incn

    def finalize(self):
        with self.es, self.nc.Block() as blk:
            for engname in ENGINES:
                items = self.streams[engname]
                if not items:
                    continue
                deco = getattr(blk, engname)

                def make(items):
                    def body(eng):
                        for w, emit, inc, incn in items:
                            for s, v in w:
                                if v > 0:
                                    eng.wait_ge(self.sem[s], v)
                            instr = emit(eng)
                            if inc is not None:
                                instr.then_inc(self.sem[inc], incn)
                    return body

                deco(make(items))
        return self.nc


def emit_ln(p, src, out_bf, w0=None, out_f32=None):
    p.do("scalar", lambda s, a=src: s.activation(
        out=p.scr[:], in_=a, func=AF.Square, accum_out=p.sq[:]),
        wait=(list(w0) if w0 else []) + [("vec", p.cnt["vec"])], inc="act")
    p.do("vector", lambda v, a=src: v.reduce_sum(p.musum[:], a, axis=AX.X),
         wait=(list(w0) if w0 else None), inc="vec")
    p.do("vector", lambda v: v.tensor_scalar(p.mu[:], p.musum[:], 1.0 / 1024.0, None, OP.mult),
         wait="vec", inc="vec")
    p.do("vector", lambda v: v.tensor_mul(p.musq[:], p.mu[:], p.mu[:]),
         wait=[("vec", p.cnt["vec"]), ("act", p.cnt["act"])], inc="vec")
    p.do("vector", lambda v: v.scalar_tensor_tensor(
        p.var[:], p.sq[:], 1.0 / 1024.0, p.musq[:], OP.mult, OP.subtract),
        wait=["vec", ("act", p.cnt["act"])], inc="vec")
    p.do("scalar", lambda s: s.activation(out=p.sd[:], in_=p.var[:], func=AF.Sqrt,
                                          bias=p.epsb[:]), wait="vec", inc="act")
    p.do("vector", lambda v: v.reciprocal(p.rstd[:], p.sd[:]), wait="act", inc="vec")
    p.do("vector", lambda v, a=src, o=out_bf: v.tensor_scalar(
        o, a, p.mu[:, 0:1], p.rstd[:, 0:1], OP.subtract, OP.mult), wait="vec", inc="vec")
    if out_f32 is not None:
        p.do("vector", lambda v, a=src, o=out_f32: v.tensor_scalar(
            o, a, p.mu[:, 0:1], p.rstd[:, 0:1], OP.subtract, OP.mult), inc="vec")


def emit_transpose(p, psT, pairs):
    for i, (srcs, dsts) in enumerate(pairs):
        bank = i % 2
        p.do("tensor", lambda t, s=srcs, b=bank: t.transpose(psT[b][:], s, p.it128[:]),
             wait=[("vec", p.cnt["vec"])], inc="mm")
        p.do("vector", lambda v, d=dsts, b=bank: v.tensor_copy(out=d, in_=psT[b][:]),
             wait="mm", inc="vec")


def emit_ln_tile(p, psT, srcs, xfm, gbase=0, w0s=None, out_f32_slices=None):
    for g in range(4):
        src = srcs[g]
        w0 = (w0s[g] if w0s else None)
        ww = list(w0) if w0 else []
        ww.append(("mm", p.cnt["mm"]))
        emit_ln(p, src, p.xlg[:], w0=ww,
                out_f32=(out_f32_slices[g] if out_f32_slices else None))
        emit_transpose(p, psT, [
            (p.xlg[:, dc * 128:(dc + 1) * 128],
             xfm[:, dc * 512 + g * 128: dc * 512 + g * 128 + 128]) for dc in range(DC)])


def build_launch_A(debug=False, stages=4):
    p = Prog("A")
    xb = p.din("xb", [S, D])
    xmy = p.din("xmy", [TPC, D])
    wq0 = p.din("wq0", [D, D]); wk0 = p.din("wk0", [D, D]); wv0 = p.din("wv0", [D, D])
    wout = p.din("wout", [D, D])
    w1t = p.din("w1t", [D, 4096]); w2t = p.din("w2t", [D, 4096])
    h1t = p.din("h1t", [4096, 4096]); h2t = p.din("h2t", [4096, 4096])
    w3t = p.din("w3t", [4096, D])
    s1 = p.din("s1", [128, 32]); s2 = p.din("s2", [128, 32])
    sh1 = p.din("sh1", [128, 32]); sh2 = p.din("sh2", [128, 32])
    s3 = p.din("s3", [128, 8])
    wq1 = p.din("wq1", [D, D]); wk1 = p.din("wk1", [D, D]); wv1 = p.din("wv1", [D, D])
    bq1 = p.din("bq1", [1, D]); bk1 = p.din("bk1", [1, D]); bv1 = p.din("bv1", [1, D])
    ident = p.din("ident", [128, 128])

    mk = p.dout if debug else p.dtmp
    kfull = mk("kfull", [D, S], BF16)
    vfull = mk("vfull", [S, D], BF16)
    h1b = p.dtmp("h1b", [32, 128, 32 * 128], BF16)
    h2b = p.dtmp("h2b", [32, 128, 32 * 128], BF16)
    w12b = p.dtmp("w12b", [32, 128, 2 * DC * 128], BF16)
    w3b = p.dtmp("w3b", [8, 128, 32 * 128], BF16)

    x2o = p.dout("x2o", [TPC, D])
    x1o = p.dout("x1o", [TPC, D]) if debug else None
    kchk = p.dout("kchk", [128, 1024], BF16) if debug else None
    attnpre = p.dout("attnpre", [D, TPC], BF16) if debug else None
    qbo = p.dout("qbo", [D, TPC], BF16) if debug else None
    xfmo = p.dout("xfmo", [128, DC * 512], BF16) if debug else None
    wqso = p.dout("wqso", [128, DC * 1024], BF16) if debug else None
    wqso2 = p.dout("wqso2", [128, DC * 1024], BF16) if debug else None
    wkso = p.dout("wkso", [128, DC * 1024], BF16) if debug else None
    xfmo2 = p.dout("xfmo2", [128, DC * 512], BF16) if debug else None
    psoo = p.dout("psoo", [128, 512]) if debug else None
    qbo2 = p.dout("qbo2", [D, TPC], BF16) if debug else None
    pswo = p.dout("pswo", [NG, 128, 512]) if debug else None
    attno = p.dout("attno", [D, TPC], BF16) if debug else None
    qfo = p.dout("qfo", [D, TPC], BF16)
    kvo = p.dout("kvo", [H, HD, 65])

    resid = p.sb("resid", [128, NG * 1024], F32)
    p.it128 = p.sb("it128", [128, 128], BF16)
    onesbf = p.sb("onesbf", [128, 512], BF16)
    onesf32 = p.sb("onesf32", [128, 64], F32)
    p.scr = p.sb("scr", [128, 1024], F32)
    p.mu = p.sb("mu", [128, 1]); p.sq = p.sb("sq", [128, 1])
    p.var = p.sb("var", [128, 1]); p.rstd = p.sb("rstd", [128, 1])
    p.musum = p.sb("musum", [128, 1]); p.musq = p.sb("musq", [128, 1])
    p.sd = p.sb("sd", [128, 1])
    p.xlg = p.sb("xlg", [128, 1024], BF16)
    p.epsb = p.sb("epsb", [128, 1], F32)
    p.cshb = p.sb("cshb", [128, 1], F32)

    p.do("gpsimd", lambda g: g.dma_start(out=p.it128[:], in_=ident[:, :]), inc="ld", incn=16)
    for g in range(NG):
        p.do("sync", lambda sy, g=g: sy.dma_start(
            out=resid[:, g * 1024:(g + 1) * 1024], in_=xmy[g * 128:(g + 1) * 128, :]),
            inc="ld", incn=16)
    p.do("vector", lambda v: v.memset(onesbf[:], 1.0), inc="vec")
    p.do("vector", lambda v: v.memset(p.epsb[:], EPS_LN), inc="vec")
    p.do("vector", lambda v: v.memset(p.cshb[:], -CSHIFT), inc="vec")
    p.do("vector", lambda v: v.memset(onesf32[:], 1.0), inc="vec")

    with ExitStack() as e12:
        qbuf = p.sb("qbuf", [128, DC * TPC], BF16, e12)
        attn = p.sb("attn", [128, DC * TPC], BF16, e12)

        with ExitStack() as e1:
            wqs = p.sb("wqs", [128, DC * 1024], BF16, e1)
            wks = p.sb("wks", [128, DC * 1024], BF16, e1)
            wvs = p.sb("wvs", [128, DC * 1024], BF16, e1)
            xg = [p.sb(f"xg{i}", [128, 1024], F32, e1) for i in range(2)]
            xfmA = p.sb("xfm1", [128, DC * 512], BF16, e1)
            kev = [p.sb(f"kev{i}", [128, 512], BF16, e1) for i in range(2)]
            vev = [p.sb(f"vev{i}", [128, 512], BF16, e1) for i in range(2)]
            psT1 = [p.ps(f"psT1{i}", [128, 128], BF16, e1) for i in range(2)]
            psO1 = [p.ps(f"psO1{i}", [128, 512], F32, e1) for i in range(4)]

            for w, t_ in ((wq0, wqs), (wk0, wks), (wv0, wvs)):
                p.do("gpsimd", lambda g, w=w, t_=t_: g.dma_start(
                    out=t_[:].rearrange("p (dc c) -> p dc c", dc=DC),
                    in_=w.rearrange("(dc p) c -> p dc c", p=128)), inc="lw", incn=16)
            lw_w = p.cnt["lw"]
            for oc in range(32):
                for hsrc, hdst in ((h1t, h1b), (h2t, h2b)):
                    p.do("gpsimd", lambda g, oc=oc, hsrc=hsrc, hdst=hdst: g.dma_start(
                        out=hdst[oc].rearrange("p (ic c) -> p ic c", ic=32),
                        in_=hsrc.rearrange("(ic p) o -> p ic o", p=128)[:, :, oc * 128:(oc + 1) * 128]),
                        inc="lw", incn=16)
                for wi, wsrc in ((0, w1t), (1, w2t)):
                    p.do("gpsimd", lambda g, oc=oc, wi=wi, wsrc=wsrc: g.dma_start(
                        out=w12b[oc].rearrange("p (w dc c) -> p w dc c", w=2, dc=DC)[:, wi],
                        in_=wsrc.rearrange("(dc p) o -> p dc o", p=128)[:, :, oc * 128:(oc + 1) * 128]),
                        inc="lw", incn=16)
                if oc < 8:
                    p.do("gpsimd", lambda g, oc=oc: g.dma_start(
                        out=w3b[oc].rearrange("p (ic c) -> p ic c", ic=32),
                        in_=w3t.rearrange("(ic p) o -> p ic o", p=128)[:, :, oc * 128:(oc + 1) * 128]),
                        inc="lw", incn=16)
            lw_cast = p.cnt["lw"]

            def proj_fm(wbuf, dst_fn, extra_wait):
                for cc in range(DC):
                    bank = cc % 2
                    for dc in range(DC):
                        p.do("tensor", lambda t, cc=cc, dc=dc, bank=bank, wbuf=wbuf: t.matmul(
                            psO1[bank][:], wbuf[:, dc * 1024 + cc * 128: dc * 1024 + cc * 128 + 128],
                            xfmA[:, dc * 512:(dc + 1) * 512],
                            start=(dc == 0), stop=(dc == DC - 1)),
                            wait=(([("lw", lw_w), ("vec", p.cnt["vec"])] + list(extra_wait))
                                  if dc == 0 else None),
                            inc=("mm" if dc == DC - 1 else None))
                    dst_fn(cc, bank)

            if debug:
                p.do("sync", lambda sy: sy.dma_start(out=wqso[:, :], in_=wqs[:]),
                     wait=[("lw", p.cnt["lw"])], inc="st", incn=16)
                p.do("sync", lambda sy: sy.dma_start(out=wkso[:, :], in_=wks[:]),
                     inc="st", incn=16)
            for bt in range(16):
                ld0 = p.cnt["ld"]
                for g in range(4):
                    xgb = xg[g % 2]
                    p.do("sync", lambda sy, bt=bt, g=g, xgb=xgb: sy.dma_start(
                        out=xgb[:], in_=xb[bt * 512 + g * 128: bt * 512 + (g + 1) * 128, :]),
                        wait=[("act", p.cnt["act"]), ("vec", p.cnt["vec"])], inc="ld", incn=16)
                    emit_ln(p, xgb[:], p.xlg[:],
                            w0=[("ld", p.cnt["ld"]), ("mm", p.cnt["mm"])])
                    emit_transpose(p, psT1, [
                        (p.xlg[:, dc * 128:(dc + 1) * 128],
                         xfmA[:, dc * 512 + g * 128: dc * 512 + g * 128 + 128]) for dc in range(DC)])

                def k_dst(cc, bank):
                    kb = kev[cc % 2]
                    p.do("vector", lambda v, kb=kb, bank=bank: v.tensor_copy(
                        out=kb[:], in_=psO1[bank][:]),
                        wait=["mm", ("st", p.cnt["st"])], inc="vec")
                    p.do("sync", lambda sy, kb=kb, cc=cc, bt=bt: sy.dma_start(
                        out=kfull[cc * 128:(cc + 1) * 128, bt * 512:(bt + 1) * 512], in_=kb[:]),
                        wait="vec", inc="st", incn=16)
                proj_fm(wks, k_dst, [])
                for ts in range(4):
                    for half in range(2):
                        bank = 2 + (ts * 2 + half) % 2
                        vb = vev[(ts * 2 + half) % 2]
                        for dc in range(DC):
                            p.do("tensor", lambda t, ts=ts, half=half, dc=dc, bank=bank: t.matmul(
                                psO1[bank][:],
                                xfmA[:, dc * 512 + ts * 128: dc * 512 + ts * 128 + 128],
                                wvs[:, dc * 1024 + half * 512: dc * 1024 + half * 512 + 512],
                                start=(dc == 0), stop=(dc == DC - 1)),
                                wait=([("vec", p.cnt["vec"])] if dc == 0 else None),
                                inc=("mm" if dc == DC - 1 else None))
                        p.do("vector", lambda v, vb=vb, bank=bank: v.tensor_copy(
                            out=vb[:], in_=psO1[bank][:]),
                            wait=["mm", ("st", p.cnt["st"])], inc="vec")
                        p.do("sync", lambda sy, vb=vb, ts=ts, half=half, bt=bt: sy.dma_start(
                            out=vfull[bt * 512 + ts * 128: bt * 512 + ts * 128 + 128,
                                      half * 512:(half + 1) * 512], in_=vb[:]),
                            wait="vec", inc="st", incn=16)

            if debug:
                p.do("sync", lambda sy: sy.dma_start(out=wqso2[:, :], in_=wqs[:]),
                     wait=[("vec", p.cnt["vec"]), ("mm", p.cnt["mm"]),
                           ("act", p.cnt["act"]), ("st", p.cnt["st"])], inc="st", incn=16)
            for qt in range(4):
                if debug and qt == 1:
                    for _cc in range(DC):
                        p.do("sync", lambda sy, _cc=_cc: sy.dma_start(
                            out=xfmo[:, _cc * 512:(_cc + 1) * 512],
                            in_=xfmA[:, _cc * 512:(_cc + 1) * 512]),
                            wait=[("vec", p.cnt["vec"]), ("mm", p.cnt["mm"])], inc="st", incn=16)
                    p.do("vector", lambda v: v.nop(), wait=[("st", p.cnt["st"])], inc="vec")
                emit_ln_tile(p, psT1,
                             [resid[:, (qt * 4 + g) * 1024:(qt * 4 + g + 1) * 1024] for g in range(4)],
                             xfmA)

                def q_dst(cc, bank, qt=qt):
                    if debug and qt == 0 and cc == 0:
                        p.do("vector", lambda v, bank=bank: v.tensor_copy(
                            out=p.scr[:, 0:512], in_=psO1[bank][:]), wait="mm", inc="vec")
                        p.do("sync", lambda sy: sy.dma_start(out=psoo[:, :], in_=p.scr[:, 0:512]),
                             wait="vec", inc="st", incn=16)
                        p.do("vector", lambda v: v.nop(), wait=[("st", p.cnt["st"])], inc="vec")
                    p.do("vector", lambda v, cc=cc, bank=bank, qt=qt: v.tensor_copy(
                        out=qbuf[:, cc * TPC + qt * 512: cc * TPC + qt * 512 + 512],
                        in_=psO1[bank][:]), wait="mm", inc="vec")
                proj_fm(wqs, q_dst, [])

        if debug:
            for cc in range(DC):
                p.do("sync", lambda sy, cc=cc: sy.dma_start(
                    out=qbo[cc * 128:(cc + 1) * 128, :], in_=qbuf[:, cc * TPC:(cc + 1) * TPC]),
                    wait=[("vec", p.cnt["vec"])], inc="st", incn=16)
        if stages < 1.5:
            p.do("sync", lambda sy: sy.nop(), wait=[("st", p.cnt["st"])])
            return p
        with ExitStack() as e2:
            kpair = p.sb("kpair", [128, S], BF16, e2)
            vones = [p.sb(f"vones{i}", [128, 64 * 65], BF16, e2) for i in range(2)]
            pT = [p.sb(f"pT{i}", [128, 1024], BF16, e2) for i in range(2)]
            znorm = p.sb("znorm", [128, 512], F32, e2)
            attnS = p.sb("attnS", [128, 512], F32, e2)
            psS = [[p.ps(f"psS{i}_{j}", [128, 512], F32, e2) for j in range(2)] for i in range(2)]
            psAV = p.ps("psAV", [128, 512], F32, e2)
            psN = p.ps("psN", [128, 512], F32, e2)

            for i in range(2):
                p.do("vector", lambda v, i=i: v.memset(
                    vones[i][:].rearrange("p (j m) -> p j m", m=65)[:, :, 64:65], 1.0), inc="vec")

            hp_list = range(8) if stages >= 2.0 else ([0] if stages >= 1.5 else [])
            for hp in hp_list:
                p.do("sync", lambda sy, hp=hp: sy.dma_start(
                    out=kpair[0:64, :], in_=kfull[(2 * hp) * 64:(2 * hp) * 64 + 64, :]),
                    wait=[("mm", p.cnt["mm"])], inc="ld", incn=16)
                p.do("sync", lambda sy, hp=hp: sy.dma_start(
                    out=kpair[64:128, :], in_=kfull[(2 * hp + 1) * 64:(2 * hp + 1) * 64 + 64, :]),
                    inc="ld", incn=16)
                for i in range(2):
                    h = 2 * hp + i
                    p.do("sync", lambda sy, h=h, i=i: sy.dma_start(
                        out=vones[i][:].rearrange("p (j m) -> p j m", m=65)[:, :, 0:64],
                        in_=vfull.rearrange("(j p) c -> p j c", p=128)[:, :, h * 64:(h + 1) * 64]),
                        inc="ld", incn=16)
                ld_hp = p.cnt["ld"]
                for i in (range(2) if stages >= 2.0 else []):
                    hbase = i * 64
                    for qt in range(4):
                        act_base = p.cnt["act"]
                        for g in range(32):
                            for j in range(2):
                                kt = 2 * g + j
                                w = None
                                if j == 0:
                                    w = [("act", p.cnt["act"]), ("vec", p.cnt["vec"]),
                                         ("ld", ld_hp)]
                                p.do("tensor", lambda t, g=g, j=j, kt=kt, hbase=hbase, hp=hp, qt=qt:
                                     t.matmul(psS[g % 2][j][:],
                                              kpair[hbase:hbase + 64, kt * 128:(kt + 1) * 128],
                                              qbuf[hbase:hbase + 64,
                                                   hp * 2048 + qt * 512: hp * 2048 + qt * 512 + 512],
                                              start=True, stop=True),
                                     wait=w, inc=("mm" if j == 1 else None))
                            if g >= 1:
                                for j in range(2):
                                    kt = 2 * (g - 1) + j
                                    p.do("tensor", lambda t, g=g, j=j, kt=kt, i=i:
                                         t.matmul(psAV[0:65, :],
                                                  vones[i][:, kt * 65: kt * 65 + 65],
                                                  pT[(g - 1) % 2][:, j * 512:(j + 1) * 512],
                                                  start=(kt == 0), stop=False),
                                         wait=([("act", p.cnt["act"])] if j == 0 else None),
                                         inc=("mm" if j == 1 else None))
                            p.do("scalar", lambda s, g=g: s.activation(
                                out=pT[g % 2][:, 0:512], in_=psS[g % 2][0][:], func=AF.Exp,
                                scale=0.125, bias=p.cshb[:]),
                                wait=[("mm", p.cnt["mm"])])
                            p.do("scalar", lambda s, g=g: s.activation(
                                out=pT[g % 2][:, 512:1024], in_=psS[g % 2][1][:], func=AF.Exp,
                                scale=0.125, bias=p.cshb[:]), inc="act")
                        for j in range(2):
                            kt = 62 + j
                            p.do("tensor", lambda t, j=j, kt=kt, i=i:
                                 t.matmul(psAV[0:65, :], vones[i][:, kt * 65: kt * 65 + 65],
                                          pT[1][:, j * 512:(j + 1) * 512],
                                          start=False, stop=(j == 1)),
                                 wait=([("act", p.cnt["act"])] if j == 0 else None),
                                 inc=("mm" if j == 1 else None))
                        p.do("vector", lambda v: v.reciprocal(znorm[64:65, :], psAV[64:65, :]),
                             wait="mm", inc="vec")
                        p.do("tensor", lambda t: t.matmul(psN[0:64, :], onesf32[64:65, 0:64],
                                                          znorm[64:65, :], start=True, stop=True),
                             wait="vec", inc="mm")
                        p.do("vector", lambda v: v.tensor_copy(out=attnS[0:64, :], in_=psAV[0:64, :]),
                             inc="vec")
                        p.do("vector", lambda v, i=i, hp=hp, qt=qt: v.tensor_tensor(
                            attn[i * 64:(i + 1) * 64,
                                 hp * 2048 + qt * 512: hp * 2048 + qt * 512 + 512],
                            attnS[0:64, :], psN[0:64, :], OP.mult),
                            wait="mm", inc="vec")

        if stages < 2.0:
            p.do("sync", lambda sy: sy.nop(), wait=[("st", p.cnt["st"]), ("ld", p.cnt["ld"])])
            return p
        with ExitStack() as e2b:
            wos = p.sb("wos", [128, DC * 1024], BF16, e2b)
            psW = [p.ps(f"psW{i}", [128, 512], F32, e2b) for i in range(2)]
            p.do("gpsimd", lambda g: g.dma_start(
                out=wos[:].rearrange("p (dc c) -> p dc c", dc=DC),
                in_=wout.rearrange("(dc p) c -> p dc c", p=128)),
                wait=[("mm", p.cnt["mm"]), ("act", p.cnt["act"]), ("vec", p.cnt["vec"]), ("st", p.cnt["st"])], inc="lw", incn=16)
            lw_o = p.cnt["lw"]
            p.do("vector", lambda v: v.nop(),
                 wait=[("mm", p.cnt["mm"]), ("act", p.cnt["act"]), ("st", p.cnt["st"]),
                       ("ld", p.cnt["ld"]), ("lw", p.cnt["lw"])], inc="vec")
            if debug:
                for cc in range(DC):
                    p.do("sync", lambda sy, cc=cc: sy.dma_start(
                        out=attnpre[cc * 128:(cc + 1) * 128, :],
                        in_=attn[:, cc * TPC:(cc + 1) * TPC]),
                        wait=[("vec", p.cnt["vec"])], inc="st", incn=16)
                    p.do("sync", lambda sy, cc=cc: sy.dma_start(
                        out=qbo2[cc * 128:(cc + 1) * 128, :], in_=qbuf[:, cc * TPC:(cc + 1) * TPC]),
                        inc="st", incn=16)
            for tg in range(NG):
                for half in range(2):
                    bank = (tg * 2 + half) % 2
                    for hp in range(8):
                        p.do("tensor", lambda t, tg=tg, half=half, hp=hp, bank=bank: t.matmul(
                            psW[bank][:],
                            attn[:, hp * 2048 + tg * 128: hp * 2048 + tg * 128 + 128],
                            wos[:, hp * 1024 + half * 512: hp * 1024 + half * 512 + 512],
                            start=(hp == 0), stop=(hp == 7)),
                            wait=([("lw", lw_o), ("vec", p.cnt["vec"])] if hp == 0 else None),
                            inc=("mm" if hp == 7 else None))
                    if debug and half == 0:
                        p.do("vector", lambda v, bank=bank: v.tensor_copy(
                            out=p.scr[:, 0:512], in_=psW[bank][:]), wait="mm", inc="vec")
                        p.do("sync", lambda sy, tg=tg: sy.dma_start(
                            out=pswo[tg], in_=p.scr[:, 0:512]), wait="vec", inc="st", incn=16)
                    p.do("vector", lambda v, tg=tg, half=half, bank=bank: v.tensor_tensor(
                        resid[:, tg * 1024 + half * 512: tg * 1024 + half * 512 + 512],
                        resid[:, tg * 1024 + half * 512: tg * 1024 + half * 512 + 512],
                        psW[bank][:], OP.add),
                        wait=["mm", ("st", p.cnt["st"])], inc="vec")
            if debug:
                p.do("sync", lambda sy: sy.dma_start(
                    out=x1o.rearrange("(g p) d -> p g d", p=128),
                    in_=resid[:].rearrange("p (g d) -> p g d", g=NG)),
                    wait=[("vec", p.cnt["vec"])], inc="st", incn=16)
                for cc in range(DC):
                    p.do("sync", lambda sy, cc=cc: sy.dma_start(
                        out=attno[cc * 128:(cc + 1) * 128, :], in_=attn[:, cc * TPC:(cc + 1) * TPC]),
                        wait=[("vec", p.cnt["vec"])], inc="st", incn=16)

    if stages < 3:
        p.do("sync", lambda sy: sy.nop(), wait=[("st", p.cnt["st"])])
        return p
    with ExitStack() as e3:
        xfmF = p.sb("xfm3", [128, DC * 512], BF16, e3)
        abuf = p.sb("abuf", [128, 32 * 512], BF16, e3)
        bbuf = p.sb("bbuf", [128, 32 * 512], BF16, e3)
        t1 = p.sb("t1", [128, 512], F32, e3)
        t2 = p.sb("t2", [128, 512], F32, e3)
        w3tmp = p.sb("w3tmp", [128, 512], BF16, e3)
        tf32 = p.sb("tf32", [128, 128], F32, e3)
        w12s = [p.sb(f"w12s{i}", [128, DC * 256], BF16, e3) for i in range(2)]
        whs = [p.sb(f"whs{i}", [128, 32 * 128], BF16, e3) for i in range(2)]
        scl1 = p.sb("scl1", [128, 32], F32, e3)
        scl2 = p.sb("scl2", [128, 32], F32, e3)
        sclh1 = p.sb("sclh1", [128, 32], F32, e3)
        sclh2 = p.sb("sclh2", [128, 32], F32, e3)
        scl3 = p.sb("scl3", [128, 8], F32, e3)
        psT3v = [p.ps(f"psT3{i}", [128, 128], BF16, e3) for i in range(2)]
        psA = [p.ps(f"psA{i}", [128, 512], F32, e3) for i in range(4)]

        for src, dst in ((s1, scl1), (s2, scl2), (sh1, sclh1), (sh2, sclh2), (s3, scl3)):
            p.do("sync", lambda sy, src=src, dst=dst: sy.dma_start(
                out=dst[:], in_=src[:, :]),
                wait=[("mm", p.cnt["mm"]), ("act", p.cnt["act"]), ("vec", p.cnt["vec"]), ("st", p.cnt["st"])], inc="ld", incn=16)

        def h_stage(inbuf, outbuf, wsrc_b, sclbuf, func, nout=32, nin=32):
            mm_entry = p.cnt["mm"]
            mm_after = []
            for oc in range(nout):
                hbuf = whs[oc % 2]
                qeng, qsem = (("gpsimd", "lw") if oc % 2 == 0 else ("sync", "ld"))
                mmw = mm_after[oc - 2] if oc >= 2 else mm_entry
                p.do(qeng, lambda g, oc=oc, hbuf=hbuf, wsrc_b=wsrc_b, nin=nin: g.dma_start(
                    out=hbuf[:, 0:nin * 128], in_=wsrc_b[oc]),
                    wait=[("mm", mmw), ("lw", lw_cast)], inc=qsem, incn=16)
                ldw = (qsem, p.cnt[qsem])
                bank = oc % 2
                for ic in range(nin):
                    p.do("tensor", lambda t, oc=oc, ic=ic, bank=bank, hbuf=hbuf, inbuf=inbuf: t.matmul(
                        psA[bank][:], hbuf[:, ic * 128:(ic + 1) * 128],
                        inbuf[:, ic * 512:(ic + 1) * 512],
                        start=(ic == 0), stop=(ic == nin - 1)),
                        wait=([ldw, ("act", p.cnt["act"]),
                               ("vec", p.cnt["vec"])] if ic == 0 else None),
                        inc=("mm" if ic == nin - 1 else None))
                mm_after.append(p.cnt["mm"])
                yield oc, bank

        for st in range(4):
            emit_ln_tile(p, psT3v,
                         [resid[:, (st * 4 + g) * 1024:(st * 4 + g + 1) * 1024] for g in range(4)],
                         xfmF)
            mm_e12 = p.cnt["mm"]
            mm_a12 = []
            for oc in range(32):
                buf = w12s[oc % 2]
                qeng, qsem = (("gpsimd", "lw") if oc % 2 == 0 else ("sync", "ld"))
                mmw = mm_a12[oc - 2] if oc >= 2 else mm_e12
                p.do(qeng, lambda g, oc=oc, buf=buf: g.dma_start(
                    out=buf[:], in_=w12b[oc]),
                    wait=[("mm", mmw), ("lw", lw_cast)], inc=qsem, incn=16)
                ldw = (qsem, p.cnt[qsem])
                for wi in range(2):
                    bank = 2 * (oc % 2) + wi
                    for dc in range(DC):
                        p.do("tensor", lambda t, oc=oc, wi=wi, dc=dc, bank=bank, buf=buf: t.matmul(
                            psA[bank][:],
                            buf[:].rearrange("p (w dc c) -> p w dc c", w=2, dc=DC)[:, wi, dc],
                            xfmF[:, dc * 512:(dc + 1) * 512],
                            start=(dc == 0), stop=(dc == DC - 1)),
                            wait=([ldw, ("vec", p.cnt["vec"]),
                                   ("act", p.cnt["act"])] if (dc == 0 and wi == 0) else None),
                            inc=("mm" if dc == DC - 1 else None))
                mm_a12.append(p.cnt["mm"])
                p.do("scalar", lambda s, oc=oc: s.activation(
                    out=t1[:], in_=psA[2 * (oc % 2)][:], func=AF.Silu,
                    scale=scl1[:, oc:oc + 1]), wait=["mm", ("vec", p.cnt["vec"])], inc="act")
                p.do("scalar", lambda s, oc=oc: s.activation(
                    out=t2[:], in_=psA[2 * (oc % 2) + 1][:], func=AF.Identity,
                    scale=scl2[:, oc:oc + 1]), inc="act")
                p.do("vector", lambda v, oc=oc: v.tensor_tensor(
                    abuf[:, oc * 512:(oc + 1) * 512], t1[:], t2[:], OP.mult),
                    wait="act", inc="vec")
            for oc, bank in h_stage(abuf, bbuf, h1b, sclh1, AF.Gelu):
                p.do("scalar", lambda s, oc=oc, bank=bank: s.activation(
                    out=bbuf[:, oc * 512:(oc + 1) * 512], in_=psA[bank][:],
                    func=AF.Gelu, scale=sclh1[:, oc:oc + 1]), wait="mm", inc="act")
            for oc, bank in h_stage(bbuf, abuf, h2b, sclh2, AF.Gelu):
                p.do("scalar", lambda s, oc=oc, bank=bank: s.activation(
                    out=abuf[:, oc * 512:(oc + 1) * 512], in_=psA[bank][:],
                    func=AF.Gelu, scale=sclh2[:, oc:oc + 1]), wait="mm", inc="act")
            for oc, bank in h_stage(abuf, None, w3b, scl3, None, nout=8):
                p.do("scalar", lambda s, oc=oc, bank=bank: s.activation(
                    out=w3tmp[:], in_=psA[bank][:], func=AF.Identity,
                    scale=scl3[:, oc:oc + 1]), wait=["mm", ("vec", p.cnt["vec"])], inc="act")
                for tsub in range(4):
                    tb = 2 + tsub % 2
                    p.do("tensor", lambda t, tsub=tsub, tb=tb: t.transpose(
                        psT3v[tb - 2][:], w3tmp[:, tsub * 128:(tsub + 1) * 128], p.it128[:]),
                        wait=[("act", p.cnt["act"]), ("vec", p.cnt["vec"])], inc="mm")
                    p.do("vector", lambda v, tb=tb: v.tensor_copy(out=tf32[:], in_=psT3v[tb - 2][:]),
                         wait="mm", inc="vec")
                    gg = st * 4 + tsub
                    p.do("vector", lambda v, gg=gg, oc=oc: v.tensor_tensor(
                        resid[:, gg * 1024 + oc * 128: gg * 1024 + oc * 128 + 128],
                        resid[:, gg * 1024 + oc * 128: gg * 1024 + oc * 128 + 128],
                        tf32[:], OP.add), inc="vec")

    if stages < 4:
        p.do("sync", lambda sy: sy.nop(), wait=[("st", p.cnt["st"])])
        return p
    with ExitStack() as e4:
        qf4 = p.sb("qf4", [128, DC * TPC], BF16, e4)
        xfmL = p.sb("xfm4", [128, DC * 512], BF16, e4)
        wq1s = p.sb("wq1s", [128, DC * 1024], BF16, e4)
        wk1s = p.sb("wk1s", [128, DC * 1024], BF16, e4)
        wv1s = p.sb("wv1s", [128, DC * 1024], BF16, e4)
        bqs = p.sb("bqs", [1, D], BF16, e4)
        bks = p.sb("bks", [1, D], BF16, e4)
        bvs = p.sb("bvs", [1, D], BF16, e4)
        kf = p.sb("kf", [128, 4 * 1024], BF16, e4)
        vf = p.sb("vf", [128, 4 * 16 * 65], BF16, e4)
        ta = p.sb("ta", [128, 512], F32, e4)
        tc = p.sb("tc", [128, 512], F32, e4)
        tb_ = p.sb("tb", [128, 512], F32, e4)
        kvsb = p.sb("kvsb", [128, 16 * 65], F32, e4)
        psT4v = [p.ps(f"psT4{i}", [128, 128], BF16, e4) for i in range(2)]
        psOL = [p.ps(f"psO4{i}", [128, 512], F32, e4) for i in range(3)]
        psKV = p.ps("psKV", [128, 1536], F32, e4)

        for w, t_ in ((wq1, wq1s), (wk1, wk1s), (wv1, wv1s)):
            p.do("gpsimd", lambda g, w=w, t_=t_: g.dma_start(
                out=t_[:].rearrange("p (dc c) -> p dc c", dc=DC),
                in_=w.rearrange("(dc p) c -> p dc c", p=128)),
                wait=[("mm", p.cnt["mm"]), ("act", p.cnt["act"]), ("vec", p.cnt["vec"]), ("st", p.cnt["st"])], inc="lw", incn=16)
        for bsrc, bdst in ((bq1, bqs), (bk1, bks), (bv1, bvs)):
            p.do("gpsimd", lambda g, bsrc=bsrc, bdst=bdst: g.dma_start(
                out=bdst[:], in_=bsrc[:, :]), inc="lw", incn=16)
        lw4 = p.cnt["lw"]

        kv_started = set()
        for st in range(4):
            emit_ln_tile(p, psT4v,
                         [resid[:, (st * 4 + g) * 1024:(st * 4 + g + 1) * 1024] for g in range(4)],
                         xfmL)
            p.do("vector", lambda v: v.memset(
                vf[:].rearrange("p (ts h m) -> p ts h m", ts=4, m=65)[:, :, :, 64:65], 1.0),
                wait=[("mm", p.cnt["mm"])], inc="vec")
            for cc in range(DC):
                bank = 0
                for dc in range(DC):
                    p.do("tensor", lambda t, cc=cc, dc=dc, bank=bank: t.matmul(
                        psOL[bank][:], wq1s[:, dc * 1024 + cc * 128: dc * 1024 + cc * 128 + 128],
                        xfmL[:, dc * 512:(dc + 1) * 512],
                        start=(dc == 0), stop=False),
                        wait=([("lw", lw4), ("vec", p.cnt["vec"]),
                               ("act", p.cnt["act"])] if dc == 0 else None))
                p.do("tensor", lambda t, cc=cc, bank=bank: t.matmul(
                    psOL[bank][:], bqs[0:1, cc * 128:(cc + 1) * 128], onesbf[0:1, :],
                    start=False, stop=True), inc="mm")
                p.do("vector", lambda v, bank=bank: v.tensor_scalar_min(ta[:], psOL[bank][:], 0.0),
                     wait=["mm", ("act", p.cnt["act"])], inc="vec")
                p.do("scalar", lambda s: s.activation(out=tc[:], in_=ta[:], func=AF.Exp),
                     wait="vec", inc="act")
                p.do("vector", lambda v, bank=bank: v.tensor_scalar_max(tb_[:], psOL[bank][:], 0.0),
                     inc="vec")
                p.do("vector", lambda v, cc=cc, st=st: v.tensor_tensor(
                    qf4[:, cc * TPC + st * 512: cc * TPC + st * 512 + 512], tc[:], tb_[:], OP.add),
                    wait=["act", ("vec", p.cnt["vec"])], inc="vec")
            for ts in range(4):
                for half in range(2):
                    for bank, wbuf, brow in ((1, wk1s, bks), (2, wv1s, bvs)):
                        for dc in range(DC):
                            p.do("tensor", lambda t, ts=ts, half=half, dc=dc, bank=bank, wbuf=wbuf:
                                 t.matmul(
                                     psOL[bank][:],
                                     xfmL[:, dc * 512 + ts * 128: dc * 512 + ts * 128 + 128],
                                     wbuf[:, dc * 1024 + half * 512: dc * 1024 + half * 512 + 512],
                                     start=(dc == 0), stop=False),
                                 wait=([("vec", p.cnt["vec"]),
                                        ("act", p.cnt["act"])] if dc == 0 else None))
                        p.do("tensor", lambda t, ts=ts, half=half, bank=bank, brow=brow: t.matmul(
                            psOL[bank][:], onesbf[0:1, ts * 128: ts * 128 + 128],
                            brow[0:1, half * 512:(half + 1) * 512],
                            start=False, stop=True), inc="mm")
                    p.do("vector", lambda v: v.tensor_scalar_min(ta[:], psOL[1][:], 0.0),
                         wait=["mm", ("act", p.cnt["act"])], inc="vec")
                    p.do("scalar", lambda s: s.activation(out=tc[:], in_=ta[:], func=AF.Exp),
                         wait="vec", inc="act")
                    p.do("vector", lambda v: v.tensor_scalar_max(tb_[:], psOL[1][:], 0.0), inc="vec")
                    p.do("vector", lambda v, ts=ts, half=half: v.tensor_tensor(
                        kf[:, ts * 1024 + half * 512: ts * 1024 + half * 512 + 512],
                        tc[:], tb_[:], OP.add), wait=["act", ("vec", p.cnt["vec"])], inc="vec")
                    p.do("vector", lambda v, ts=ts, half=half: v.tensor_copy(
                        out=vf[:].rearrange("p (ts h m) -> p ts h m", ts=4, m=65)
                        [:, ts, half * 8:(half + 1) * 8, 0:64],
                        in_=psOL[2][:].rearrange("p (h m) -> p h m", m=64)), inc="vec")
            for h in range(H):
                for ts in range(4):
                    bank_first = (st == 0 and ts == 0 and h in (0, 7, 14))
                    last = (st == 3 and ts == 3)
                    kc = (h // 7) * 512 + (h % 7) * 65
                    p.do("tensor", lambda t, h=h, ts=ts, bf=bank_first, last=last, kc=kc:
                         t.matmul(psKV[0:64, kc:kc + 65],
                                  kf[:, ts * 1024 + h * 64: ts * 1024 + h * 64 + 64],
                                  vf[:, ts * 1040 + h * 65: ts * 1040 + h * 65 + 65],
                                  start=bf, stop=last,
                                  skip_group_check=True),
                         wait=([("vec", p.cnt["vec"])] if (ts == 0 and h == 0) else None),
                         inc=("mm" if (h == H - 1 and ts == 3) else None))
        for h in range(H):
            kc = (h // 7) * 512 + (h % 7) * 65
            p.do("vector", lambda v, h=h, kc=kc: v.tensor_copy(
                out=kvsb[0:64, h * 65: h * 65 + 65],
                in_=psKV[0:64, kc:kc + 65]),
                wait="mm", inc="vec")
            p.do("sync", lambda sy, h=h: sy.dma_start(
                out=kvo[h], in_=kvsb[0:64, h * 65: h * 65 + 65]),
                wait="vec", inc="st", incn=16)
        for cc in range(DC):
            p.do("sync", lambda sy, cc=cc: sy.dma_start(
                out=qfo[cc * 128:(cc + 1) * 128, :], in_=qf4[:, cc * TPC:(cc + 1) * TPC]),
                wait=[("vec", p.cnt["vec"])], inc="st", incn=16)
        p.do("sync", lambda sy: sy.dma_start(
            out=x2o.rearrange("(g p) d -> p g d", p=128),
            in_=resid[:].rearrange("p (g d) -> p g d", g=NG)),
            wait=[("vec", p.cnt["vec"])], inc="st", incn=16)
        if debug:
            p.do("sync", lambda sy: sy.dma_start(out=ta[:].bitcast(BF16)[:, 0:1024],
                                                 in_=kfull[0:128, 0:1024]),
                 wait=[("st", p.cnt["st"])], inc="ld", incn=16)
            p.do("sync", lambda sy: sy.dma_start(out=kchk[:, :], in_=ta[:].bitcast(BF16)[:, 0:1024]),
                 wait=[("ld", p.cnt["ld"])], inc="st", incn=16)
        p.do("sync", lambda sy: sy.nop(), wait=[("st", p.cnt["st"])])

    return p



def build_launch_B():
    p = Prog("B")
    x2i = p.din("x2i", [TPC, D])
    qfi = p.din("qfi", [D, TPC], BF16)
    kvp = p.din("kvp", [4, H, HD, 65])
    wo1 = p.din("wo1", [D, D])
    bo1 = p.din("bo1", [1, D])
    rt = p.din("rt", [D, 64])
    ident = p.din("ident", [128, 128])
    x3o = p.dout("x3o", [TPC, D])
    xl4o = p.dout("xl4o", [TPC, D], BF16)
    idxo = p.dout("idxo", [TPC, 8])
    wno = p.dout("wno", [TPC, 8])
    toto = p.dout("toto", [1, 64])

    resid = p.sb("residB", [128, NG * 1024], F32)
    p.it128 = p.sb("it128B", [128, 128], BF16)
    onesbfB = p.sb("onesbfB", [128, 512], BF16)
    p.scr = p.sb("scrB", [128, 1024], F32)
    p.mu = p.sb("muB", [128, 1]); p.sq = p.sb("sqB", [128, 1])
    p.var = p.sb("varB", [128, 1]); p.rstd = p.sb("rstdB", [128, 1])
    p.musum = p.sb("musumB", [128, 1]); p.musq = p.sb("musqB", [128, 1])
    p.sd = p.sb("sdB", [128, 1])
    p.xlg = p.sb("xlgB", [128, 1024], BF16)
    p.epsb = p.sb("epsbB", [128, 1], F32)
    hif = p.sb("hifB", [128, 1024], F32)
    xlf = p.sb("xlfB", [128, 1024], F32)
    iot64 = p.sb("iot64B", [128, 64], F32)
    iomb = p.sb("iombB", [128, 64], F32)
    probs = p.sb("probsB", [128, 64], F32)
    pn = p.sb("pnB", [128, 64], F32)
    nmax = p.sb("nmaxB", [128, 1], F32)
    rsum = p.sb("rsumB", [128, 1], F32)
    ssum = p.sb("ssumB", [128, 1], F32)
    wv8 = p.sb("wv8B", [128, 8], F32)
    iv8 = p.sb("iv8B", [128, 8], F32)
    t64 = p.sb("t64B", [128, 64], F32)
    obf = p.sb("obfB", [128, 64], BF16)
    onesf1 = p.sb("onesf1B", [128, 1], BF16)
    eB1 = ExitStack()
    qfB = p.sb("qfB", [128, DC * TPC], BF16, eB1)
    kvl = [p.sb(f"kvl{i}", [128, 8 * 65], F32, eB1) for i in range(2)]
    kvs = p.sb("kvs", [128, 8 * 65], F32, eB1)
    kvsbf = p.sb("kvsbf", [128, 8 * 65], BF16, eB1)
    ytok = p.sb("ytok", [128, 1024], BF16, eB1)
    zcol = p.sb("zcolB", [128, 1], F32, eB1)
    yfm = p.sb("yfmB", [128, DC * TPC], BF16, eB1)
    wo1s = p.sb("wo1s", [128, DC * 1024], BF16, eB1)
    bo1s = p.sb("bo1s", [1, D], BF16, eB1)
    psT = [p.ps(f"psTB{i}", [128, 128], BF16) for i in range(2)]
    psY = [p.ps(f"psYB{i}", [128, 512], F32) for i in range(2)]
    psW = [p.ps(f"psWB{i}", [128, 512], F32) for i in range(2)]
    psL = p.ps("psLB", [128, 64], F32)
    psTT = p.ps("psTTB", [128, 64], F32)

    p.do("gpsimd", lambda g: g.dma_start(out=p.it128[:], in_=ident[:, :]), inc="ld", incn=16)
    p.do("gpsimd", lambda g: g.dma_start(
        out=wo1s[:].rearrange("p (dc c) -> p dc c", dc=DC),
        in_=wo1.rearrange("(dc p) c -> p dc c", p=128)), inc="lw", incn=16)
    p.do("gpsimd", lambda g: g.dma_start(out=bo1s[:], in_=bo1[:, :]), inc="lw", incn=16)
    p.do("gpsimd", lambda g: g.dma_start(out=qfB[:].rearrange("p (dc t) -> p dc t", dc=DC),
                                         in_=qfi.rearrange("(dc p) t -> p dc t", p=128)),
         inc="lw", incn=16)
    for g2 in range(NG):
        p.do("sync", lambda sy, g2=g2: sy.dma_start(
            out=resid[:, g2 * 1024:(g2 + 1) * 1024], in_=x2i[g2 * 128:(g2 + 1) * 128, :]),
            inc="ld", incn=16)
    p.do("vector", lambda v: v.memset(onesbfB[:], 1.0), inc="vec")
    p.do("vector", lambda v: v.memset(onesf1[:], 1.0), inc="vec")
    p.do("vector", lambda v: v.memset(p.epsb[:], EPS_LN), inc="vec")
    p.do("gpsimd", lambda g: g.iota(iot64[:], pattern=[[1, 64]], base=0, channel_multiplier=0,
                                    allow_small_or_imprecise_dtypes=True), inc="gp")
    p.do("vector", lambda v: v.tensor_scalar_add(iomb[:], iot64[:], -BIG),
         wait=[("gp", p.cnt["gp"])], inc="vec")
    for gi in range(4):
        buf = kvl[gi % 2]
        p.do("sync", lambda sy, gi=gi, buf=buf: sy.dma_start(
            out=buf[:, :].rearrange("p (hp c) -> p hp c", hp=8),
            in_=kvp[gi].rearrange("(hp i) d c -> (i d) hp c", i=2)),
            wait=[("vec", p.cnt["vec"])], inc="ld", incn=16)
        if gi == 0:
            p.dov(lambda v, buf=buf: v.tensor_copy(out=kvs[:], in_=buf[:]), wait="ld")
        else:
            p.dov(lambda v, buf=buf: v.tensor_tensor(kvs[:], kvs[:], buf[:], OP.add), wait="ld")
    p.do("vector", lambda v: v.tensor_copy(out=kvsbf[:], in_=kvs[:]), wait="vec", inc="vec")
    for tg in range(NG):
        for hp in range(DC):
            for i in range(2):
                hb = i * 64
                p.do("tensor", lambda t, tg=tg, hp=hp, i=i, hb=hb: t.matmul(
                    psY[i][:, 0:65],
                    qfB[hb:hb + 64, hp * TPC + tg * 128: hp * TPC + tg * 128 + 128],
                    kvsbf[hb:hb + 64, hp * 65: hp * 65 + 65],
                    start=True, stop=True),
                    wait=([("lw", p.cnt["lw"]), ("vec", p.cnt["vec"])] if i == 0 else None),
                    inc=("mm" if i == 1 else None))
            for i in range(2):
                p.do("vector", lambda v, i=i: v.reciprocal(zcol[:], psY[i][:, 64:65]),
                     wait=(["mm"] if i == 0 else [("vec", p.cnt["vec"])]), inc="vec")
                p.do("vector", lambda v, i=i, hp=hp: v.tensor_scalar(
                    ytok[:, (2 * hp + i) * 64:(2 * hp + i) * 64 + 64],
                    psY[i][:, 0:64], zcol[:, 0:1], None, OP.mult),
                    wait="vec", inc="vec")
        emit_transpose(p, psT, [
            (ytok[:, hp * 128:(hp + 1) * 128],
             yfm[:, hp * TPC + tg * 128: hp * TPC + tg * 128 + 128]) for hp in range(DC)])
    for tg in range(NG):
        for half in range(2):
            bank = (tg * 2 + half) % 2
            for hp in range(DC):
                p.do("tensor", lambda t, tg=tg, half=half, hp=hp, bank=bank: t.matmul(
                    psW[bank][:], yfm[:, hp * TPC + tg * 128: hp * TPC + tg * 128 + 128],
                    wo1s[:, hp * 1024 + half * 512: hp * 1024 + half * 512 + 512],
                    start=(hp == 0), stop=False),
                    wait=([("vec", p.cnt["vec"])] if hp == 0 else None))
            p.do("tensor", lambda t, tg=tg, half=half, bank=bank: t.matmul(
                psW[bank][:], onesbfB[0:1, 0:128],
                bo1s[0:1, half * 512:(half + 1) * 512],
                start=False, stop=True), inc="mm")
            p.do("vector", lambda v, tg=tg, half=half, bank=bank: v.tensor_tensor(
                resid[:, tg * 1024 + half * 512: tg * 1024 + half * 512 + 512],
                resid[:, tg * 1024 + half * 512: tg * 1024 + half * 512 + 512],
                psW[bank][:], OP.add), wait="mm", inc="vec")
    eB1.close()
    xhi = p.sb("xhiB", [128, DC * TPC], BF16)
    xlo = p.sb("xloB", [128, DC * TPC], BF16)
    rts = p.sb("rtsB", [128, 8 * 64], F32)
    rthi = p.sb("rthiB", [128, 8 * 64], BF16)
    rtlo = p.sb("rtloB", [128, 8 * 64], BF16)
    p.do("sync", lambda sy: sy.dma_start(out=rts[:].rearrange("p (dc c) -> p dc c", dc=DC),
                                         in_=rt.rearrange("(dc p) c -> p dc c", p=128)),
         wait=[("mm", p.cnt["mm"]), ("vec", p.cnt["vec"]), ("act", p.cnt["act"]),
               ("st", p.cnt["st"])], inc="ld", incn=16)
    p.do("vector", lambda v: v.tensor_copy(out=rthi[:], in_=rts[:]), wait="ld", inc="vec")
    p.do("vector", lambda v: v.tensor_tensor(rtlo[:], rts[:], rthi[:], OP.subtract),
         wait="vec", inc="vec")
    for g2 in range(NG):
        emit_ln(p, resid[:, g2 * 1024:(g2 + 1) * 1024], p.xlg[:],
                w0=[("vec", p.cnt["vec"]), ("mm", p.cnt["mm"]), ("st", p.cnt["st"])],
                out_f32=xlf[:])
        p.do("sync", lambda sy, g2=g2: sy.dma_start(
            out=xl4o[g2 * 128:(g2 + 1) * 128, :], in_=p.xlg[:]), wait="vec", inc="st", incn=16)
        p.dov(lambda v: v.tensor_copy(out=hif[:], in_=p.xlg[:]))
        p.do("vector", lambda v: v.tensor_tensor(p.scr[:], xlf[:], hif[:], OP.subtract),
             wait="vec", inc="vec")
        emit_transpose(p, psT, [
            (p.xlg[:, dc * 128:(dc + 1) * 128],
             xhi[:, dc * TPC + g2 * 128: dc * TPC + g2 * 128 + 128]) for dc in range(DC)])
        p.do("vector", lambda v: v.tensor_copy(out=p.xlg[:], in_=p.scr[:]),
             wait=[("mm", p.cnt["mm"]), ("st", p.cnt["st"])], inc="vec")
        emit_transpose(p, psT, [
            (p.xlg[:, dc * 128:(dc + 1) * 128],
             xlo[:, dc * TPC + g2 * 128: dc * TPC + g2 * 128 + 128]) for dc in range(DC)])
    for tg in range(NG):
        combos = [(xhi, rthi), (xlo, rthi), (xhi, rtlo), (xlo, rtlo)]
        nmm = 0
        for xb_, rb_ in combos:
            for dc in range(DC):
                nmm += 1
                p.do("tensor", lambda t, tg=tg, dc=dc, xb_=xb_, rb_=rb_, nmm=nmm: t.matmul(
                    psL[:], xb_[:, dc * TPC + tg * 128: dc * TPC + tg * 128 + 128],
                    rb_[:, dc * 64:(dc + 1) * 64],
                    start=(nmm == 1), stop=(nmm == 32)),
                    wait=([("act", p.cnt["act"]), ("vec", p.cnt["vec"])] if nmm == 1 else None),
                    inc=("mm" if nmm == 32 else None))
        p.do("vector", lambda v: v.reduce_max(nmax[:], psL[:], axis=AX.X, negate=True),
             wait="mm", inc="vec")
        p.do("scalar", lambda s: s.activation(out=probs[:], in_=psL[:], func=AF.Exp,
                                              bias=nmax[:, 0:1], accum_out=ssum[:]),
             wait="vec", inc="act")
        p.do("vector", lambda v: v.reciprocal(rsum[:], ssum[:]), wait="act", inc="vec")
        p.do("vector", lambda v: v.tensor_scalar(pn[:], probs[:], rsum[:, 0:1], None, OP.mult),
             wait="vec", inc="vec")
        for k in range(8):
            p.do("vector", lambda v, k=k: v.reduce_max(wv8[:, k:k + 1], pn[:], axis=AX.X),
                 wait=(["vec", ("st", p.cnt["st"])] if k == 0 else ["vec"]), inc="vec")
            p.dov(lambda v, k=k: v.tensor_scalar(t64[:], pn[:], wv8[:, k:k + 1],
                                                 None, OP.is_equal))
            p.dov(lambda v: v.tensor_mul(t64[:], t64[:], iomb[:]))
            p.dov(lambda v: v.tensor_scalar_add(t64[:], t64[:], BIG))
            p.dov(lambda v, k=k: v.tensor_reduce(iv8[:, k:k + 1], t64[:], axis=AX.X,
                                                 op=OP.min))
            p.dov(lambda v, k=k: v.tensor_scalar(t64[:], iot64[:], iv8[:, k:k + 1],
                                                 None, OP.is_equal))
            p.dov(lambda v: v.scalar_tensor_tensor(pn[:], t64[:], -BIG, pn[:],
                                                   OP.mult, OP.add))
            p.dov(lambda v: v.tensor_copy(out=obf[:], in_=t64[:]))
            p.do("tensor", lambda t, tg=tg, k=k: t.matmul(
                psTT[0:1, :], onesf1[:, 0:1], obf[:],
                start=(tg == 0 and k == 0), stop=(tg == NG - 1 and k == 7),
                skip_group_check=True),
                wait=[("vec", p.cnt["vec"])], inc="mm")
        p.do("vector", lambda v: v.reduce_sum(p.musum[:], wv8[:], axis=AX.X),
             wait=[("vec", p.cnt["vec"])], inc="vec")
        p.do("vector", lambda v: v.reciprocal(p.sd[:], p.musum[:]), wait="vec", inc="vec")
        p.do("vector", lambda v: v.tensor_scalar(t64[:, 0:8], wv8[:], p.sd[:, 0:1], None,
                                                 OP.mult), wait="vec", inc="vec")
        p.do("sync", lambda sy, tg=tg: sy.dma_start(
            out=idxo[tg * 128:(tg + 1) * 128, :], in_=iv8[:]), wait="vec", inc="st", incn=16)
        p.do("sync", lambda sy, tg=tg: sy.dma_start(
            out=wno[tg * 128:(tg + 1) * 128, :], in_=t64[:, 0:8]), wait="vec", inc="st", incn=16)
    p.do("vector", lambda v: v.tensor_copy(out=p.scr[0:1, 0:64], in_=psTT[0:1, :]),
         wait="mm", inc="vec")
    p.do("sync", lambda sy: sy.dma_start(out=toto[:, :], in_=p.scr[0:1, 0:64]),
         wait="vec", inc="st", incn=16)
    p.do("sync", lambda sy: sy.dma_start(
        out=x3o.rearrange("(g p) d -> p g d", p=128),
        in_=resid[:].rearrange("p (g d) -> p g d", g=NG)),
        wait=[("vec", p.cnt["vec"])], inc="st", incn=16)
    p.do("sync", lambda sy: sy.nop(), wait=[("st", p.cnt["st"])])
    return p



def build_launch_B2(debug=False, scat=True):
    p = Prog("B2")
    idxi = p.din("idxi", [TPC, 8])
    wni = p.din("wni", [TPC, 8])
    tota = p.din("tota", [8, 64])
    premask = p.din("premask", [8, 1])
    gtok = p.din("gtok", [TPC, 1])
    uts = p.din("uts", [128, 128])
    plisto = p.dout("plisto", [E_EXP * CAP + 128, 1])
    goffo = p.dout("goffo", [TPC, 8])
    wvo = p.dout("wvo", [TPC, 8])
    zpl = p.sb("zpl2", [128, 257], F32)
    posmo = p.dout("posmo", [TPC, 8]) if debug else None
    mfo = p.dout("mfo", [TPC, 64]) if debug else None
    crowo = p.dout("crowo", [NG, 64]) if debug else None
    basebo = p.dout("basebo", [128, 64]) if debug else None
    sallo = p.dout("sallo", [TPC, 64]) if debug else None
    idxto = p.dout("idxto", [TPC, 8]) if debug else None

    totas = p.sb("totas2", [8, 64], F32)
    pres = p.sb("pres2", [8, 1], F32)
    utsb = p.sb("utsb2", [128, 128], BF16)
    onesb2 = p.sb("onesb2", [128, 1], BF16)
    onesf2 = p.sb("onesf2", [128, 128], F32)
    baseb = p.sb("baseb2", [128, 64], F32)
    brow = p.sb("brow2", [1, 64], F32)
    crow = p.sb("crow2", [1, 64], F32)
    iot64 = p.sb("iot642", [128, 64], F32)
    idxt = p.sb("idxt2", [128, 8], F32)
    wnt = p.sb("wnt2", [128, 8], F32)
    gtt = p.sb("gtt2", [128, 1], F32)
    Ok = [p.sb(f"Ok2_{k}", [128, 64], F32) for k in range(8)]
    Sall = p.sb("Sall2", [128, 64], F32)
    Sbf = p.sb("Sbf2", [128, 64], BF16)
    M1 = p.sb("M12", [128, 64], F32)
    Mfull = p.sb("Mfull2", [128, 64], F32)
    posm = p.sb("posm2", [128, 8], F32)
    valid = p.sb("valid2", [128, 8], F32)
    goff = p.sb("goff2", [128, 8], F32)
    gint = p.sb("gint2", [128, 8], I32)
    t8 = p.sb("t82", [128, 8], F32)
    psP = p.ps("psP2", [128, 64], F32)
    psC = p.ps("psC2", [128, 64], F32)
    psB = p.ps("psB2", [128, 64], F32)
    psBC = p.ps("psBC2", [128, 64], F32)

    p.dov(lambda v: v.memset(zpl[:], 0.0))
    NZ = (E_EXP * CAP + 128) // 128
    for j in range(NZ):
        p.do("sync", lambda sy, j=j: sy.dma_start(out=plisto[j * 128:(j + 1) * 128, :],
                                                  in_=zpl[:, j:j + 1]),
             wait=(["vec"] if j == 0 else None), inc="st", incn=16)
    p.do("sync", lambda sy: sy.dma_start(out=totas[:], in_=tota[:, :]), inc="ld", incn=16)
    p.do("sync", lambda sy: sy.dma_start(out=pres[:], in_=premask[:, :]), inc="ld", incn=16)
    p.do("gpsimd", lambda g: g.dma_start(out=utsb[:], in_=uts[:, :]), inc="lw", incn=16)
    p.dov(lambda v: v.memset(onesb2[:], 1.0))
    p.dov(lambda v: v.memset(onesf2[:], 1.0))
    p.do("gpsimd", lambda g: g.iota(iot64[:], pattern=[[1, 64]], base=0, channel_multiplier=0,
                                    allow_small_or_imprecise_dtypes=True), inc="gp")
    p.do("tensor", lambda t: t.matmul(psB[0:1, :], pres[0:8, 0:1], totas[0:8, :],
                                      start=True, stop=True), wait=["ld"], inc="mm")
    p.dov(lambda v: v.tensor_copy(out=brow[:], in_=psB[0:1, :]), wait="mm")
    p.do("tensor", lambda t: t.matmul(psBC[:], onesf2[0:1, 0:128], brow[0:1, :],
                                      start=True, stop=True), wait="vec", inc="mm")
    p.dov(lambda v: v.tensor_copy(out=baseb[:], in_=psBC[:]), wait="mm")
    if debug:
        p.do("sync", lambda sy: sy.dma_start(out=basebo[:, :], in_=baseb[:]),
             wait="vec", inc="st", incn=16)

    for tg in range(NG):
        p.do("sync", lambda sy, tg=tg: sy.dma_start(out=idxt[:], in_=idxi[tg * 128:(tg + 1) * 128, :]),
             wait=[("vec", p.cnt["vec"]), ("st", p.cnt["st"]), ("gp", p.cnt["gp"])], inc="ld", incn=16)
        p.do("sync", lambda sy, tg=tg: sy.dma_start(out=wnt[:], in_=wni[tg * 128:(tg + 1) * 128, :]),
             inc="ld", incn=16)
        p.do("sync", lambda sy, tg=tg: sy.dma_start(out=gtt[:], in_=gtok[tg * 128:(tg + 1) * 128, :]),
             inc="ld", incn=16)
        for k in range(8):
            p.dov(lambda v, k=k: v.tensor_scalar(Ok[k][:], iot64[:], idxt[:, k:k + 1],
                                                          None, OP.is_equal),
                 wait=[("ld", p.cnt["ld"])])
        p.dov(lambda v: v.tensor_copy(out=Sall[:], in_=Ok[0][:]))
        for k in range(1, 8):
            p.dov(lambda v, k=k: v.tensor_add(Sall[:], Sall[:], Ok[k][:]))
        p.dov(lambda v: v.tensor_copy(out=Sbf[:], in_=Sall[:]), wait="vec")
        if debug:
            p.do("sync", lambda sy, tg=tg: sy.dma_start(out=sallo[tg * 128:(tg + 1) * 128, :],
                                                        in_=Sall[:]), wait="vec", inc="st", incn=16)
            p.do("sync", lambda sy, tg=tg: sy.dma_start(out=idxto[tg * 128:(tg + 1) * 128, :],
                                                        in_=idxt[:]), inc="st", incn=16)
        if tg == 0:
            p.dov(lambda v: v.memset(crow[:], 0.0))
        else:
            p.dov(lambda v: v.tensor_copy(out=crow[:], in_=psC[0:1, :]),
                 wait=[("mm", p.cnt["mm"])])
        p.do("tensor", lambda t: t.matmul(psP[:], utsb[:], Sbf[:], start=True, stop=True),
             wait=[("vec", p.cnt["vec"]), ("lw", p.cnt["lw"])], inc="mm")
        p.do("tensor", lambda t, tg=tg: t.matmul(psC[0:1, :], onesb2[:, 0:1], Sbf[:],
                                                 start=(tg == 0), stop=(tg == NG - 1),
                                                 skip_group_check=True),
             wait=[("vec", p.cnt["vec"])], inc="mm")
        p.do("tensor", lambda t: t.matmul(psBC[:], onesf2[0:1, 0:128], crow[0:1, :],
                                          start=True, stop=True),
             wait=[("vec", p.cnt["vec"])], inc="mm")
        p.dov(lambda v: v.tensor_copy(out=M1[:], in_=psP[:]), wait="mm")
        p.dov(lambda v: v.tensor_tensor(M1[:], M1[:], psBC[:], OP.add))
        p.dov(lambda v: v.tensor_tensor(Mfull[:], M1[:], baseb[:], OP.add))
        if debug:
            p.do("sync", lambda sy, tg=tg: sy.dma_start(out=crowo[tg:tg + 1, :], in_=crow[:]),
                 wait="vec", inc="st", incn=16)
            p.do("sync", lambda sy, tg=tg: sy.dma_start(out=mfo[tg * 128:(tg + 1) * 128, :],
                                                        in_=Mfull[:]), inc="st", incn=16)
        for k in range(8):
            p.dov(lambda v, k=k: v.tensor_mul(Ok[k][:], Ok[k][:], Mfull[:]))
            p.dov(lambda v, k=k: v.reduce_sum(posm[:, k:k + 1], Ok[k][:], axis=AX.X))
        if debug:
            p.do("sync", lambda sy, tg=tg: sy.dma_start(out=posmo[tg * 128:(tg + 1) * 128, :],
                                                        in_=posm[:]), wait="vec", inc="st", incn=16)
        p.dov(lambda v: v.tensor_scalar(valid[:], posm[:], float(CAP), None, OP.is_lt),
             wait="vec")
        p.dov(lambda v: v.tensor_tensor(t8[:], wnt[:], valid[:], OP.mult))
        p.do("sync", lambda sy, tg=tg: sy.dma_start(out=wvo[tg * 128:(tg + 1) * 128, :], in_=t8[:]),
             wait="vec", inc="st", incn=16)
        p.dov(lambda v: v.tensor_scalar(goff[:], idxt[:], float(CAP), None, OP.mult))
        p.dov(lambda v: v.tensor_add(goff[:], goff[:], posm[:]))
        p.dov(lambda v: v.tensor_mul(t8[:], goff[:], valid[:]),
             wait=[("st", p.cnt["st"])])
        p.do("sync", lambda sy, tg=tg: sy.dma_start(out=goffo[tg * 128:(tg + 1) * 128, :],
                                                    in_=t8[:]), wait="vec", inc="st", incn=16)
        TRASH = float(E_EXP * CAP)
        p.dov(lambda v: v.tensor_scalar_add(goff[:], goff[:], -TRASH))
        p.dov(lambda v: v.tensor_mul(goff[:], goff[:], valid[:]))
        p.dov(lambda v: v.tensor_scalar_add(goff[:], goff[:], TRASH))
        p.dov(lambda v: v.tensor_scalar(goff[:], goff[:], TRASH, None, OP.min))
        p.dov(lambda v: v.tensor_scalar(goff[:], goff[:], 0.0, None, OP.max))
        p.dov(lambda v: v.tensor_copy(out=gint[:], in_=goff[:]))
        for k in range(8):
            if not scat:
                break
            p.do("gpsimd", lambda g, k=k: g.indirect_dma_start(
                plisto[0:128, :], IndirectOffsetOnAxis(gint[:, k:k + 1], 0),
                gtt[:], None),
                wait=[("vec", p.cnt["vec"]), ("st", NZ * 16)], inc="st", incn=16)
    p.do("sync", lambda sy: sy.nop(), wait=[("st", p.cnt["st"])])
    return p


def build_launch_C():
    p = Prog("C")
    plists = p.din("plists", [8, 128, 32])
    xall = p.din("xall", [B * S, D], BF16)
    wgs_in = p.din("wgi", [EPPC, D, FEXP])
    wus_in = p.din("wui", [EPPC, D, FEXP])
    wds_in = p.din("wdi", [EPPC, FEXP, D])
    ident = p.din("identC", [128, 128])
    yo = p.dout("yo", [EPPC * CAP, D], BF16)

    p.it128 = p.sb("it128C", [128, 128], BF16)
    pls = [p.sb(f"pls{i}", [128, 32], F32) for i in range(2)]
    fl = p.sb("flC", [128, 32], F32)
    t32 = p.sb("t32C", [128, 32], F32)
    gint = p.sb("gintC", [128, 32], I32)
    xbg = [p.sb(f"xbg{i}", [128, 1024], BF16) for i in range(4)]
    xbfm = p.sb("xbfmC", [128, DC * 512], BF16)
    wgs = [p.sb(f"wgsC{i}", [128, DC * FEXP], BF16) for i in range(2)]
    wus = [p.sb(f"wusC{i}", [128, DC * FEXP], BF16) for i in range(2)]
    wds = [p.sb(f"wdsC{i}", [128, 6 * D], BF16) for i in range(2)]
    t1 = p.sb("t1C", [128, 512], F32)
    t2 = p.sb("t2C", [128, 512], F32)
    hfm = p.sb("hfmC", [128, 6 * 512], BF16)
    yev = p.sb("yevC", [128, 512], BF16)
    psT = [p.ps(f"psTC{i}", [128, 128], BF16) for i in range(2)]
    psG = [p.ps(f"psGC{i}", [128, 512], F32) for i in range(4)]
    psD = [p.ps(f"psDC{i}", [128, 512], F32) for i in range(2)]

    p.do("gpsimd", lambda g: g.dma_start(out=p.it128[:], in_=ident[:, :]), inc="ld", incn=16)
    for s_ in range(8):
        buf = pls[s_ % 2]
        p.do("sync", lambda sy, s_=s_, buf=buf: sy.dma_start(out=buf[:], in_=plists[s_]),
             wait=[("vec", p.cnt["vec"])], inc="ld", incn=16)
        if s_ == 0:
            p.dov(lambda v, buf=buf: v.tensor_copy(out=fl[:], in_=buf[:]), wait="ld")
        else:
            p.dov(lambda v, buf=buf: v.tensor_add(fl[:], fl[:], buf[:]), wait="ld")
    p.dov(lambda v: v.tensor_scalar_add(t32[:], fl[:], -1.0))
    p.dov(lambda v: v.tensor_scalar_max(t32[:], t32[:], 0.0))
    p.dov(lambda v: v.tensor_copy(out=gint[:], in_=t32[:]))

    mm_entryC = p.cnt["mm"]
    mm_afterC = []
    for e in range(EPPC):
        wg, wu, wd = wgs[e % 2], wus[e % 2], wds[e % 2]
        mmwC = mm_afterC[e - 2] if e >= 2 else mm_entryC
        p.do("gpsimd", lambda g, e=e, wg=wg: g.dma_start(
            out=wg[:].rearrange("p (dc f) -> p dc f", dc=DC),
            in_=wgs_in[e].rearrange("(dc p) f -> p dc f", p=128)),
            wait=[("mm", mmwC)], inc="lw", incn=16)
        p.do("gpsimd", lambda g, e=e, wu=wu: g.dma_start(
            out=wu[:].rearrange("p (dc f) -> p dc f", dc=DC),
            in_=wus_in[e].rearrange("(dc p) f -> p dc f", p=128)), inc="lw", incn=16)
        p.do("gpsimd", lambda g, e=e, wd=wd: g.dma_start(
            out=wd[:].rearrange("p (fc c) -> p fc c", fc=6),
            in_=wds_in[e].rearrange("(fc p) c -> p fc c", p=128)), inc="lw", incn=16)
        lwc = p.cnt["lw"]
        for j4 in range(4):
            jj = e * 4 + j4
            p.do("gpsimd", lambda g, jj=jj, j4=j4: g.indirect_dma_start(
                xbg[j4][:], None, xall[0:128, :], IndirectOffsetOnAxis(gint[:, jj:jj + 1], 0)),
                wait=[("mm", p.cnt["mm"]), ("vec", p.cnt["vec"])], inc="ld", incn=16)
        p.do("vector", lambda v: v.nop(), wait=[("ld", p.cnt["ld"])], inc="vec")
        for j4 in range(4):
            emit_transpose(p, psT, [
                (xbg[j4][:, dc * 128:(dc + 1) * 128],
                 xbfm[:, dc * 512 + j4 * 128: dc * 512 + j4 * 128 + 128]) for dc in range(DC)])
        for fc in range(6):
            for wi, wbuf in ((0, wg), (1, wu)):
                bank = 2 * (fc % 2) + wi
                for dc in range(DC):
                    p.do("tensor", lambda t, fc=fc, dc=dc, bank=bank, wbuf=wbuf: t.matmul(
                        psG[bank][:], wbuf[:, dc * FEXP + fc * 128: dc * FEXP + fc * 128 + 128],
                        xbfm[:, dc * 512:(dc + 1) * 512],
                        start=(dc == 0), stop=(dc == DC - 1)),
                        wait=(([("lw", lwc), ("vec", p.cnt["vec"]), ("act", p.cnt["act"])])
                              if dc == 0 else None),
                        inc=("mm" if dc == DC - 1 else None))
            p.do("scalar", lambda s, fc=fc: s.activation(
                out=t1[:], in_=psG[2 * (fc % 2)][:], func=AF.Silu),
                wait=["mm", ("vec", p.cnt["vec"])], inc="act")
            p.do("scalar", lambda s, fc=fc: s.activation(
                out=t2[:], in_=psG[2 * (fc % 2) + 1][:], func=AF.Identity), inc="act")
            p.do("vector", lambda v, fc=fc: v.tensor_tensor(
                hfm[:, fc * 512:(fc + 1) * 512], t1[:], t2[:], OP.mult),
                wait="act", inc="vec")
        for tsub in range(4):
            for half in range(2):
                bank = (tsub * 2 + half) % 2
                for fc in range(6):
                    p.do("tensor", lambda t, tsub=tsub, half=half, fc=fc, bank=bank, wd=wd: t.matmul(
                        psD[bank][:],
                        hfm[:, fc * 512 + tsub * 128: fc * 512 + tsub * 128 + 128],
                        wd[:, fc * 1024 + half * 512: fc * 1024 + half * 512 + 512],
                        start=(fc == 0), stop=(fc == 5)),
                        wait=([("vec", p.cnt["vec"]), ("st", p.cnt["st"])] if fc == 0 else None),
                        inc=("mm" if fc == 5 else None))
                p.do("vector", lambda v, bank=bank: v.tensor_copy(out=yev[:], in_=psD[bank][:]),
                     wait=["mm", ("st", p.cnt["st"])], inc="vec")
                p.do("sync", lambda sy, e=e, tsub=tsub, half=half: sy.dma_start(
                    out=yo[e * CAP + tsub * 128: e * CAP + tsub * 128 + 128,
                           half * 512:(half + 1) * 512], in_=yev[:]),
                    wait="vec", inc="st", incn=16)
        mm_afterC.append(p.cnt["mm"])
    p.do("sync", lambda sy: sy.nop(), wait=[("st", p.cnt["st"])])
    return p


def build_launch_E():
    p = Prog("E")
    x3i = p.din("x3i", [TPC, D])
    xl4i = p.din("xl4i", [TPC, D], BF16)
    goffi = p.din("goffi", [TPC, 8])
    wvi = p.din("wvi", [TPC, 8])
    yall = p.din("yall", [E_EXP * CAP, D], BF16)
    wsgu = p.din("wsgu", [D, 2 * FEXP])
    wsd = p.din("wsd", [FEXP, D])
    ident = p.din("identE", [128, 128])
    x4o = p.dout("x4o", [TPC, D])

    resid = p.sb("residE", [128, NG * 1024], F32)
    p.it128 = p.sb("it128E", [128, 128], BF16)
    xl4t = p.sb("xl4tE", [128, 1024], BF16)
    x4fm = p.sb("x4fmE", [128, DC * TPC], BF16)
    wgus = p.sb("wgusE", [128, DC * 2 * FEXP], BF16)
    wsds = p.sb("wsdsE", [128, 6 * D], BF16)
    t1 = p.sb("t1E", [128, 512], F32)
    t2 = p.sb("t2E", [128, 512], F32)
    hsh = p.sb("hshE", [128, 6 * 512], BF16)
    goffs = p.sb("goffsE", [128, 8], F32)
    wvs = p.sb("wvsE", [128, 8], F32)
    gint = p.sb("gintE", [128, 8], I32)
    rows = [p.sb(f"rowsE{k}", [128, 1024], BF16) for k in range(8)]
    acc = p.sb("accE", [128, 1024], F32)
    psT = [p.ps(f"psTE{i}", [128, 128], BF16) for i in range(2)]
    psG = [p.ps(f"psGE{i}", [128, 512], F32) for i in range(4)]
    psD = [p.ps(f"psDE{i}", [128, 512], F32) for i in range(2)]

    p.do("gpsimd", lambda g: g.dma_start(out=p.it128[:], in_=ident[:, :]), inc="ld", incn=16)
    p.do("gpsimd", lambda g: g.dma_start(
        out=wgus[:].rearrange("p (dc f) -> p dc f", dc=DC),
        in_=wsgu.rearrange("(dc p) f -> p dc f", p=128)), inc="lw", incn=16)
    p.do("gpsimd", lambda g: g.dma_start(
        out=wsds[:].rearrange("p (fc c) -> p fc c", fc=6),
        in_=wsd.rearrange("(fc p) c -> p fc c", p=128)), inc="lw", incn=16)
    for g2 in range(NG):
        p.do("sync", lambda sy, g2=g2: sy.dma_start(
            out=resid[:, g2 * 1024:(g2 + 1) * 1024], in_=x3i[g2 * 128:(g2 + 1) * 128, :]),
            inc="ld", incn=16)
    ld_resid = p.cnt["ld"]
    for k in range(8):
        p.do("vector", lambda v, k=k: v.memset(rows[k][:], 0.0), inc="vec")
    for g2 in range(NG):
        p.do("sync", lambda sy, g2=g2: sy.dma_start(
            out=xl4t[:], in_=xl4i[g2 * 128:(g2 + 1) * 128, :]),
            wait=[("mm", p.cnt["mm"])], inc="ld", incn=16)
        p.do("vector", lambda v: v.nop(), wait=[("ld", p.cnt["ld"])], inc="vec")
        emit_transpose(p, psT, [
            (xl4t[:, dc * 128:(dc + 1) * 128],
             x4fm[:, dc * TPC + g2 * 128: dc * TPC + g2 * 128 + 128]) for dc in range(DC)])
    for st4 in range(4):
        for gc in range(6):
            for wi in range(2):
                bank = 2 * (gc % 2) + wi
                for dc in range(DC):
                    p.do("tensor", lambda t, st4=st4, gc=gc, wi=wi, dc=dc, bank=bank: t.matmul(
                        psG[bank][:],
                        wgus[:, dc * 2 * FEXP + wi * FEXP + gc * 128:
                             dc * 2 * FEXP + wi * FEXP + gc * 128 + 128],
                        x4fm[:, dc * TPC + st4 * 512: dc * TPC + st4 * 512 + 512],
                        start=(dc == 0), stop=(dc == DC - 1)),
                        wait=([("lw", p.cnt["lw"]), ("vec", p.cnt["vec"]),
                               ("act", p.cnt["act"])] if dc == 0 else None),
                        inc=("mm" if dc == DC - 1 else None))
            p.do("scalar", lambda s, gc=gc: s.activation(
                out=t1[:], in_=psG[2 * (gc % 2)][:], func=AF.Silu),
                wait=["mm", ("vec", p.cnt["vec"])], inc="act")
            p.do("scalar", lambda s, gc=gc: s.activation(
                out=t2[:], in_=psG[2 * (gc % 2) + 1][:], func=AF.Identity), inc="act")
            p.do("vector", lambda v, gc=gc: v.tensor_tensor(
                hsh[:, gc * 512:(gc + 1) * 512], t1[:], t2[:], OP.mult),
                wait="act", inc="vec")
        for tsub in range(4):
            for half in range(2):
                bank = (tsub * 2 + half) % 2
                for fc in range(6):
                    p.do("tensor", lambda t, st4=st4, tsub=tsub, half=half, fc=fc, bank=bank:
                         t.matmul(psD[bank][:],
                                  hsh[:, fc * 512 + tsub * 128: fc * 512 + tsub * 128 + 128],
                                  wsds[:, fc * 1024 + half * 512: fc * 1024 + half * 512 + 512],
                                  start=(fc == 0), stop=(fc == 5)),
                         wait=([("vec", p.cnt["vec"])] if fc == 0 else None),
                         inc=("mm" if fc == 5 else None))
                g2 = st4 * 4 + tsub
                p.do("vector", lambda v, g2=g2, half=half, bank=bank: v.tensor_tensor(
                    resid[:, g2 * 1024 + half * 512: g2 * 1024 + half * 512 + 512],
                    resid[:, g2 * 1024 + half * 512: g2 * 1024 + half * 512 + 512],
                    psD[bank][:], OP.add), wait=["mm", ("ld", ld_resid)], inc="vec")
    for tg in range(NG):
        p.do("sync", lambda sy, tg=tg: sy.dma_start(out=goffs[:], in_=goffi[tg * 128:(tg + 1) * 128, :]),
             wait=[("vec", p.cnt["vec"]), ("gp", p.cnt["gp"])], inc="ld", incn=16)
        p.do("sync", lambda sy, tg=tg: sy.dma_start(out=wvs[:], in_=wvi[tg * 128:(tg + 1) * 128, :]),
             inc="ld", incn=16)
        p.do("vector", lambda v: v.tensor_copy(out=gint[:], in_=goffs[:]),
             wait=[("ld", p.cnt["ld"])], inc="vec")
        p.do("vector", lambda v: v.nop(), inc="gp")
        for k in range(8):
            p.do("gpsimd", lambda g, k=k: g.indirect_dma_start(
                rows[k][:], None, yall[0:128, :],
                IndirectOffsetOnAxis(gint[:, k:k + 1], 0)),
                wait=[("gp", p.cnt["gp"]), ("vec", p.cnt["vec"])], inc="ld", incn=16)
        p.do("vector", lambda v: v.tensor_scalar(acc[:], rows[0][:], wvs[:, 0:1], None,
                                                 OP.mult), wait=[("ld", p.cnt["ld"])], inc="vec")
        for k in range(1, 8):
            p.do("vector", lambda v, k=k: v.scalar_tensor_tensor(
                acc[:], rows[k][:], wvs[:, k:k + 1], acc[:],
                OP.mult, OP.add), wait="vec", inc="vec")
        p.do("vector", lambda v, tg=tg: v.tensor_tensor(
            resid[:, tg * 1024:(tg + 1) * 1024], resid[:, tg * 1024:(tg + 1) * 1024],
            acc[:], OP.add), wait="vec", inc="vec")
    p.do("sync", lambda sy: sy.dma_start(
        out=x4o.rearrange("(g p) d -> p g d", p=128),
        in_=resid[:].rearrange("p (g d) -> p g d", g=NG)),
        wait=[("vec", p.cnt["vec"])], inc="st", incn=16)
    p.do("sync", lambda sy: sy.nop(), wait=[("st", p.cnt["st"])])
    return p


def host_layouts(inputs):
    inp = {k: np.asarray(v) for k, v in inputs.items()}
    L = {}
    wqkv = np.ascontiguousarray(inp["L0_attn_W_qkv"].reshape(D, H, HD, 3))
    L["wq0"] = np.ascontiguousarray(wqkv[..., 0].reshape(D, D))
    L["wk0"] = np.ascontiguousarray(wqkv[..., 1].reshape(D, D))
    L["wv0"] = np.ascontiguousarray(wqkv[..., 2].reshape(D, D))
    L["wout"] = np.ascontiguousarray(inp["L0_attn_W_out"])
    L["w1t"] = np.ascontiguousarray(inp["L0_ffn_w1_w"].T)
    L["w2t"] = np.ascontiguousarray(inp["L0_ffn_w2_w"].T)
    L["h1t"] = np.ascontiguousarray(inp["L0_ffn_h1_w"].T)
    L["h2t"] = np.ascontiguousarray(inp["L0_ffn_h2_w"].T)
    L["w3t"] = np.ascontiguousarray(inp["L0_ffn_w3_w"].T)
    for nm, key in (("s1", "L0_ffn_w1_s"), ("s2", "L0_ffn_w2_s"),
                    ("sh1", "L0_ffn_h1_s"), ("sh2", "L0_ffn_h2_s"), ("s3", "L0_ffn_w3_s")):
        L[nm] = np.ascontiguousarray(inp[key].reshape(-1, 128).T)
    L["wq1"] = np.ascontiguousarray(inp["L1_attn_Wq"].reshape(D, D))
    L["wk1"] = np.ascontiguousarray(inp["L1_attn_Wk"].reshape(D, D))
    L["wv1"] = np.ascontiguousarray(inp["L1_attn_Wv"].reshape(D, D))
    L["wo1"] = np.ascontiguousarray(inp["L1_attn_Wo"].reshape(D, D))
    L["bq1"] = np.ascontiguousarray(inp["L1_attn_bq"].reshape(1, D))
    L["bk1"] = np.ascontiguousarray(inp["L1_attn_bk"].reshape(1, D))
    L["bv1"] = np.ascontiguousarray(inp["L1_attn_bv"].reshape(1, D))
    L["bo1"] = np.ascontiguousarray(inp["L1_attn_bo"].reshape(1, D))
    L["rt"] = np.ascontiguousarray(inp["L1_ffn_router_w"].T)
    L["wsgu"] = np.ascontiguousarray(inp["L1_ffn_ws_gate_up"])
    L["wsd"] = np.ascontiguousarray(inp["L1_ffn_ws_down"])
    L["wg"] = inp["L1_ffn_w_gate"]
    L["wu"] = inp["L1_ffn_w_up"]
    L["wd"] = inp["L1_ffn_w_down"]
    L["X"] = np.ascontiguousarray(inp["L0_attn_X"].reshape(B * S, D))
    L["ident"] = np.eye(128, dtype=np.float32)
    return L


def run_A(L, debug=False, stages=4):
    p = build_launch_A(debug, stages)
    nc = p.finalize()
    in_maps = []
    for c in range(NCORE):
        bi = c // 4
        im = dict(
            xb=L["X"][bi * S:(bi + 1) * S],
            xmy=L["X"][c * TPC:(c + 1) * TPC],
            wq0=L["wq0"], wk0=L["wk0"], wv0=L["wv0"], wout=L["wout"],
            w1t=L["w1t"], w2t=L["w2t"], h1t=L["h1t"], h2t=L["h2t"], w3t=L["w3t"],
            s1=L["s1"], s2=L["s2"], sh1=L["sh1"], sh2=L["sh2"], s3=L["s3"],
            wq1=L["wq1"], wk1=L["wk1"], wv1=L["wv1"],
            bq1=L["bq1"], bk1=L["bk1"], bv1=L["bv1"],
            ident=L["ident"],
        )
        in_maps.append(im)
    res = run_bass_kernel_spmd(nc, in_maps, core_ids=list(range(NCORE)))
    return res.results


def _run(build_fn, in_maps, _cache={}):
    key = build_fn.__name__
    if key not in _cache:
        p = build_fn() if build_fn is not build_launch_A else build_launch_A(False, 4)
        _cache[key] = p.finalize()
    nc = _cache[key]
    try:
        res = run_bass_kernel_spmd(nc, in_maps, core_ids=list(range(NCORE)))
    except Exception:
        res = run_bass_kernel_spmd(nc, in_maps, core_ids=list(range(NCORE)))
    return res.results


def kernel(**inputs):
    L = host_layouts(inputs)
    strict_upper = np.triu(np.ones((128, 128), np.float32), 1)

    inA = []
    for c in range(NCORE):
        bi = c // 4
        inA.append(dict(
            xb=L["X"][bi * S:(bi + 1) * S], xmy=L["X"][c * TPC:(c + 1) * TPC],
            wq0=L["wq0"], wk0=L["wk0"], wv0=L["wv0"], wout=L["wout"],
            w1t=L["w1t"], w2t=L["w2t"], h1t=L["h1t"], h2t=L["h2t"], w3t=L["w3t"],
            s1=L["s1"], s2=L["s2"], sh1=L["sh1"], sh2=L["sh2"], s3=L["s3"],
            wq1=L["wq1"], wk1=L["wk1"], wv1=L["wv1"],
            bq1=L["bq1"], bk1=L["bk1"], bv1=L["bv1"], ident=L["ident"]))
    rA = _run(build_launch_A, inA)

    inB = []
    for c in range(NCORE):
        grp = range(0, 4) if c < 4 else range(4, 8)
        kvp = np.stack([rA[g]["kvo"] for g in grp], 0)
        inB.append(dict(x2i=rA[c]["x2o"], qfi=rA[c]["qfo"], kvp=kvp,
                        wo1=L["wo1"], bo1=L["bo1"], rt=L["rt"], ident=L["ident"]))
    rB = _run(build_launch_B, inB)

    tota = np.concatenate([rB[c]["toto"] for c in range(NCORE)], 0)
    inB2 = []
    for c in range(NCORE):
        premask = (np.arange(NCORE) < c).astype(np.float32).reshape(8, 1)
        gtok = (np.arange(TPC, dtype=np.float32) + c * TPC + 1.0).reshape(TPC, 1)
        inB2.append(dict(idxi=rB[c]["idxo"], wni=rB[c]["wno"], tota=tota,
                         premask=premask, gtok=gtok, uts=strict_upper))
    rB2 = _run(build_launch_B2, inB2)

    xall = np.concatenate([rB[c]["xl4o"] for c in range(NCORE)], 0)
    plist_all = [np.asarray(rB2[c]["plisto"])[:E_EXP * CAP].reshape(E_EXP, CAP)
                 for c in range(NCORE)]
    inC = []
    for c in range(NCORE):
        mine = np.stack([pl[c * EPPC:(c + 1) * EPPC].reshape(-1).reshape(32, 128).T.copy()
                         for pl in plist_all], 0)
        inC.append(dict(plists=np.ascontiguousarray(mine), xall=xall,
                        wgi=L["wg"][c * EPPC:(c + 1) * EPPC],
                        wui=L["wu"][c * EPPC:(c + 1) * EPPC],
                        wdi=L["wd"][c * EPPC:(c + 1) * EPPC], identC=L["ident"]))
    rC = _run(build_launch_C, inC)

    yall = np.concatenate([rC[c]["yo"] for c in range(NCORE)], 0)
    inE = []
    for c in range(NCORE):
        inE.append(dict(x3i=rB[c]["x3o"], xl4i=rB[c]["xl4o"], goffi=rB2[c]["goffo"],
                        wvi=rB2[c]["wvo"], yall=yall, wsgu=L["wsgu"], wsd=L["wsd"],
                        identE=L["ident"]))
    rE = _run(build_launch_E, inE)

    out = np.concatenate([rE[c]["x4o"] for c in range(NCORE)], 0).reshape(B, S, D)
    return out.astype(np.float32)
